# version 04
# baseline (speedup 1.0000x reference)
import contextlib
import numpy as np
import concourse.bass as bass
import concourse.mybir as mybir
from concourse.bass_utils import run_bass_kernel_spmd

F32 = mybir.dt.float32
BF16 = mybir.dt.bfloat16
I32 = mybir.dt.int32
U8 = mybir.dt.uint8
AF = mybir.ActivationFunctionType
ALU = mybir.AluOpType
AXX = mybir.AxisListType.X

T = 8192
D = 1024
CH = 512
NH = 8
HK = 64
C = 128
GT = 256
NCHUNK = 2
NG = T // GT
HALF = 4096
SUB = 256
NSUB = HALF // SUB
FFH = 3072
PAD = 136


class Res:
    __slots__ = ("w", "rs", "wkind")

    def __init__(self):
        self.w = None
        self.rs = []
        self.wkind = "compute"


class Sched:
    ENGS = ("sync", "tensor", "vector", "scalar", "gpsimd")
    NPOOL = 6

    def __init__(self, carry=None, dmagen=None):
        self.ops = []
        self.carry = carry or {e: 0 for e in self.ENGS}
        self.carry.setdefault("cc", 0)
        self.dmagen = list(dmagen) if dmagen else [0] * self.NPOOL
        self.dmaslot_next = 0

    def op(self, engine, fn, reads=(), writes=(), kind="compute",
           extra_deps=(), n_dma=1):
        deps = set()
        for r in reads:
            if r.w is not None:
                deps.add(r.w)
        for w in writes:
            if w.w is not None:
                deps.add(w.w)
            deps.update(w.rs)
        for t_ in extra_deps:
            if t_ is not None:
                deps.add(t_)
        tok = len(self.ops)
        o = dict(engine=engine, fn=fn, deps=deps, kind=kind, inc=False,
                 cnt=None, slot=None, base=None, n_dma=n_dma)
        if kind == "dma":
            sl = self.dmaslot_next
            self.dmaslot_next = (sl + 1) % self.NPOOL
            o["slot"] = sl
            o["base"] = self.dmagen[sl]
            self.dmagen[sl] += 16 * n_dma
            o["cnt"] = self.dmagen[sl]
        self.ops.append(o)
        for r in reads:
            if tok not in r.rs:
                r.rs.append(tok)
        for w in writes:
            w.w = tok
            w.rs = []
        return tok

    def emit(self, block, sems, barrier_in=False):
        for o in self.ops:
            if o["kind"] == "cc":
                o["inc"] = True
            for d in o["deps"]:
                dop = self.ops[d]
                if dop["kind"] in ("dma", "cc"):
                    dop["inc"] = True
                elif (dop["engine"] != o["engine"]
                      or dop["engine"] != "tensor"):
                    dop["inc"] = True
        last = {}
        for i, o in enumerate(self.ops):
            if o["kind"] == "compute":
                last[o["engine"]] = i
        for e_, i in last.items():
            self.ops[i]["inc"] = True
        cnt = dict(self.carry)
        for o in self.ops:
            if o["kind"] == "dma":
                continue
            if o["inc"]:
                key = "cc" if o["kind"] == "cc" else o["engine"]
                cnt[key] += 1
                o["cnt"] = cnt[key]
        self.final = cnt

        active = {o["engine"] for o in self.ops}
        for ename in self.ENGS:
            if ename not in active:
                continue

            def body(e, ename=ename):
                hw = {k: self.carry.get(k, 0) for k in
                      list(self.ENGS) + ["cc"]}
                dhw = [0] * self.NPOOL
                if barrier_in:
                    for o_e in list(self.ENGS) + ["cc"]:
                        if o_e != ename and self.carry.get(o_e, 0) > 0:
                            e.wait_ge(sems[o_e], self.carry[o_e])
                    for sl in range(self.NPOOL):
                        g0 = self.carry.get(f"d{sl}", 0)
                        if g0 > 0:
                            e.wait_ge(sems[f"d{sl}"], g0)
                            dhw[sl] = g0
                for o in self.ops:
                    if o["engine"] != ename:
                        continue
                    for d in sorted(o["deps"]):
                        dop = self.ops[d]
                        if dop["kind"] == "dma":
                            sl, c_ = dop["slot"], dop["cnt"]
                            if c_ > dhw[sl]:
                                e.wait_ge(sems[f"d{sl}"], c_)
                                dhw[sl] = c_
                            continue
                        need = (dop["engine"] != ename
                                or dop["kind"] == "cc"
                                or ename != "tensor")
                        if not need:
                            continue
                        key = "cc" if dop["kind"] == "cc" else dop["engine"]
                        c_ = dop["cnt"]
                        if c_ is not None and c_ > hw[key]:
                            e.wait_ge(sems[key], c_)
                            hw[key] = c_
                    if o["kind"] == "dma":
                        sl = o["slot"]
                        if o["base"] > dhw[sl]:
                            e.wait_ge(sems[f"d{sl}"], o["base"])
                            dhw[sl] = o["base"]
                        inst = o["fn"](e, sems[f"d{sl}"])
                    else:
                        inst = o["fn"](e)
                        if o["inc"]:
                            if o["kind"] == "cc":
                                inst.then_inc(sems["cc"])
                            else:
                                inst.then_inc(sems[ename], 1)
            getattr(block, ename)(body)


def build_nc(T_=T, HALF_=HALF, ncores=8, no_branch=False, no_cc=False, simdma=False, skipb=False, skipa=False):
    NGl = T_ // GT
    NSUBl = HALF_ // SUB
    rgroups = [[2 * i, 2 * i + 1] for i in range(ncores // 2)]
    nc = bass.Bass("TRN2", target_bir_lowering=False)

    def din(name, shape, dt):
        return nc.dram_tensor(name, list(shape), dt, kind="ExternalInput").ap()

    x_in = din("x", (T_, D), F32)
    xh_in = din("xh", (HALF_, D), F32)
    wq_in = din("wq", (D, CH), F32)
    wk_in = din("wk", (D, CH), F32)
    wv_in = din("wv", (D, CH), F32)
    cvw_in = din("cvw", (3, CH, 4), F32)
    cvb_in = din("cvb", (3, CH, 1), F32)
    wf1_in = din("wf1", (D, 64), F32)
    wf2_in = din("wf2", (64, CH), F32)
    wb_in = din("wb", (D, NH), F32)
    alog_in = din("alog", (NH, 1), F32)
    dtb_in = din("dtb", (CH, 1), F32)
    wg1_in = din("wg1", (D, 64), F32)
    wg2_in = din("wg2", (64, CH), F32)
    bg2_in = din("bg2", (CH, 1), F32)
    onw_in = din("onw", (128, 1), F32)
    wo_in = din("wo", (D, D), F32)
    wgT_in = din("wgT", (D, FFH), I32)
    wuT_in = din("wuT", (D, FFH), I32)
    wdT_in = din("wdT", (FFH, D), I32)
    ident_in = din("ident", (128, 128), F32)
    masks_in = din("masks", (4, C, 4 * C), U8)
    ind8_in = din("ind8", (4, NH, 128), F32)
    ind8T_in = din("ind8T", (4, 128, NH), F32)
    coreoff_in = din("coreoff", (1, 1), I32)

    out_t = nc.dram_tensor("out", [HALF_, D], F32, kind="ExternalOutput").ap()
    cc_in_f = nc.dram_tensor("cc_in", [CH, T_ // 2], F32, kind="Internal").ap()
    cc_out_f = nc.dram_tensor("cc_out", [D, T_ // 2], F32, kind="Internal").ap()
    cc_in = cc_in_f.bitcast(BF16)
    cc_out = cc_out_f.bitcast(BF16)

    ctx = contextlib.ExitStack()
    with ctx:
        def sb(name, shape, dt, c=ctx):
            return c.enter_context(nc.sbuf_tensor(name, list(shape), dt))

        sems = {e: ctx.enter_context(nc.semaphore("sem_" + e))
                for e in Sched.ENGS}
        sems["cc"] = ctx.enter_context(nc.semaphore("sem_cc"))
        for sl in range(Sched.NPOOL):
            sems[f"d{sl}"] = ctx.enter_context(nc.semaphore(f"sem_d{sl}"))
        PB = [ctx.enter_context(nc.psum_tensor(f"PB{i}", [128, 512], F32))
              for i in range(8)]
        PBb = [p[:].bitcast(BF16) for p in PB]

        R = {}

        def B(buf):
            key = id(buf)
            if key not in R:
                R[key] = Res()
            return R[key]

        ccin_res = Res()
        outres = Res()

        S = Sched()

        def mkdma(SS):
            def dma(engine, dst, src, reads=(), writes=(), deps=()):
                return SS.op(engine,
                             lambda e, dsem, dst=dst, src=src: e.dma_start(
                                 out=dst, in_=src).then_inc(dsem, 16),
                             reads=reads, writes=writes, kind="dma",
                             extra_deps=deps)
            return dma
        dma = mkdma(S)

        ident_f = sb("ident_f", (128, 128), F32)
        ident_b = sb("ident_b", (128, 128), BF16)
        soff = sb("soff", (1, 1), I32)
        constc = sb("constc", (128, 8), F32)
        dma("sync", ident_f[:], ident_in, writes=[B(ident_f)])
        dma("sync", soff[:], coreoff_in, writes=[B(soff)])
        S.op("vector", lambda e: e.tensor_copy(out=ident_b[:], in_=ident_f[:]),
             reads=[B(ident_f)], writes=[B(ident_b)])
        for ci, cv in enumerate((1e-5, -0.5, 0.5, 1.0 / HK, 1.0 / 127.0)):
            S.op("vector", lambda e, ci=ci, cv=cv: e.memset(
                constc[:, ci:ci + 1], cv), writes=[B(constc)])

        actx = contextlib.ExitStack()
        sba = lambda n, s_, d_: sb(n, s_, d_, actx)
        with actx:
            masks_s = sba("masks_s", (128, 4 * 4 * C), U8)
            ind8_s = sba("ind8_s", (NH, 4 * 128), BF16)
            ind8T_s = sba("ind8T_s", (128, 4 * NH), BF16)
            stage_f = sba("stage_f", (128, 8 * CH), F32)

            wq_s = sba("wq_s", (128, 8 * CH), BF16)
            wk_s = sba("wk_s", (128, 8 * CH), BF16)
            wv_s = sba("wv_s", (128, 8 * CH), BF16)
            wf1_s = sba("wf1_s", (128, 8 * 64), BF16)
            wg1_s = sba("wg1_s", (128, 8 * 64), BF16)
            wb_s = sba("wb_s", (128, 8 * NH), BF16)
            wf2_s = sba("wf2_s", (64, CH), BF16)
            wg2_s = sba("wg2_s", (64, CH), BF16)

            cvw_s = sba("cvw_s", (128, 3 * 4 * 4), F32)
            cvb_s = sba("cvb_s", (128, 3 * 4), F32)
            dtb_s = sba("dtb_s", (128, 4), F32)
            bg2_s = sba("bg2_s", (128, 4), F32)
            acol_s = sba("acol_s", (128, 4), F32)
            alog_s = sba("alog_s", (NH, 1), F32)
            aexp_b = sba("aexp_b", (NH, 1), BF16)

            xbuf = [sba(f"xbuf{i}", (128, D), F32) for i in range(2)]
            lnstat = sba("lnstat", (128, 16), F32)
            scr_big = sba("scr_big", (128, D), F32)
            xnb = sba("xnb", (128, D), BF16)
            xnTg = sba("xnTg", (128, 8 * GT), BF16)

            qs = [sba(f"qs{cb}", (128, GT), BF16) for cb in range(4)]
            ks = [sba(f"ks{cb}", (128, GT), BF16) for cb in range(4)]
            vs = [sba(f"vs{cb}", (128, GT), BF16) for cb in range(4)]
            cbuf = [sba(f"cbuf{p}", (128, 4 * (GT + 4)), BF16)
                    for p in range(3)]
            sqscr = sba("sqscr", (128, 512), BF16)
            rq_s = sba("rq_s", (NH, GT), BF16)
            rk_s = sba("rk_s", (NH, GT), BF16)
            kn = [sba(f"kn{cb}", (128, GT), BF16) for cb in range(4)]
            qn = [sba(f"qn{cb}", (128, GT), BF16) for cb in range(4)]
            bcols = sba("bcols", (128, NCHUNK * NH), F32)
            g1s = sba("g1s", (64, GT), BF16)
            spscr = [sba(f"spscr{i}", (128, GT), F32) for i in range(2)]
            gplus = [sba(f"gplus{cb}", (128, GT + 8), F32) for cb in range(4)]
            negb = [sba(f"negb{cb}", (128, NCHUNK * PAD), F32)
                    for cb in range(4)]
            zeros128 = sba("zeros128", (128, 128), F32)
            sgate = [sba(f"sgate{cb}", (128, GT), BF16) for cb in range(4)]

            lamf = [sba(f"lamf{cb}", (128, GT), BF16) for cb in range(4)]
            kp = [sba(f"kp{cb}", (128, GT), BF16) for cb in range(4)]
            qp = [sba(f"qp{cb}", (128, GT), BF16) for cb in range(4)]
            kd = [sba(f"kd{cb}", (128, GT), BF16) for cb in range(4)]
            facscr = sba("facscr", (128, GT), F32)
            rowF = sba("rowF", (128, GT), BF16)
            colF = sba("colF", (128, GT), BF16)
            rnR = [[sba(f"rnR{lv}_{cb}", (128, GT), BF16) for cb in range(4)]
                   for lv in range(3)]
            raR = [[sba(f"raR{lv}_{cb}", (128, GT), BF16) for cb in range(4)]
                   for lv in range(3)]
            ckR = [[sba(f"ckR{lv}_{cb}", (128, GT), BF16) for cb in range(4)]
                   for lv in range(3)]
            rnL = [[sba(f"rnL{lv}_{cb}", (128, GT), BF16) for cb in range(4)]
                   for lv in range(3)]
            raL = [[sba(f"raL{lv}_{cb}", (128, GT), BF16) for cb in range(4)]
                   for lv in range(3)]
            ckL = [[sba(f"ckL{lv}_{cb}", (128, GT), BF16) for cb in range(4)]
                   for lv in range(3)]
            ck2 = [sba(f"ck2_{cb}", (128, GT), BF16) for cb in range(4)]
            e2 = sba("e2", (128, GT + 8), BF16)
            aqd_s = sba("aqd_s", (128, NCHUNK * NH), F32)
            rstd_s = sba("rstd_s", (128, NH), F32)
            lamC = sba("lamC", (128, 4), F32)
            negcol = sba("negcol", (128, 4), F32)
            bmap = sba("bmap", (128, NH * HK), F32)

            AT0 = sba("AT0", (128, NH * C), BF16)
            AQT = sba("AQT", (128, NH * C), BF16)
            A0 = sba("A0", (128, NH * C), BF16)
            Afrm = [sba(f"Afrm{k}", (128, NH * C), BF16) for k in range(2)]
            ATk = [sba(f"ATk{k}", (128, NH * C), BF16) for k in range(6)]
            Bfrm = sba("Bfrm", (128, NH * C), BF16)
            selscr = sba("selscr", (128, 512), BF16)
            farsb = sba("farsb", (128, 512), BF16)
            Xb = [sba(f"Xb{k}", (128, NH * HK), BF16) for k in range(2)]
            Rb = sba("Rb", (128, NH * HK), BF16)
            Xscr = sba("Xscr", (128, NH * HK), F32)
            vT_s = sba("vT_s", (128, NH * HK), BF16)
            Ub = sba("Ub", (128, NH * HK), BF16)
            S0 = [sba(f"S0_{cb}", (128, 128), F32) for cb in range(4)]
            S0b = [sba(f"S0b_{cb}", (128, 128), BF16) for cb in range(4)]
            kdT_s = sba("kdT_s", (128, NH * HK), BF16)
            omid = sba("omid", (128, NH * HK), BF16)
            ofinT = sba("ofinT", (128, 4 * GT), BF16)
            qkscr = sba("qkscr", (128, C), BF16)

            import os
            KLVL = int(os.environ.get("KLVL", "9"))
            for li in range(4 if KLVL >= 1 else 0):
                dma("sync", masks_s[:, li * 4 * C:(li + 1) * 4 * C],
                    masks_in[li], writes=[B(masks_s)])
            for cb in range(4 if KLVL >= 1 else 0):
                dma("sync", stage_f[0:NH, cb * 128:(cb + 1) * 128],
                    ind8_in[cb], writes=[B(stage_f)])
            S.op("vector", lambda e: e.tensor_copy(
                out=ind8_s[:], in_=stage_f[0:NH, 0:4 * 128]),
                reads=[B(stage_f)], writes=[B(ind8_s)])
            for cb in range(4):
                dma("sync", stage_f[:, cb * NH:(cb + 1) * NH], ind8T_in[cb],
                    reads=[B(ind8_s)], writes=[B(stage_f)])
            S.op("vector", lambda e: e.tensor_copy(
                out=ind8T_s[:], in_=stage_f[:, 0:4 * NH]),
                reads=[B(stage_f)], writes=[B(ind8T_s)])
            for win, wsb in (((wq_in, wq_s), (wk_in, wk_s), (wv_in, wv_s))
                             if KLVL >= 2 else ()):
                for j in range(8):
                    dma("sync", stage_f[:, j * CH:(j + 1) * CH],
                        win[j * 128:(j + 1) * 128, :],
                        reads=[B(wsb)], writes=[B(stage_f)])
                S.op("vector", lambda e, wsb=wsb: e.tensor_copy(
                    out=wsb[:], in_=stage_f[:]),
                    reads=[B(stage_f)], writes=[B(wsb)])
            for win, wsb, w_ in (((wf1_in, wf1_s, 64), (wg1_in, wg1_s, 64),
                                 (wb_in, wb_s, NH)) if KLVL >= 2 else ()):
                for j in range(8):
                    dma("sync", stage_f[:, j * w_:(j + 1) * w_],
                        win[j * 128:(j + 1) * 128, :],
                        reads=[B(wsb)], writes=[B(stage_f)])
                S.op("vector", lambda e, wsb=wsb, w_=w_: e.tensor_copy(
                    out=wsb[:], in_=stage_f[:, 0:8 * w_]),
                    reads=[B(stage_f)], writes=[B(wsb)])
            for win, wsb in (((wf2_in, wf2_s), (wg2_in, wg2_s))
                             if KLVL >= 2 else ()):
                dma("sync", stage_f[0:64, 0:CH], win,
                    reads=[B(wsb)], writes=[B(stage_f)])
                S.op("vector", lambda e, wsb=wsb: e.tensor_copy(
                    out=wsb[:], in_=stage_f[0:64, 0:CH]),
                    reads=[B(stage_f)], writes=[B(wsb)])
            for pj in range(3 if KLVL >= 3 else 0):
                for cb in range(4):
                    dma("sync", cvw_s[:, (pj * 4 + cb) * 4:(pj * 4 + cb) * 4 + 4],
                        cvw_in[pj, cb * 128:(cb + 1) * 128, :],
                        writes=[B(cvw_s)])
                    dma("sync", cvb_s[:, pj * 4 + cb:pj * 4 + cb + 1],
                        cvb_in[pj, cb * 128:(cb + 1) * 128, :],
                        writes=[B(cvb_s)])
            for cb in range(4 if KLVL >= 3 else 0):
                dma("sync", dtb_s[:, cb:cb + 1],
                    dtb_in[cb * 128:(cb + 1) * 128, :], writes=[B(dtb_s)])
                dma("sync", bg2_s[:, cb:cb + 1],
                    bg2_in[cb * 128:(cb + 1) * 128, :], writes=[B(bg2_s)])
            if KLVL >= 4:
                dma("sync", alog_s[:], alog_in, writes=[B(alog_s)])
                S.op("scalar", lambda e: e.activation(
                    out=aexp_b[:], in_=alog_s[:], func=AF.Exp),
                    reads=[B(alog_s)], writes=[B(aexp_b)])
                for cb in range(4):
                    S.op("tensor", lambda e, cb=cb: e.matmul(
                        out=PB[5][:, cb:cb + 1],
                        lhsT=ind8_s[:, cb * 128:(cb + 1) * 128], rhs=aexp_b[:],
                        start=(cb == 0), stop=(cb == 3)),
                        reads=[B(ind8_s), B(aexp_b)], writes=[B(PB[5])])
                S.op("scalar", lambda e: e.copy(
                    out=acol_s[:], in_=PB[5][:, 0:4]),
                    reads=[B(PB[5])], writes=[B(acol_s)])
            S.op("vector", lambda e: e.memset(zeros128[:], 0.0),
                 writes=[B(zeros128)])
            for cb in range(4):
                S.op("vector", lambda e, cb=cb: e.memset(S0[cb][:], 0.0),
                     writes=[B(S0[cb])])
                S.op("vector", lambda e, cb=cb: e.memset(S0b[cb][:], 0.0),
                     writes=[B(S0b[cb])])
                S.op("vector", lambda e, cb=cb: e.memset(negb[cb][:], 1e9),
                     writes=[B(negb[cb])])
                S.op("vector", lambda e, cb=cb: e.memset(gplus[cb][:], 1e9),
                     writes=[B(gplus[cb])])
            for p in range(3):
                S.op("vector", lambda e, p=p: e.memset(cbuf[p][:], 0.0),
                     writes=[B(cbuf[p])])
            S.op("vector", lambda e: e.memset(e2[:], 0.0), writes=[B(e2)])

            x_t = x_in.rearrange("(n p) d -> n p d", p=128)
            nbv = [negb[cb][:].rearrange("p (c s) -> p c s", c=NCHUNK)
                   for cb in range(4)]

            def emit_ln(SS, dmaf, xb, src_ap, stat, scr, xout, sem_unused):
                dmaf("sync", xb[:], src_ap, writes=[B(xb)])
                SS.op("vector", lambda e: e.reduce_sum(
                    out=stat[:, 0:1], in_=xb[:], axis=AXX),
                    reads=[B(xb)], writes=[B(stat)])
                SS.op("scalar", lambda e: e.activation(
                    out=scr[:], in_=xb[:], func=AF.Square),
                    reads=[B(xb)], writes=[B(scr)])
                SS.op("vector", lambda e: e.reduce_sum(
                    out=stat[:, 1:2], in_=scr[:], axis=AXX),
                    reads=[B(scr)], writes=[B(stat)])
                SS.op("vector", lambda e: e.tensor_scalar(
                    out=stat[:, 2:3], in0=stat[:, 0:1], scalar1=1.0 / D,
                    scalar2=None, op0=ALU.mult),
                    reads=[B(stat)], writes=[B(stat)])
                SS.op("vector", lambda e: e.tensor_tensor(
                    out=stat[:, 3:4], in0=stat[:, 2:3], in1=stat[:, 2:3],
                    op=ALU.mult), reads=[B(stat)], writes=[B(stat)])
                SS.op("vector", lambda e: e.tensor_scalar(
                    out=stat[:, 4:5], in0=stat[:, 1:2], scalar1=1.0 / D,
                    scalar2=stat[:, 3:4], op0=ALU.mult, op1=ALU.subtract),
                    reads=[B(stat)], writes=[B(stat)])
                SS.op("scalar", lambda e: e.activation(
                    out=stat[:, 5:6], in_=stat[:, 4:5], func=AF.Ln,
                    bias=constc[:, 0:1]),
                    reads=[B(stat), B(constc)], writes=[B(stat)])
                SS.op("scalar", lambda e: e.activation(
                    out=stat[:, 6:7], in_=stat[:, 5:6], func=AF.Exp,
                    scale=constc[:, 1:2]),
                    reads=[B(stat), B(constc)], writes=[B(stat)])
                SS.op("vector", lambda e: e.tensor_scalar(
                    out=xout[:], in0=xb[:], scalar1=stat[:, 2:3],
                    scalar2=stat[:, 6:7], op0=ALU.subtract, op1=ALU.mult),
                    reads=[B(xb), B(stat)], writes=[B(xout)])

            for g in range(0 if skipa else NGl):
                T0 = g * GT
                for i in range(GT // 128):
                    ti = g * (GT // 128) + i
                    xb = xbuf[ti % 2]
                    emit_ln(S, dma, xb, x_t[ti], lnstat, scr_big, xnb, None)
                    for j in range(8):
                        S.op("tensor", lambda e, j=j: e.transpose(
                            out=PBb[7][:, j * 128:(j + 1) * 128],
                            in_=xnb[:, j * 128:(j + 1) * 128],
                            identity=ident_b[:]),
                            reads=[B(xnb), B(ident_b)], writes=[B(PB[7])])
                    S.op("vector", lambda e, i=i: e.tensor_copy(
                        out=xnTg[:].rearrange("p (b t) -> p b t", b=8)
                        [:, :, i * 128:(i + 1) * 128],
                        in_=PBb[7][:].rearrange("p (b t) -> p b t", b=8)),
                        reads=[B(PB[7])], writes=[B(xnTg)])

                for cb in range(4):
                    for pj, wsb in enumerate((wq_s, wk_s, wv_s)):
                        pp = PB[pj]
                        for j in range(8):
                            S.op("tensor", lambda e, j=j, cb=cb, wsb=wsb,
                                 pp=pp: e.matmul(
                                     out=pp[:, 0:GT],
                                     lhsT=wsb[:, j * CH + cb * 128:
                                              j * CH + (cb + 1) * 128],
                                     rhs=xnTg[:, j * GT:(j + 1) * GT],
                                     start=(j == 0), stop=(j == 7)),
                                 reads=[B(wsb), B(xnTg)], writes=[B(pp)])
                    for pj in range(3):
                        pp = PB[pj]
                        dstl = [qs, ks, vs][pj]
                        cbf = cbuf[pj]
                        c0 = cb * (GT + 4)
                        S.op("vector", lambda e, cbf=cbf, c0=c0: e.tensor_copy(
                            out=cbf[:, c0:c0 + 3],
                            in_=cbf[:, c0 + GT:c0 + GT + 3]),
                            reads=[B(cbf)], writes=[B(cbf)])
                        S.op("scalar", lambda e, cbf=cbf, c0=c0, pp=pp: e.copy(
                            out=cbf[:, c0 + 3:c0 + 3 + GT], in_=pp[:, 0:GT]),
                            reads=[B(pp)], writes=[B(cbf)])
                        wc0 = (pj * 4 + cb) * 4
                        S.op("vector", lambda e, cbf=cbf, c0=c0, wc0=wc0:
                             e.tensor_scalar(
                                 out=scr_big[:, 0:GT], in0=cbf[:, c0:c0 + GT],
                                 scalar1=cvw_s[:, wc0:wc0 + 1], scalar2=None,
                                 op0=ALU.mult),
                             reads=[B(cbf), B(cvw_s)], writes=[B(scr_big)])
                        for tp in range(1, 4):
                            S.op("vector", lambda e, cbf=cbf, c0=c0, wc0=wc0,
                                 tp=tp: e.scalar_tensor_tensor(
                                     out=scr_big[:, 0:GT],
                                     in0=cbf[:, c0 + tp:c0 + tp + GT],
                                     scalar=cvw_s[:, wc0 + tp:wc0 + tp + 1],
                                     in1=scr_big[:, 0:GT], op0=ALU.mult,
                                     op1=ALU.add),
                                 reads=[B(cbf), B(cvw_s), B(scr_big)],
                                 writes=[B(scr_big)])
                        bc = pj * 4 + cb
                        S.op("scalar", lambda e, bc=bc, dst=dstl[cb]:
                             e.activation(
                                 out=dst[:], in_=scr_big[:, 0:GT],
                                 func=AF.Silu, bias=cvb_s[:, bc:bc + 1]),
                             reads=[B(scr_big), B(cvb_s)], writes=[B(dstl[cb])])

                for j in range(8):
                    S.op("tensor", lambda e, j=j: e.matmul(
                        out=PB[3][0:NH, 0:GT],
                        lhsT=wb_s[:, j * NH:(j + 1) * NH],
                        rhs=xnTg[:, j * GT:(j + 1) * GT],
                        start=(j == 0), stop=(j == 7)),
                        reads=[B(wb_s), B(xnTg)], writes=[B(PB[3])])
                S.op("scalar", lambda e: e.activation(
                    out=rq_s[:], in_=PB[3][0:NH, 0:GT], func=AF.Sigmoid),
                    reads=[B(PB[3])], writes=[B(rq_s)])
                for c in range(NCHUNK):
                    S.op("tensor", lambda e, c=c: e.transpose(
                        out=PBb[5][0:C, c * NH:(c + 1) * NH],
                        in_=rq_s[0:NH, c * 128:(c + 1) * 128],
                        identity=ident_b[0:NH, 0:NH]),
                        reads=[B(rq_s), B(ident_b)], writes=[B(PB[5])])
                S.op("scalar", lambda e: e.copy(
                    out=bcols[:], in_=PBb[5][:, 0:NCHUNK * NH]),
                    reads=[B(PB[5])], writes=[B(bcols)])

                for src, rmap in ((qs, rq_s), (ks, rk_s)):
                    for cb in range(4):
                        S.op("scalar", lambda e, src=src, cb=cb: e.activation(
                            out=sqscr[:, 0:GT], in_=src[cb][:], func=AF.Square),
                            reads=[B(src[cb])], writes=[B(sqscr)])
                        S.op("tensor", lambda e, cb=cb: e.matmul(
                            out=PB[3][0:NH, 0:GT],
                            lhsT=ind8T_s[:, cb * NH:(cb + 1) * NH],
                            rhs=sqscr[:, 0:GT], start=(cb == 0), stop=(cb == 3)),
                            reads=[B(ind8T_s), B(sqscr)], writes=[B(PB[3])])
                    S.op("scalar", lambda e: e.activation(
                        out=spscr[0][0:NH, 0:GT], in_=PB[3][0:NH, 0:GT],
                        func=AF.Ln), reads=[B(PB[3])], writes=[B(spscr[0])])
                    S.op("scalar", lambda e: e.activation(
                        out=spscr[0][0:NH, 0:GT], in_=spscr[0][0:NH, 0:GT],
                        func=AF.Exp, scale=constc[0:NH, 2:3]),
                        reads=[B(spscr[0]), B(constc)], writes=[B(spscr[0])])
                    S.op("vector", lambda e: e.tensor_scalar(
                        out=spscr[0][0:NH, 0:GT], in0=spscr[0][0:NH, 0:GT],
                        scalar1=1e-6, scalar2=None, op0=ALU.add),
                        reads=[B(spscr[0])], writes=[B(spscr[0])])
                    S.op("vector", lambda e: e.reciprocal(
                        out=spscr[1][0:NH, 0:GT], in_=spscr[0][0:NH, 0:GT]),
                        reads=[B(spscr[0])], writes=[B(spscr[1])])
                    S.op("vector", lambda e, rmap=rmap: e.tensor_copy(
                        out=rmap[:], in_=spscr[1][0:NH, 0:GT]),
                        reads=[B(spscr[1])], writes=[B(rmap)])
                for cb in range(4):
                    S.op("tensor", lambda e, cb=cb: e.matmul(
                        out=PB[4][:, 0:GT],
                        lhsT=ind8_s[:, cb * 128:(cb + 1) * 128], rhs=rk_s[:],
                        start=True, stop=True),
                        reads=[B(ind8_s), B(rk_s)], writes=[B(PB[4])])
                    S.op("vector", lambda e, cb=cb: e.tensor_tensor(
                        out=kn[cb][:], in0=ks[cb][:], in1=PB[4][:, 0:GT],
                        op=ALU.mult),
                        reads=[B(ks[cb]), B(PB[4])], writes=[B(kn[cb])])
                    S.op("tensor", lambda e, cb=cb: e.matmul(
                        out=PB[4][:, 0:GT],
                        lhsT=ind8_s[:, cb * 128:(cb + 1) * 128], rhs=rq_s[:],
                        start=True, stop=True),
                        reads=[B(ind8_s), B(rq_s)], writes=[B(PB[4])])
                    S.op("vector", lambda e, cb=cb: e.tensor_tensor(
                        out=qn[cb][:], in0=qs[cb][:], in1=PB[4][:, 0:GT],
                        op=ALU.mult),
                        reads=[B(qs[cb]), B(PB[4])], writes=[B(qn[cb])])

                for w1sb, w2sb, is_gate in ((wf1_s, wf2_s, False),
                                            (wg1_s, wg2_s, True)):
                    for j in range(8):
                        S.op("tensor", lambda e, j=j, w1sb=w1sb: e.matmul(
                            out=PB[4][0:64, 0:GT],
                            lhsT=w1sb[:, j * 64:(j + 1) * 64],
                            rhs=xnTg[:, j * GT:(j + 1) * GT],
                            start=(j == 0), stop=(j == 7)),
                            reads=[B(w1sb), B(xnTg)], writes=[B(PB[4])])
                    S.op("scalar", lambda e: e.copy(
                        out=g1s[:], in_=PB[4][0:64, 0:GT]),
                        reads=[B(PB[4])], writes=[B(g1s)])
                    for cb in range(4):
                        S.op("tensor", lambda e, cb=cb, w2sb=w2sb: e.matmul(
                            out=PB[5][:, 0:GT],
                            lhsT=w2sb[:, cb * 128:(cb + 1) * 128], rhs=g1s[:],
                            start=True, stop=True),
                            reads=[B(w2sb), B(g1s)], writes=[B(PB[5])])
                        if is_gate:
                            S.op("scalar", lambda e, cb=cb: e.activation(
                                out=sgate[cb][:], in_=PB[5][:, 0:GT],
                                func=AF.Sigmoid, bias=bg2_s[:, cb:cb + 1]),
                                reads=[B(PB[5]), B(bg2_s)],
                                writes=[B(sgate[cb])])
                        else:
                            S.op("scalar", lambda e, cb=cb: e.activation(
                                out=spscr[0][:], in_=PB[5][:, 0:GT],
                                func=AF.Abs, bias=dtb_s[:, cb:cb + 1]),
                                reads=[B(PB[5]), B(dtb_s)],
                                writes=[B(spscr[0])])
                            S.op("scalar", lambda e: e.activation(
                                out=spscr[0][:], in_=spscr[0][:], func=AF.Exp,
                                scale=-1.0),
                                reads=[B(spscr[0])], writes=[B(spscr[0])])
                            S.op("scalar", lambda e: e.activation(
                                out=spscr[0][:], in_=spscr[0][:], func=AF.Ln,
                                bias=1.0),
                                reads=[B(spscr[0])], writes=[B(spscr[0])])
                            S.op("scalar", lambda e, cb=cb: e.activation(
                                out=spscr[1][:], in_=PB[5][:, 0:GT],
                                func=AF.Relu, bias=dtb_s[:, cb:cb + 1]),
                                reads=[B(PB[5]), B(dtb_s)],
                                writes=[B(spscr[1])])
                            S.op("vector", lambda e: e.tensor_tensor(
                                out=spscr[0][:], in0=spscr[0][:],
                                in1=spscr[1][:], op=ALU.add),
                                reads=[B(spscr[0]), B(spscr[1])],
                                writes=[B(spscr[0])])
                            S.op("vector", lambda e, cb=cb: e.tensor_scalar(
                                out=gplus[cb][:, 0:GT], in0=spscr[0][:],
                                scalar1=acol_s[:, cb:cb + 1], scalar2=None,
                                op0=ALU.mult),
                                reads=[B(spscr[0]), B(acol_s)],
                                writes=[B(gplus[cb])])

                for cb in range(4):
                    for c in range(NCHUNK):
                        S.op("vector", lambda e, cb=cb, c=c:
                             e.tensor_tensor_scan(
                                 out=negb[cb][:, c * PAD:c * PAD + 128],
                                 data0=gplus[cb][:, c * 128:(c + 1) * 128],
                                 data1=zeros128[:], initial=0.0, op0=ALU.add,
                                 op1=ALU.add),
                             reads=[B(gplus[cb]), B(zeros128)],
                             writes=[B(negb[cb])])
                    S.op("scalar", lambda e, cb=cb: e.activation(
                        out=lamf[cb][:].rearrange("p (c s) -> p c s", c=NCHUNK),
                        in_=nbv[cb][:, :, 0:128], func=AF.Exp, scale=-1.0),
                        reads=[B(negb[cb])], writes=[B(lamf[cb])])
                    S.op("vector", lambda e, cb=cb: e.tensor_tensor(
                        out=kp[cb][:], in0=kn[cb][:], in1=lamf[cb][:],
                        op=ALU.mult),
                        reads=[B(kn[cb]), B(lamf[cb])], writes=[B(kp[cb])])
                    S.op("vector", lambda e, cb=cb: e.scalar_tensor_tensor(
                        out=qp[cb][:], in0=qn[cb][:], scalar=0.125,
                        in1=lamf[cb][:], op0=ALU.mult, op1=ALU.mult),
                        reads=[B(qn[cb]), B(lamf[cb])], writes=[B(qp[cb])])

                    def facpair(h_, cb=cb):
                        na = 128 // h_
                        S.op("vector", lambda e, cb=cb, h_=h_, na=na:
                             e.tensor_tensor(
                                 out=facscr[:].rearrange(
                                     "p (c a b) -> p c a b", c=NCHUNK, b=h_),
                                 in0=nbv[cb][:, :, 0:128:h_].unsqueeze(3)
                                 .broadcast_to((128, NCHUNK, na, h_)),
                                 in1=nbv[cb][:, :, 0:128].rearrange(
                                     "p c (a b) -> p c a b", b=h_),
                                 op=ALU.subtract),
                             reads=[B(negb[cb])], writes=[B(facscr)])
                        S.op("scalar", lambda e: e.activation(
                            out=rowF[:], in_=facscr[:], func=AF.Exp),
                            reads=[B(facscr)], writes=[B(rowF)])
                        S.op("vector", lambda e, cb=cb, h_=h_, na=na:
                             e.tensor_tensor(
                                 out=facscr[:].rearrange(
                                     "p (c a b) -> p c a b", c=NCHUNK, b=h_),
                                 in0=nbv[cb][:, :, 0:128].rearrange(
                                     "p c (a b) -> p c a b", b=h_),
                                 in1=nbv[cb][:, :, h_:129:h_].unsqueeze(3)
                                 .broadcast_to((128, NCHUNK, na, h_)),
                                 op=ALU.subtract),
                             reads=[B(negb[cb])], writes=[B(facscr)])
                        S.op("scalar", lambda e: e.activation(
                            out=colF[:], in_=facscr[:], func=AF.Exp),
                            reads=[B(facscr)], writes=[B(colF)])

                    def rowmuls(rn_, ra_, cb=cb):
                        S.op("vector", lambda e, cb=cb, rn_=rn_:
                             e.tensor_tensor(out=rn_[:], in0=kn[cb][:],
                                             in1=rowF[:], op=ALU.mult),
                             reads=[B(kn[cb]), B(rowF)], writes=[B(rn_)])
                        S.op("vector", lambda e, cb=cb, ra_=ra_:
                             e.scalar_tensor_tensor(
                                 out=ra_[:], in0=qn[cb][:], scalar=0.125,
                                 in1=rowF[:], op0=ALU.mult, op1=ALU.mult),
                             reads=[B(qn[cb]), B(rowF)], writes=[B(ra_)])

                    def colmul(ck_, cb=cb):
                        S.op("vector", lambda e, cb=cb, ck_=ck_:
                             e.tensor_tensor(out=ck_[:], in0=kn[cb][:],
                                             in1=colF[:], op=ALU.mult),
                             reads=[B(kn[cb]), B(colF)], writes=[B(ck_)])

                    for li, hl in enumerate((16, 32, 64)):
                        facpair(hl)
                        rowmuls(rnR[li][cb], raR[li][cb])
                        colmul(ckR[li][cb])
                    for li, hl in enumerate((2, 4, 8)):
                        facpair(hl)
                        rowmuls(rnL[li][cb], raL[li][cb])
                        colmul(ckL[li][cb])
                    S.op("scalar", lambda e, cb=cb: e.activation(
                        out=e2[:, 0:GT], in_=gplus[cb][:, 1:GT + 1],
                        func=AF.Exp, scale=-1.0),
                        reads=[B(gplus[cb])], writes=[B(e2)])
                    S.op("vector", lambda e, cb=cb: e.tensor_tensor(
                        out=ck2[cb][:], in0=kn[cb][:], in1=e2[:, 0:GT],
                        op=ALU.mult),
                        reads=[B(kn[cb]), B(e2)], writes=[B(ck2[cb])])
                    for c in range(NCHUNK):
                        S.op("vector", lambda e, cb=cb, c=c: e.tensor_scalar(
                            out=negcol[:, cb:cb + 1],
                            in0=negb[cb][:, c * PAD + 127:c * PAD + 128],
                            scalar1=-1.0, scalar2=None, op0=ALU.mult),
                            reads=[B(negb[cb])], writes=[B(negcol)])
                        S.op("scalar", lambda e, cb=cb, c=c: e.activation(
                            out=facscr[:, c * 128:(c + 1) * 128],
                            in_=negb[cb][:, c * PAD:c * PAD + 128],
                            func=AF.Exp, bias=negcol[:, cb:cb + 1]),
                            reads=[B(negb[cb]), B(negcol)], writes=[B(facscr)])
                    S.op("vector", lambda e, cb=cb: e.tensor_tensor(
                        out=kd[cb][:], in0=kn[cb][:], in1=facscr[:],
                        op=ALU.mult),
                        reads=[B(kn[cb]), B(facscr)], writes=[B(kd[cb])])

                for c in range(NCHUNK):
                    for cb in range(4):
                        S.op("vector", lambda e, cb=cb, c=c: e.tensor_tensor(
                            out=qkscr[:], in0=qn[cb][:, c * 128:(c + 1) * 128],
                            in1=kn[cb][:, c * 128:(c + 1) * 128], op=ALU.mult),
                            reads=[B(qn[cb]), B(kn[cb])], writes=[B(qkscr)])
                        S.op("tensor", lambda e, cb=cb, c=c: e.matmul(
                            out=PB[5][:, 480 + c * NH:480 + (c + 1) * NH],
                            lhsT=qkscr[:],
                            rhs=ind8T_s[:, cb * NH:(cb + 1) * NH],
                            start=(cb == 0 and c == 0),
                            stop=(cb == 3 and c == NCHUNK - 1)),
                            reads=[B(qkscr), B(ind8T_s)], writes=[B(PB[5])])
                S.op("scalar", lambda e: e.mul(
                    out=aqd_s[:], in_=PB[5][:, 480:480 + NCHUNK * NH],
                    mul=0.125),
                    reads=[B(PB[5])], writes=[B(aqd_s)])

                for c in range(NCHUNK):
                    tc = slice(c * 128, (c + 1) * 128)
                    for cb in range(4):
                        S.op("scalar", lambda e, cb=cb, c=c: e.activation(
                            out=lamC[:, cb:cb + 1],
                            in_=negb[cb][:, c * PAD + 127:c * PAD + 128],
                            func=AF.Exp, scale=-1.0),
                            reads=[B(negb[cb])], writes=[B(lamC)])
                    S.op("vector", lambda e, c=c: e.tensor_copy(
                        out=bmap[:].rearrange("p (h v) -> p h v", h=NH),
                        in_=bcols[:, c * NH:(c + 1) * NH].unsqueeze(2)
                        .broadcast_to((128, NH, HK))),
                        reads=[B(bcols)], writes=[B(bmap)])

                    RECTS = ((0, [(0, 16), (32, 48), (64, 80), (96, 112)]),
                             (1, [(0, 32), (64, 96)]),
                             (2, [(0, 64)]))
                    for mat in range(2):
                        rR_ = rnR if mat == 0 else raR
                        rL_ = rnL if mat == 0 else raL
                        dst = AT0 if mat == 0 else AQT
                        for grp in range(2):
                            S.op("vector", lambda e: e.memset(PB[0][:], 0.0),
                                 writes=[B(PB[0])])
                            for hh in range(4):
                                h = grp * 4 + hh
                                cb, par = h // 2, h % 2
                                kb = par * 64
                                for li, rects in RECTS:
                                    hw_ = 16 << li
                                    for s0, t0 in rects:
                                        S.op("tensor", lambda e, hh=hh, li=li,
                                             cb=cb, kb=kb, c=c, rR_=rR_,
                                             s0=s0, t0=t0, hw_=hw_: e.matmul(
                                                 out=PB[0][s0:s0 + hw_,
                                                           hh * 128 + t0:
                                                           hh * 128 + t0 + hw_],
                                                 lhsT=ckR[li][cb][kb:kb + 64,
                                                                  c * 128 + s0:
                                                                  c * 128 + s0 + hw_],
                                                 rhs=rR_[li][cb][kb:kb + 64,
                                                                 c * 128 + t0:
                                                                 c * 128 + t0 + hw_],
                                                 start=True, stop=True,
                                                 tile_position=(kb, s0 - s0 % 32)),
                                             reads=[B(ckR[li][cb]), B(rR_[li][cb])],
                                             writes=[B(PB[0])])
                                br = kn if mat == 0 else qn
                                S.op("tensor", lambda e, hh=hh, cb=cb, kb=kb,
                                     c=c, br=br: e.matmul(
                                         out=PB[1][:, hh * 128:(hh + 1) * 128],
                                         lhsT=ck2[cb][kb:kb + 64,
                                                      c * 128:(c + 1) * 128],
                                         rhs=br[cb][kb:kb + 64,
                                                    c * 128:(c + 1) * 128],
                                         start=True, stop=True,
                                         tile_position=(kb, 0)),
                                     reads=[B(ck2[cb]), B(br[cb])],
                                     writes=[B(PB[1])])
                                for li in range(3):
                                    pbk = PB[2 + li]
                                    S.op("tensor", lambda e, hh=hh, cb=cb,
                                         kb=kb, c=c, li=li, pbk=pbk, rL_=rL_:
                                         e.matmul(
                                             out=pbk[:, hh * 128:(hh + 1) * 128],
                                             lhsT=ckL[li][cb][kb:kb + 64,
                                                             c * 128:
                                                             (c + 1) * 128],
                                             rhs=rL_[li][cb][kb:kb + 64,
                                                             c * 128:
                                                             (c + 1) * 128],
                                             start=True, stop=True,
                                             tile_position=(kb, 0)),
                                         reads=[B(ckL[li][cb]),
                                                B(rL_[li][cb])],
                                         writes=[B(pbk)])
                            S.op("scalar", lambda e: e.copy(
                                out=farsb[:], in_=PB[0][:]),
                                reads=[B(PB[0])], writes=[B(farsb)])
                            mview = lambda li: masks_s[:, li * 512:
                                                       (li + 1) * 512]
                            S.op("vector", lambda e, mview=mview: e.select(
                                out=selscr[:], mask=mview(3),
                                on_true=PB[4][:], on_false=farsb[:]),
                                reads=[B(PB[4]), B(farsb), B(masks_s)],
                                writes=[B(selscr)])
                            S.op("vector", lambda e, mview=mview: e.select(
                                out=selscr[:], mask=mview(2),
                                on_true=PB[3][:], on_false=selscr[:]),
                                reads=[B(PB[3]), B(selscr), B(masks_s)],
                                writes=[B(selscr)])
                            S.op("vector", lambda e, mview=mview: e.select(
                                out=selscr[:], mask=mview(1),
                                on_true=PB[2][:], on_false=selscr[:]),
                                reads=[B(PB[2]), B(selscr), B(masks_s)],
                                writes=[B(selscr)])
                            S.op("vector", lambda e, dst=dst, grp=grp,
                                 mview=mview: e.select(
                                     out=dst[:, grp * 512:(grp + 1) * 512],
                                     mask=mview(0), on_true=PB[1][:],
                                     on_false=selscr[:]),
                                 reads=[B(PB[1]), B(selscr), B(masks_s)],
                                 writes=[B(dst)])

                    for h in range(NH):
                        S.op("tensor", lambda e, h=h: e.transpose(
                            out=PBb[6][:, (h % 4) * 128:(h % 4 + 1) * 128],
                            in_=AT0[:, h * 128:(h + 1) * 128],
                            identity=ident_b[:]),
                            reads=[B(AT0), B(ident_b)], writes=[B(PB[6])])
                        if h == 3 or h == 7:
                            off = (h // 4) * 512
                            S.op("vector", lambda e, off=off: e.tensor_copy(
                                out=A0[:, off:off + 512],
                                in_=PBb[6][:, 0:512]),
                                reads=[B(PB[6])], writes=[B(A0)])

                    prevA, prevAT = A0, AT0
                    for k_ in range(1, 7):
                        curA = Afrm[k_ % 2]
                        curAT = ATk[k_ - 1]
                        for h in range(NH):
                            S.op("vector", lambda e, h=h, prevA=prevA, c=c:
                                 e.tensor_scalar(
                                     out=Bfrm[:, h * 128:(h + 1) * 128],
                                     in0=prevA[:, h * 128:(h + 1) * 128],
                                     scalar1=bcols[:, c * NH + h:
                                                   c * NH + h + 1],
                                     scalar2=None, op0=ALU.mult),
                                 reads=[B(prevA), B(bcols)], writes=[B(Bfrm)])
                        for h in range(NH):
                            pa = PB[0] if h < 4 else PB[1]
                            pt_ = PB[2] if h < 4 else PB[3]
                            hh = h % 4
                            S.op("tensor", lambda e, h=h, hh=hh, pa=pa,
                                 prevAT=prevAT: e.matmul(
                                     out=pa[:, hh * 128:(hh + 1) * 128],
                                     lhsT=prevAT[:, h * 128:(h + 1) * 128],
                                     rhs=Bfrm[:, h * 128:(h + 1) * 128],
                                     start=True, stop=True),
                                 reads=[B(prevAT), B(Bfrm)], writes=[B(pa)])
                            S.op("tensor", lambda e, h=h, hh=hh, pt_=pt_,
                                 prevAT=prevAT: e.matmul(
                                     out=pt_[:, hh * 128:(hh + 1) * 128],
                                     lhsT=Bfrm[:, h * 128:(h + 1) * 128],
                                     rhs=prevAT[:, h * 128:(h + 1) * 128],
                                     start=True, stop=True),
                                 reads=[B(prevAT), B(Bfrm)], writes=[B(pt_)])
                        for pi, dst_, off in ((0, curA, 0), (1, curA, 512),
                                              (2, curAT, 0), (3, curAT, 512)):
                            S.op("vector", lambda e, pi=pi, dst_=dst_, off=off:
                                 e.tensor_copy(
                                     out=dst_[:, off:off + 512],
                                     in_=PB[pi][:]),
                                 reads=[B(PB[pi])], writes=[B(dst_)])
                        prevA, prevAT = curA, curAT

                    for cb in range(4):
                        S.op("tensor", lambda e, cb=cb, c=c: e.matmul(
                            out=PB[5][:, cb * 128:(cb + 1) * 128],
                            lhsT=kp[cb][:, c * 128:(c + 1) * 128],
                            rhs=S0b[cb][:], start=(cb == 0), stop=(cb == 3)),
                            reads=[B(kp[cb]), B(S0b[cb])], writes=[B(PB[5])])
                        S.op("tensor", lambda e, cb=cb, c=c: e.transpose(
                            out=PBb[6][:, cb * 128:(cb + 1) * 128],
                            in_=vs[cb][:, c * 128:(c + 1) * 128],
                            identity=ident_b[:]),
                            reads=[B(vs[cb]), B(ident_b)], writes=[B(PB[6])])
                    S.op("vector", lambda e: e.tensor_copy(
                        out=vT_s[:], in_=PBb[6][:, 0:512]),
                        reads=[B(PB[6])], writes=[B(vT_s)])
                    S.op("vector", lambda e: e.tensor_tensor(
                        out=Xscr[:], in0=vT_s[:], in1=PB[5][:, 0:512],
                        op=ALU.subtract),
                        reads=[B(vT_s), B(PB[5])], writes=[B(Xscr)])
                    S.op("vector", lambda e: e.tensor_tensor(
                        out=Rb[:], in0=Xscr[:], in1=bmap[:], op=ALU.mult),
                        reads=[B(Xscr), B(bmap)], writes=[B(Rb)])

                    Xcur = Rb
                    seq = [ATk[5], ATk[4], ATk[3], ATk[2], ATk[1], ATk[0], AT0]
                    for ridx, frm in enumerate(seq):
                        minus = ridx == len(seq) - 1
                        Xnext = Ub if minus else Xb[ridx % 2]
                        for h in range(NH):
                            S.op("tensor", lambda e, h=h, frm=frm, Xcur=Xcur:
                                 e.matmul(
                                     out=PB[4][:, h * HK:(h + 1) * HK],
                                     lhsT=frm[:, h * 128:(h + 1) * 128],
                                     rhs=Xcur[:, h * HK:(h + 1) * HK],
                                     start=(h == 0), stop=(h == 7)),
                                 reads=[B(frm), B(Xcur)], writes=[B(PB[4])])
                        S.op("vector", lambda e: e.tensor_tensor(
                            out=Xscr[:], in0=PB[4][:, 0:512], in1=bmap[:],
                            op=ALU.mult),
                            reads=[B(PB[4]), B(bmap)], writes=[B(Xscr)])
                        if minus:
                            S.op("vector", lambda e, Xcur=Xcur, Xnext=Xnext:
                                 e.tensor_tensor(
                                     out=Xnext[:], in0=Xcur[:], in1=Xscr[:],
                                     op=ALU.subtract),
                                 reads=[B(Xcur), B(Xscr)], writes=[B(Xnext)])
                        else:
                            S.op("vector", lambda e, Xcur=Xcur, Xnext=Xnext:
                                 e.tensor_tensor(
                                     out=Xnext[:], in0=Xscr[:], in1=Xcur[:],
                                     op=ALU.add),
                                 reads=[B(Xscr), B(Xcur)], writes=[B(Xnext)])
                        Xcur = Xnext

                    for cb in range(4):
                        S.op("tensor", lambda e, cb=cb, c=c: e.matmul(
                            out=PB[5][:, cb * 128:(cb + 1) * 128],
                            lhsT=qp[cb][:, c * 128:(c + 1) * 128],
                            rhs=S0b[cb][:], start=(cb == 0), stop=False),
                            reads=[B(qp[cb]), B(S0b[cb])], writes=[B(PB[5])])
                    for h in range(NH):
                        S.op("tensor", lambda e, h=h: e.matmul(
                            out=PB[5][:, h * HK:(h + 1) * HK],
                            lhsT=AQT[:, h * 128:(h + 1) * 128],
                            rhs=Ub[:, h * HK:(h + 1) * HK],
                            start=False, stop=(h == 7)),
                            reads=[B(AQT), B(Ub)], writes=[B(PB[5])])
                    S.op("scalar", lambda e: e.activation(
                        out=sqscr[:, 0:512], in_=PB[5][:, 0:512],
                        func=AF.Square),
                        reads=[B(PB[5])], writes=[B(sqscr)])
                    S.op("vector", lambda e: e.tensor_reduce(
                        out=rstd_s[:].unsqueeze(2),
                        in_=sqscr[:, 0:512].rearrange("p (h v) -> p h v",
                                                      h=NH),
                        axis=AXX, op=ALU.add),
                        reads=[B(sqscr)], writes=[B(rstd_s)])
                    S.op("scalar", lambda e: e.activation(
                        out=rstd_s[:], in_=rstd_s[:], func=AF.Ln,
                        scale=constc[:, 3:4], bias=constc[:, 0:1]),
                        reads=[B(rstd_s), B(constc)], writes=[B(rstd_s)])
                    S.op("scalar", lambda e: e.activation(
                        out=rstd_s[:], in_=rstd_s[:], func=AF.Exp,
                        scale=constc[:, 1:2]),
                        reads=[B(rstd_s), B(constc)], writes=[B(rstd_s)])
                    for h in range(NH):
                        S.op("vector", lambda e, h=h, c=c:
                             e.scalar_tensor_tensor(
                                 out=omid[:, h * HK:(h + 1) * HK],
                                 in0=Ub[:, h * HK:(h + 1) * HK],
                                 scalar=aqd_s[:, c * NH + h:c * NH + h + 1],
                                 in1=PB[5][:, h * HK:(h + 1) * HK],
                                 op0=ALU.mult, op1=ALU.add),
                             reads=[B(Ub), B(aqd_s), B(PB[5])],
                             writes=[B(omid)])
                        S.op("vector", lambda e, h=h: e.tensor_scalar(
                            out=omid[:, h * HK:(h + 1) * HK],
                            in0=omid[:, h * HK:(h + 1) * HK],
                            scalar1=rstd_s[:, h:h + 1], scalar2=None,
                            op0=ALU.mult),
                            reads=[B(omid), B(rstd_s)], writes=[B(omid)])
                    for cb in range(4):
                        S.op("tensor", lambda e, cb=cb: e.transpose(
                            out=PBb[6][:, cb * 128:(cb + 1) * 128],
                            in_=omid[:, cb * 128:(cb + 1) * 128],
                            identity=ident_b[:]),
                            reads=[B(omid), B(ident_b)], writes=[B(PB[6])])
                        S.op("vector", lambda e, cb=cb, c=c: e.tensor_tensor(
                            out=ofinT[:].rearrange("p (b t) -> p b t", b=4)
                            [:, cb, c * 128:(c + 1) * 128],
                            in0=PBb[6][:, cb * 128:(cb + 1) * 128],
                            in1=sgate[cb][:, c * 128:(c + 1) * 128],
                            op=ALU.mult),
                            reads=[B(PB[6]), B(sgate[cb])], writes=[B(ofinT)])

                    for cb in range(4):
                        S.op("tensor", lambda e, cb=cb, c=c: e.transpose(
                            out=PBb[6][:, cb * 128:(cb + 1) * 128],
                            in_=kd[cb][:, c * 128:(c + 1) * 128],
                            identity=ident_b[:]),
                            reads=[B(kd[cb]), B(ident_b)], writes=[B(PB[6])])
                    S.op("vector", lambda e: e.tensor_copy(
                        out=kdT_s[:], in_=PBb[6][:, 0:512]),
                        reads=[B(PB[6])], writes=[B(kdT_s)])
                    for h in range(NH):
                        cb, par = h // 2, h % 2
                        S.op("tensor", lambda e, h=h, cb=cb, par=par: e.matmul(
                            out=PB[7][par * 64:par * 64 + 64,
                                      cb * 128 + par * 64:
                                      cb * 128 + par * 64 + 64],
                            lhsT=kdT_s[:, cb * 128 + par * 64:
                                       cb * 128 + par * 64 + 64],
                            rhs=Ub[:, h * HK:(h + 1) * HK],
                            start=True, stop=True,
                            tile_position=(0, par * 64)),
                            reads=[B(kdT_s), B(Ub)], writes=[B(PB[7])])
                    for cb in range(4):
                        for par in range(2):
                            o_ = par * 64
                            S.op("vector", lambda e, cb=cb, o_=o_:
                                 e.scalar_tensor_tensor(
                                     out=S0[cb][o_:o_ + 64, o_:o_ + 64],
                                     in0=S0[cb][o_:o_ + 64, o_:o_ + 64],
                                     scalar=lamC[o_:o_ + 64, cb:cb + 1],
                                     in1=PB[7][o_:o_ + 64,
                                               cb * 128 + o_:
                                               cb * 128 + o_ + 64],
                                     op0=ALU.mult, op1=ALU.add),
                                 reads=[B(lamC), B(PB[7]), B(S0[cb])],
                                 writes=[B(S0[cb])])
                    for cb in range(4):
                        S.op("vector", lambda e, cb=cb: e.tensor_copy(
                            out=S0b[cb][:], in_=S0[cb][:]),
                            reads=[B(S0[cb])], writes=[B(S0b[cb])])

                for cb in range(4):
                    dma("sync",
                        cc_in.rearrange("(b p) t -> b p t", p=128)
                        [cb, :, T0:T0 + GT],
                        ofinT[:].rearrange("p (b t) -> p b t", b=4)[:, cb, :],
                        reads=[B(ofinT)], writes=[ccin_res])

            if no_cc:
                dma("gpsimd", cc_out[0:CH, :], cc_in, reads=[ccin_res],
                    writes=[])
            else:
                S.op("gpsimd", lambda e: e.collective_compute(
                    "AllGather", ALU.bypass, ins=[cc_in_f.opt()],
                    outs=[cc_out_f.opt()],
                    replica_groups=rgroups),
                    reads=[ccin_res], writes=[], kind="cc")

            with nc.Block("phaseA") as blockA:
                S.emit(blockA, sems)
            carry = S.final
            for sl in range(Sched.NPOOL):
                carry[f"d{sl}"] = S.dmagen[sl]

        for r_ in R.values():
            r_.w = None
            r_.rs = []
        ccin_res.w = None
        ccin_res.rs = []

        S2 = Sched(carry=carry, dmagen=S.dmagen)
        dma2 = mkdma(S2)

        bctx = contextlib.ExitStack()
        sbb = lambda n, s_, d_: sb(n, s_, d_, bctx)
        with bctx:
            wgT_s = sbb("wgT_s", (128, 8 * FFH), BF16)
            wuT_s = sbb("wuT_s", (128, 8 * FFH), BF16)
            wdT_s = sbb("wdT_s", (128, 24 * D), BF16)
            wo_s = sbb("wo_s", (128, 8 * D), BF16)
            onw_s = sbb("onw_s", (128, 1), F32)
            istage = [sbb(f"istage{i}", (128, FFH // 2), I32) for i in range(2)]
            fstage = sbb("fstage", (128, D), F32)
            xb2 = [sbb(f"xb2_{i}", (128, D), F32) for i in range(2)]
            ln2 = sbb("ln2", (128, 16), F32)
            scr2 = sbb("scr2", (128, D), F32)
            xnb2 = sbb("xnb2", (128, D), BF16)
            xnT2 = sbb("xnT2", (128, 8 * SUB), BF16)
            ofin2 = sbb("ofin2", (128, 8 * SUB), BF16)
            hbuf = sbb("hbuf", (128, 2 * SUB), BF16)
            sil2 = sbb("sil2", (128, SUB), F32)
            outsb = sbb("outsb", (128, D), F32)

            dma2("sync", onw_s[:], onw_in, writes=[B(onw_s)])
            for dblk in range(8):
                dma2("sync", fstage[:], wo_in[dblk * 128:(dblk + 1) * 128, :],
                     reads=[B(wo_s)], writes=[B(fstage)])
                S2.op("vector", lambda e, dblk=dblk: e.tensor_scalar(
                    out=wo_s[:, dblk * D:(dblk + 1) * D], in0=fstage[:],
                    scalar1=onw_s[:], scalar2=None, op0=ALU.mult),
                    reads=[B(fstage), B(onw_s)], writes=[B(wo_s)])
            HF = FFH // 2
            for win, wsb2 in ((wgT_in, wgT_s), (wuT_in, wuT_s)):
                for dblk in range(8):
                    for hf in range(2):
                        st_ = istage[(dblk * 2 + hf) % 2]
                        dma2("sync", st_[:],
                             win[dblk * 128:(dblk + 1) * 128,
                                 hf * HF:(hf + 1) * HF],
                             reads=[B(wsb2)], writes=[B(st_)])
                        S2.op("vector", lambda e, dblk=dblk, hf=hf, wsb2=wsb2,
                              st_=st_: e.tensor_copy(
                                  out=wsb2[:, dblk * FFH + hf * HF:
                                           dblk * FFH + (hf + 1) * HF],
                                  in_=st_[:]),
                              reads=[B(st_)], writes=[B(wsb2)])
            for hb in range(24):
                st_ = istage[hb % 2]
                dma2("sync", st_[:, 0:D], wdT_in[hb * 128:(hb + 1) * 128, :],
                     reads=[B(wdT_s)], writes=[B(st_)])
                S2.op("vector", lambda e, hb=hb, st_=st_: e.tensor_copy(
                    out=wdT_s[:, hb * D:(hb + 1) * D], in_=st_[:, 0:D]),
                    reads=[B(st_)], writes=[B(wdT_s)])

            regres = Res()

            def loadreg(e):
                reg = e.alloc_register("hgreg")
                loadreg.reg = reg
                return e.reg_load(reg, soff[0:1, 0:1])
            beng = "scalar" if simdma else "gpsimd"
            S2.op(beng, loadreg, reads=[B(soff)], writes=[regres])

            xh_t = xh_in.rearrange("(n p) d -> n p d", p=128)
            out_tt = out_t.rearrange("(n p) d -> n p d", p=128)
            ccov = cc_out.rearrange("(b p) t -> b p t", p=128)

            for s_ in range(0 if skipb else NSUBl):
                s0 = s_ * SUB
                for i in range(2):
                    xb = xb2[i]
                    emit_ln(S2, dma2, xb, xh_t[s_ * 2 + i], ln2, scr2, xnb2,
                            None)
                    for j in range(8):
                        S2.op("tensor", lambda e, j=j: e.transpose(
                            out=PBb[7][:, j * 128:(j + 1) * 128],
                            in_=xnb2[:, j * 128:(j + 1) * 128],
                            identity=ident_b[:]),
                            reads=[B(xnb2), B(ident_b)], writes=[B(PB[7])])
                    S2.op("vector", lambda e, i=i: e.tensor_copy(
                        out=xnT2[:].rearrange("p (b t) -> p b t", b=8)
                        [:, :, i * 128:(i + 1) * 128],
                        in_=PBb[7][:].rearrange("p (b t) -> p b t", b=8)),
                        reads=[B(PB[7])], writes=[B(xnT2)])

                def ccread(e, dsem, s0=s0):
                    if no_branch:
                        for bb_ in range(8):
                            e.dma_start(
                                out=ofin2[:].rearrange("p (b t) -> p b t",
                                                       b=8)[:, bb_, :],
                                in_=ccov[bb_, :, s0:s0 + SUB]).then_inc(
                                    dsem, 16)
                        return e.nop()
                    reg = loadreg.reg
                    with e.If_eq(reg, 0):
                        for bb_ in range(8):
                            e.dma_start(
                                out=ofin2[:].rearrange("p (b t) -> p b t",
                                                       b=8)[:, bb_, :],
                                in_=ccov[bb_, :, s0:s0 + SUB]).then_inc(
                                    dsem, 16)
                    with e.Else():
                        for bb_ in range(8):
                            e.dma_start(
                                out=ofin2[:].rearrange("p (b t) -> p b t",
                                                       b=8)[:, bb_, :],
                                in_=ccov[bb_, :, HALF_ + s0:HALF_ + s0 + SUB]
                            ).then_inc(dsem, 16)
                    return e.nop()
                S2.op(beng, ccread, reads=[regres], writes=[B(ofin2)],
                      kind="dma", n_dma=8)

                for hb in range(24):
                    for wsb2, pb_ in ((wgT_s, PB[4]), (wuT_s, PB[5])):
                        for j in range(8):
                            S2.op("tensor", lambda e, j=j, hb=hb, wsb2=wsb2,
                                  pb_=pb_: e.matmul(
                                      out=pb_[:, 0:SUB],
                                      lhsT=wsb2[:, j * FFH + hb * 128:
                                                j * FFH + (hb + 1) * 128],
                                      rhs=xnT2[:, j * SUB:(j + 1) * SUB],
                                      start=(j == 0), stop=(j == 7)),
                                  reads=[B(wsb2), B(xnT2)], writes=[B(pb_)])
                    S2.op("scalar", lambda e: e.activation(
                        out=sil2[:, 0:SUB], in_=PB[4][:, 0:SUB], func=AF.Silu,
                        scale=constc[:, 4:5]),
                        reads=[B(PB[4]), B(constc)], writes=[B(sil2)])
                    S2.op("vector", lambda e, hb=hb: e.scalar_tensor_tensor(
                        out=hbuf[:, (hb % 2) * SUB:(hb % 2 + 1) * SUB],
                        in0=sil2[:, 0:SUB], scalar=1.0 / 16129.0,
                        in1=PB[5][:, 0:SUB], op0=ALU.mult, op1=ALU.mult),
                        reads=[B(sil2), B(PB[5])], writes=[B(hbuf)])
                    for mb in range(2):
                        for nh_ in range(2):
                            S2.op("tensor", lambda e, hb=hb, mb=mb, nh_=nh_:
                                  e.matmul(
                                      out=PB[mb * 2 + nh_][:],
                                      lhsT=hbuf[:, (hb % 2) * SUB + mb * 128:
                                                (hb % 2) * SUB +
                                                (mb + 1) * 128],
                                      rhs=wdT_s[:, hb * D + nh_ * 512:
                                                hb * D + (nh_ + 1) * 512],
                                      start=(hb == 0), stop=False),
                                  reads=[B(hbuf), B(wdT_s)],
                                  writes=[B(PB[mb * 2 + nh_])])
                for bb_ in range(8):
                    for mb in range(2):
                        for nh_ in range(2):
                            S2.op("tensor", lambda e, bb_=bb_, mb=mb,
                                  nh_=nh_: e.matmul(
                                      out=PB[mb * 2 + nh_][:],
                                      lhsT=ofin2[:, bb_ * SUB + mb * 128:
                                                 bb_ * SUB + (mb + 1) * 128],
                                      rhs=wo_s[:, bb_ * D + nh_ * 512:
                                               bb_ * D + (nh_ + 1) * 512],
                                      start=False, stop=(bb_ == 7)),
                                  reads=[B(ofin2), B(wo_s)],
                                  writes=[B(PB[mb * 2 + nh_])])
                for mb in range(2):
                    for nh_ in range(2):
                        S2.op("vector", lambda e, mb=mb, nh_=nh_:
                              e.tensor_tensor(
                                  out=outsb[:, nh_ * 512:(nh_ + 1) * 512],
                                  in0=PB[mb * 2 + nh_][:],
                                  in1=xb2[mb][:, nh_ * 512:(nh_ + 1) * 512],
                                  op=ALU.add),
                              reads=[B(PB[mb * 2 + nh_]), B(xb2[mb])],
                              writes=[B(outsb)])
                    dma2(beng, out_tt[s_ * 2 + mb], outsb[:],
                         reads=[B(outsb)], writes=[outres])

            if skipb:
                S2.op("vector", lambda e: e.memset(outsb[:], 0.0),
                      writes=[B(outsb)])
                for ti_ in range(HALF_ // 128):
                    dma2(beng, out_tt[ti_], outsb[:], reads=[B(outsb)],
                         writes=[outres])
            S2.op(beng, lambda e: e.nop() if simdma else e.engine_nop(),
                  reads=[outres], writes=[])

            with nc.Block("phaseB") as blockB:
                S2.emit(blockB, sems, barrier_in=True)

    return nc


def kernel(**inputs):
    x = np.ascontiguousarray(inputs["L0_attn_hidden_states"])
    Wq = inputs["L0_attn_Wq"]
    Wk = inputs["L0_attn_Wk"]
    Wv = inputs["L0_attn_Wv"]
    qcw = inputs["L0_attn_qcw"]
    qcb = inputs["L0_attn_qcb"]
    kcw = inputs["L0_attn_kcw"]
    kcb = inputs["L0_attn_kcb"]
    vcw = inputs["L0_attn_vcw"]
    vcb = inputs["L0_attn_vcb"]
    Wf1 = inputs["L0_attn_Wf1"]
    Wf2 = inputs["L0_attn_Wf2"]
    Wb = inputs["L0_attn_Wb"]
    A_log = np.asarray(inputs["L0_attn_A_log"])
    dt_bias = np.asarray(inputs["L0_attn_dt_bias"])
    Wg1 = inputs["L0_attn_Wg1"]
    Wg2 = inputs["L0_attn_Wg2"]
    bg2 = np.asarray(inputs["L0_attn_bg2"])
    onw = np.asarray(inputs["L0_attn_o_norm_w"])
    Wo = inputs["L0_attn_Wo"]
    wg = inputs["L0_ffn_w_gate"]
    wu = inputs["L0_ffn_w_up"]
    wd = inputs["L0_ffn_w_down"]

    ident = np.eye(128, dtype=np.float32)
    s = np.arange(C)[:, None]
    t = np.arange(C)[None, :]
    masks = []
    for lv in (2, 4, 8, 16):
        h = lv // 2
        m = ((s // lv) == (t // lv)) & ((s % lv) < h) & ((t % lv) >= h)
        masks.append(np.tile(m.astype(np.uint8), (1, 4)))
    masks = np.ascontiguousarray(np.stack(masks))
    ind8 = np.zeros((4, NH, 128), np.float32)
    ind8T = np.zeros((4, 128, NH), np.float32)
    for cb in range(4):
        for par in range(2):
            h = cb * 2 + par
            ind8[cb, h, par * 64:(par + 1) * 64] = 1.0
            ind8T[cb, par * 64:(par + 1) * 64, h] = 1.0

    wgT = np.ascontiguousarray(wg.T)
    wuT = np.ascontiguousarray(wu.T)
    wdT = np.ascontiguousarray(wd.T)
    onw_t = np.ascontiguousarray(np.tile(onw.reshape(-1, 1), (2, 1)))

    in_maps = []
    for core in range(8):
        b, hg = core // 2, core % 2
        sl = slice(hg * CH, (hg + 1) * CH)
        hsl = slice(hg * NH, (hg + 1) * NH)
        cvw = np.ascontiguousarray(np.stack([qcw[sl], kcw[sl], vcw[sl]]))
        cvb = np.ascontiguousarray(np.stack(
            [qcb[sl].reshape(-1, 1), kcb[sl].reshape(-1, 1),
             vcb[sl].reshape(-1, 1)]))
        m = dict(
            x=np.ascontiguousarray(x[b]),
            xh=np.ascontiguousarray(x[b, hg * HALF:(hg + 1) * HALF]),
            wq=np.ascontiguousarray(Wq[:, sl]),
            wk=np.ascontiguousarray(Wk[:, sl]),
            wv=np.ascontiguousarray(Wv[:, sl]),
            cvw=cvw, cvb=cvb,
            wf1=np.ascontiguousarray(Wf1),
            wf2=np.ascontiguousarray(Wf2[:, sl]),
            wb=np.ascontiguousarray(Wb[:, hsl]),
            alog=np.ascontiguousarray(A_log[hsl].reshape(-1, 1)),
            dtb=np.ascontiguousarray(dt_bias[sl].reshape(-1, 1)),
            wg1=np.ascontiguousarray(Wg1),
            wg2=np.ascontiguousarray(Wg2[:, sl]),
            bg2=np.ascontiguousarray(bg2[sl].reshape(-1, 1)),
            onw=onw_t,
            wo=np.ascontiguousarray(Wo),
            wgT=wgT, wuT=wuT, wdT=wdT,
            ident=ident, masks=masks, ind8=ind8, ind8T=ind8T,
            coreoff=np.array([[hg]], dtype=np.int32),
        )
        in_maps.append(m)

    nc = build_nc()
    res = run_bass_kernel_spmd(nc, in_maps, core_ids=list(range(8)))
    out = np.empty((4, T, D), np.float32)
    for core in range(8):
        b, hg = core // 2, core % 2
        out[b, hg * HALF:(hg + 1) * HALF] = res.results[core]["out"]
    return out


# version 05
# speedup vs baseline: 1.0007x; 1.0007x over previous
import contextlib
import numpy as np
import concourse.bass as bass
import concourse.mybir as mybir
from concourse.bass_utils import run_bass_kernel_spmd

F32 = mybir.dt.float32
BF16 = mybir.dt.bfloat16
I32 = mybir.dt.int32
U8 = mybir.dt.uint8
AF = mybir.ActivationFunctionType
ALU = mybir.AluOpType
AXX = mybir.AxisListType.X

T = 8192
D = 1024
CH = 512
NH = 8
HK = 64
C = 128
GT = 256
NCHUNK = 2
NG = T // GT
HALF = 4096
SUB = 256
NSUB = HALF // SUB
FFH = 3072
PAD = 136


class Res:
    __slots__ = ("w", "rs", "wkind")

    def __init__(self):
        self.w = None
        self.rs = []
        self.wkind = "compute"


class Sched:
    ENGS = ("sync", "tensor", "vector", "scalar", "gpsimd")
    NPOOL = 6

    def __init__(self, carry=None, dmagen=None):
        self.ops = []
        self.carry = carry or {e: 0 for e in self.ENGS}
        self.carry.setdefault("cc", 0)
        self.dmagen = list(dmagen) if dmagen else [0] * self.NPOOL
        self.dmaslot_next = 0

    def op(self, engine, fn, reads=(), writes=(), kind="compute",
           extra_deps=(), n_dma=1):
        deps = set()
        for r in reads:
            if r.w is not None:
                deps.add(r.w)
        for w in writes:
            if w.w is not None:
                deps.add(w.w)
            deps.update(w.rs)
        for t_ in extra_deps:
            if t_ is not None:
                deps.add(t_)
        tok = len(self.ops)
        o = dict(engine=engine, fn=fn, deps=deps, kind=kind, inc=False,
                 cnt=None, slot=None, base=None, n_dma=n_dma)
        if kind == "dma":
            sl = self.dmaslot_next
            self.dmaslot_next = (sl + 1) % self.NPOOL
            o["slot"] = sl
            o["base"] = self.dmagen[sl]
            self.dmagen[sl] += 16 * n_dma
            o["cnt"] = self.dmagen[sl]
        self.ops.append(o)
        for r in reads:
            if tok not in r.rs:
                r.rs.append(tok)
        for w in writes:
            w.w = tok
            w.rs = []
        return tok

    def emit(self, block, sems, barrier_in=False):
        GAP = 6
        pos = {}
        pc = {e: 0 for e in self.ENGS}
        for i, o in enumerate(self.ops):
            pos[i] = pc[o["engine"]]
            pc[o["engine"]] += 1
        for i, o in enumerate(self.ops):
            if o["kind"] == "cc":
                o["inc"] = True
            o["syncdeps"] = []
            for d in o["deps"]:
                dop = self.ops[d]
                if dop["kind"] in ("dma", "cc"):
                    dop["inc"] = True
                    o["syncdeps"].append(d)
                elif dop["engine"] != o["engine"]:
                    dop["inc"] = True
                    o["syncdeps"].append(d)
                elif (dop["engine"] != "tensor"
                      and pos[i] - pos[d] < GAP):
                    dop["inc"] = True
                    o["syncdeps"].append(d)
        last = {}
        for i, o in enumerate(self.ops):
            if o["kind"] == "compute":
                last[o["engine"]] = i
        for e_, i in last.items():
            self.ops[i]["inc"] = True
        cnt = dict(self.carry)
        for o in self.ops:
            if o["kind"] == "dma":
                continue
            if o["inc"]:
                key = "cc" if o["kind"] == "cc" else o["engine"]
                cnt[key] += 1
                o["cnt"] = cnt[key]
        self.final = cnt

        active = {o["engine"] for o in self.ops}
        for ename in self.ENGS:
            if ename not in active:
                continue

            def body(e, ename=ename):
                hw = {k: self.carry.get(k, 0) for k in
                      list(self.ENGS) + ["cc"]}
                dhw = [0] * self.NPOOL
                if barrier_in:
                    for o_e in list(self.ENGS) + ["cc"]:
                        if o_e != ename and self.carry.get(o_e, 0) > 0:
                            e.wait_ge(sems[o_e], self.carry[o_e])
                    for sl in range(self.NPOOL):
                        g0 = self.carry.get(f"d{sl}", 0)
                        if g0 > 0:
                            e.wait_ge(sems[f"d{sl}"], g0)
                            dhw[sl] = g0
                for o in self.ops:
                    if o["engine"] != ename:
                        continue
                    for d in sorted(o["syncdeps"]):
                        dop = self.ops[d]
                        if dop["kind"] == "dma":
                            sl, c_ = dop["slot"], dop["cnt"]
                            if c_ > dhw[sl]:
                                e.wait_ge(sems[f"d{sl}"], c_)
                                dhw[sl] = c_
                            continue

                        key = "cc" if dop["kind"] == "cc" else dop["engine"]
                        c_ = dop["cnt"]
                        if c_ is not None and c_ > hw[key]:
                            e.wait_ge(sems[key], c_)
                            hw[key] = c_
                    if o["kind"] == "dma":
                        sl = o["slot"]
                        if o["base"] > dhw[sl]:
                            e.wait_ge(sems[f"d{sl}"], o["base"])
                            dhw[sl] = o["base"]
                        inst = o["fn"](e, sems[f"d{sl}"])
                    else:
                        inst = o["fn"](e)
                        if o["inc"]:
                            if o["kind"] == "cc":
                                inst.then_inc(sems["cc"])
                            else:
                                inst.then_inc(sems[ename], 1)
            getattr(block, ename)(body)


def build_nc(T_=T, HALF_=HALF, ncores=8, no_branch=False, no_cc=False, simdma=False, skipb=False, skipa=False):
    NGl = T_ // GT
    NSUBl = HALF_ // SUB
    rgroups = [[2 * i, 2 * i + 1] for i in range(ncores // 2)]
    nc = bass.Bass("TRN2", target_bir_lowering=False)

    def din(name, shape, dt):
        return nc.dram_tensor(name, list(shape), dt, kind="ExternalInput").ap()

    x_in = din("x", (T_, D), F32)
    xh_in = din("xh", (HALF_, D), F32)
    wq_in = din("wq", (D, CH), F32)
    wk_in = din("wk", (D, CH), F32)
    wv_in = din("wv", (D, CH), F32)
    cvw_in = din("cvw", (3, CH, 4), F32)
    cvb_in = din("cvb", (3, CH, 1), F32)
    wf1_in = din("wf1", (D, 64), F32)
    wf2_in = din("wf2", (64, CH), F32)
    wb_in = din("wb", (D, NH), F32)
    alog_in = din("alog", (NH, 1), F32)
    dtb_in = din("dtb", (CH, 1), F32)
    wg1_in = din("wg1", (D, 64), F32)
    wg2_in = din("wg2", (64, CH), F32)
    bg2_in = din("bg2", (CH, 1), F32)
    onw_in = din("onw", (128, 1), F32)
    wo_in = din("wo", (D, D), F32)
    wgT_in = din("wgT", (D, FFH), I32)
    wuT_in = din("wuT", (D, FFH), I32)
    wdT_in = din("wdT", (FFH, D), I32)
    ident_in = din("ident", (128, 128), F32)
    masks_in = din("masks", (4, C, 4 * C), U8)
    ind8_in = din("ind8", (4, NH, 128), F32)
    ind8T_in = din("ind8T", (4, 128, NH), F32)
    coreoff_in = din("coreoff", (1, 1), I32)

    out_t = nc.dram_tensor("out", [HALF_, D], F32, kind="ExternalOutput").ap()
    cc_in_f = nc.dram_tensor("cc_in", [CH, T_ // 2], F32, kind="Internal").ap()
    cc_out_f = nc.dram_tensor("cc_out", [D, T_ // 2], F32, kind="Internal").ap()
    cc_in = cc_in_f.bitcast(BF16)
    cc_out = cc_out_f.bitcast(BF16)

    ctx = contextlib.ExitStack()
    with ctx:
        def sb(name, shape, dt, c=ctx):
            return c.enter_context(nc.sbuf_tensor(name, list(shape), dt))

        sems = {e: ctx.enter_context(nc.semaphore("sem_" + e))
                for e in Sched.ENGS}
        sems["cc"] = ctx.enter_context(nc.semaphore("sem_cc"))
        for sl in range(Sched.NPOOL):
            sems[f"d{sl}"] = ctx.enter_context(nc.semaphore(f"sem_d{sl}"))
        PB = [ctx.enter_context(nc.psum_tensor(f"PB{i}", [128, 512], F32))
              for i in range(8)]
        PBb = [p[:].bitcast(BF16) for p in PB]

        R = {}

        def B(buf):
            key = id(buf)
            if key not in R:
                R[key] = Res()
            return R[key]

        ccin_res = Res()
        outres = Res()

        S = Sched()

        def mkdma(SS):
            def dma(engine, dst, src, reads=(), writes=(), deps=()):
                return SS.op(engine,
                             lambda e, dsem, dst=dst, src=src: e.dma_start(
                                 out=dst, in_=src).then_inc(dsem, 16),
                             reads=reads, writes=writes, kind="dma",
                             extra_deps=deps)
            return dma
        dma = mkdma(S)

        ident_f = sb("ident_f", (128, 128), F32)
        ident_b = sb("ident_b", (128, 128), BF16)
        soff = sb("soff", (1, 1), I32)
        constc = sb("constc", (128, 8), F32)
        dma("sync", ident_f[:], ident_in, writes=[B(ident_f)])
        dma("sync", soff[:], coreoff_in, writes=[B(soff)])
        S.op("vector", lambda e: e.tensor_copy(out=ident_b[:], in_=ident_f[:]),
             reads=[B(ident_f)], writes=[B(ident_b)])
        for ci, cv in enumerate((1e-5, -0.5, 0.5, 1.0 / HK, 1.0 / 127.0)):
            S.op("vector", lambda e, ci=ci, cv=cv: e.memset(
                constc[:, ci:ci + 1], cv), writes=[B(constc)])

        actx = contextlib.ExitStack()
        sba = lambda n, s_, d_: sb(n, s_, d_, actx)
        with actx:
            masks_s = sba("masks_s", (128, 4 * 4 * C), U8)
            ind8_s = sba("ind8_s", (NH, 4 * 128), BF16)
            ind8T_s = sba("ind8T_s", (128, 4 * NH), BF16)
            stage_f = sba("stage_f", (128, 8 * CH), F32)

            wq_s = sba("wq_s", (128, 8 * CH), BF16)
            wk_s = sba("wk_s", (128, 8 * CH), BF16)
            wv_s = sba("wv_s", (128, 8 * CH), BF16)
            wf1_s = sba("wf1_s", (128, 8 * 64), BF16)
            wg1_s = sba("wg1_s", (128, 8 * 64), BF16)
            wb_s = sba("wb_s", (128, 8 * NH), BF16)
            wf2_s = sba("wf2_s", (64, CH), BF16)
            wg2_s = sba("wg2_s", (64, CH), BF16)

            cvw_s = sba("cvw_s", (128, 3 * 4 * 4), F32)
            cvb_s = sba("cvb_s", (128, 3 * 4), F32)
            dtb_s = sba("dtb_s", (128, 4), F32)
            bg2_s = sba("bg2_s", (128, 4), F32)
            acol_s = sba("acol_s", (128, 4), F32)
            alog_s = sba("alog_s", (NH, 1), F32)
            aexp_b = sba("aexp_b", (NH, 1), BF16)

            xbuf = [sba(f"xbuf{i}", (128, D), F32) for i in range(2)]
            lnstat = sba("lnstat", (128, 16), F32)
            scr_big = sba("scr_big", (128, D), F32)
            xnb = sba("xnb", (128, D), BF16)
            xnTg = sba("xnTg", (128, 8 * GT), BF16)

            qs = [sba(f"qs{cb}", (128, GT), BF16) for cb in range(4)]
            ks = [sba(f"ks{cb}", (128, GT), BF16) for cb in range(4)]
            vs = [sba(f"vs{cb}", (128, GT), BF16) for cb in range(4)]
            cbuf = [sba(f"cbuf{p}", (128, 4 * (GT + 4)), BF16)
                    for p in range(3)]
            sqscr = sba("sqscr", (128, 512), BF16)
            rq_s = sba("rq_s", (NH, GT), BF16)
            rk_s = sba("rk_s", (NH, GT), BF16)
            kn = [sba(f"kn{cb}", (128, GT), BF16) for cb in range(4)]
            qn = [sba(f"qn{cb}", (128, GT), BF16) for cb in range(4)]
            bcols = sba("bcols", (128, NCHUNK * NH), F32)
            g1s = sba("g1s", (64, GT), BF16)
            spscr = [sba(f"spscr{i}", (128, GT), F32) for i in range(2)]
            gplus = [sba(f"gplus{cb}", (128, GT + 8), F32) for cb in range(4)]
            negb = [sba(f"negb{cb}", (128, NCHUNK * PAD), F32)
                    for cb in range(4)]
            zeros128 = sba("zeros128", (128, 128), F32)
            sgate = [sba(f"sgate{cb}", (128, GT), BF16) for cb in range(4)]

            lamf = [sba(f"lamf{cb}", (128, GT), BF16) for cb in range(4)]
            kp = [sba(f"kp{cb}", (128, GT), BF16) for cb in range(4)]
            qp = [sba(f"qp{cb}", (128, GT), BF16) for cb in range(4)]
            kd = [sba(f"kd{cb}", (128, GT), BF16) for cb in range(4)]
            facscr = sba("facscr", (128, GT), F32)
            rowF = sba("rowF", (128, GT), BF16)
            colF = sba("colF", (128, GT), BF16)
            rnR = [[sba(f"rnR{lv}_{cb}", (128, GT), BF16) for cb in range(4)]
                   for lv in range(3)]
            raR = [[sba(f"raR{lv}_{cb}", (128, GT), BF16) for cb in range(4)]
                   for lv in range(3)]
            ckR = [[sba(f"ckR{lv}_{cb}", (128, GT), BF16) for cb in range(4)]
                   for lv in range(3)]
            rnL = [[sba(f"rnL{lv}_{cb}", (128, GT), BF16) for cb in range(4)]
                   for lv in range(3)]
            raL = [[sba(f"raL{lv}_{cb}", (128, GT), BF16) for cb in range(4)]
                   for lv in range(3)]
            ckL = [[sba(f"ckL{lv}_{cb}", (128, GT), BF16) for cb in range(4)]
                   for lv in range(3)]
            ck2 = [sba(f"ck2_{cb}", (128, GT), BF16) for cb in range(4)]
            e2 = sba("e2", (128, GT + 8), BF16)
            aqd_s = sba("aqd_s", (128, NCHUNK * NH), F32)
            rstd_s = sba("rstd_s", (128, NH), F32)
            lamC = sba("lamC", (128, 4), F32)
            negcol = sba("negcol", (128, 4), F32)
            bmap = sba("bmap", (128, NH * HK), F32)

            AT0 = sba("AT0", (128, NH * C), BF16)
            AQT = sba("AQT", (128, NH * C), BF16)
            A0 = sba("A0", (128, NH * C), BF16)
            Afrm = [sba(f"Afrm{k}", (128, NH * C), BF16) for k in range(2)]
            ATk = [sba(f"ATk{k}", (128, NH * C), BF16) for k in range(6)]
            Bfrm = sba("Bfrm", (128, NH * C), BF16)
            selscr = sba("selscr", (128, 512), BF16)
            farsb = sba("farsb", (128, 512), BF16)
            Xb = [sba(f"Xb{k}", (128, NH * HK), BF16) for k in range(2)]
            Rb = sba("Rb", (128, NH * HK), BF16)
            Xscr = sba("Xscr", (128, NH * HK), F32)
            vT_s = sba("vT_s", (128, NH * HK), BF16)
            Ub = sba("Ub", (128, NH * HK), BF16)
            S0 = [sba(f"S0_{cb}", (128, 128), F32) for cb in range(4)]
            S0b = [sba(f"S0b_{cb}", (128, 128), BF16) for cb in range(4)]
            kdT_s = sba("kdT_s", (128, NH * HK), BF16)
            omid = sba("omid", (128, NH * HK), BF16)
            ofinT = sba("ofinT", (128, 4 * GT), BF16)
            qkscr = sba("qkscr", (128, C), BF16)

            import os
            KLVL = int(os.environ.get("KLVL", "9"))
            for li in range(4 if KLVL >= 1 else 0):
                dma("sync", masks_s[:, li * 4 * C:(li + 1) * 4 * C],
                    masks_in[li], writes=[B(masks_s)])
            for cb in range(4 if KLVL >= 1 else 0):
                dma("sync", stage_f[0:NH, cb * 128:(cb + 1) * 128],
                    ind8_in[cb], writes=[B(stage_f)])
            S.op("vector", lambda e: e.tensor_copy(
                out=ind8_s[:], in_=stage_f[0:NH, 0:4 * 128]),
                reads=[B(stage_f)], writes=[B(ind8_s)])
            for cb in range(4):
                dma("sync", stage_f[:, cb * NH:(cb + 1) * NH], ind8T_in[cb],
                    reads=[B(ind8_s)], writes=[B(stage_f)])
            S.op("vector", lambda e: e.tensor_copy(
                out=ind8T_s[:], in_=stage_f[:, 0:4 * NH]),
                reads=[B(stage_f)], writes=[B(ind8T_s)])
            for win, wsb in (((wq_in, wq_s), (wk_in, wk_s), (wv_in, wv_s))
                             if KLVL >= 2 else ()):
                for j in range(8):
                    dma("sync", stage_f[:, j * CH:(j + 1) * CH],
                        win[j * 128:(j + 1) * 128, :],
                        reads=[B(wsb)], writes=[B(stage_f)])
                S.op("vector", lambda e, wsb=wsb: e.tensor_copy(
                    out=wsb[:], in_=stage_f[:]),
                    reads=[B(stage_f)], writes=[B(wsb)])
            for win, wsb, w_ in (((wf1_in, wf1_s, 64), (wg1_in, wg1_s, 64),
                                 (wb_in, wb_s, NH)) if KLVL >= 2 else ()):
                for j in range(8):
                    dma("sync", stage_f[:, j * w_:(j + 1) * w_],
                        win[j * 128:(j + 1) * 128, :],
                        reads=[B(wsb)], writes=[B(stage_f)])
                S.op("vector", lambda e, wsb=wsb, w_=w_: e.tensor_copy(
                    out=wsb[:], in_=stage_f[:, 0:8 * w_]),
                    reads=[B(stage_f)], writes=[B(wsb)])
            for win, wsb in (((wf2_in, wf2_s), (wg2_in, wg2_s))
                             if KLVL >= 2 else ()):
                dma("sync", stage_f[0:64, 0:CH], win,
                    reads=[B(wsb)], writes=[B(stage_f)])
                S.op("vector", lambda e, wsb=wsb: e.tensor_copy(
                    out=wsb[:], in_=stage_f[0:64, 0:CH]),
                    reads=[B(stage_f)], writes=[B(wsb)])
            for pj in range(3 if KLVL >= 3 else 0):
                for cb in range(4):
                    dma("sync", cvw_s[:, (pj * 4 + cb) * 4:(pj * 4 + cb) * 4 + 4],
                        cvw_in[pj, cb * 128:(cb + 1) * 128, :],
                        writes=[B(cvw_s)])
                    dma("sync", cvb_s[:, pj * 4 + cb:pj * 4 + cb + 1],
                        cvb_in[pj, cb * 128:(cb + 1) * 128, :],
                        writes=[B(cvb_s)])
            for cb in range(4 if KLVL >= 3 else 0):
                dma("sync", dtb_s[:, cb:cb + 1],
                    dtb_in[cb * 128:(cb + 1) * 128, :], writes=[B(dtb_s)])
                dma("sync", bg2_s[:, cb:cb + 1],
                    bg2_in[cb * 128:(cb + 1) * 128, :], writes=[B(bg2_s)])
            if KLVL >= 4:
                dma("sync", alog_s[:], alog_in, writes=[B(alog_s)])
                S.op("scalar", lambda e: e.activation(
                    out=aexp_b[:], in_=alog_s[:], func=AF.Exp),
                    reads=[B(alog_s)], writes=[B(aexp_b)])
                for cb in range(4):
                    S.op("tensor", lambda e, cb=cb: e.matmul(
                        out=PB[5][:, cb:cb + 1],
                        lhsT=ind8_s[:, cb * 128:(cb + 1) * 128], rhs=aexp_b[:],
                        start=(cb == 0), stop=(cb == 3)),
                        reads=[B(ind8_s), B(aexp_b)], writes=[B(PB[5])])
                S.op("scalar", lambda e: e.copy(
                    out=acol_s[:], in_=PB[5][:, 0:4]),
                    reads=[B(PB[5])], writes=[B(acol_s)])
            S.op("vector", lambda e: e.memset(zeros128[:], 0.0),
                 writes=[B(zeros128)])
            for cb in range(4):
                S.op("vector", lambda e, cb=cb: e.memset(S0[cb][:], 0.0),
                     writes=[B(S0[cb])])
                S.op("vector", lambda e, cb=cb: e.memset(S0b[cb][:], 0.0),
                     writes=[B(S0b[cb])])
                S.op("vector", lambda e, cb=cb: e.memset(negb[cb][:], 1e9),
                     writes=[B(negb[cb])])
                S.op("vector", lambda e, cb=cb: e.memset(gplus[cb][:], 1e9),
                     writes=[B(gplus[cb])])
            for p in range(3):
                S.op("vector", lambda e, p=p: e.memset(cbuf[p][:], 0.0),
                     writes=[B(cbuf[p])])
            S.op("vector", lambda e: e.memset(e2[:], 0.0), writes=[B(e2)])

            x_t = x_in.rearrange("(n p) d -> n p d", p=128)
            nbv = [negb[cb][:].rearrange("p (c s) -> p c s", c=NCHUNK)
                   for cb in range(4)]

            def emit_ln(SS, dmaf, xb, src_ap, stat, scr, xout, sem_unused):
                dmaf("sync", xb[:], src_ap, writes=[B(xb)])
                SS.op("vector", lambda e: e.reduce_sum(
                    out=stat[:, 0:1], in_=xb[:], axis=AXX),
                    reads=[B(xb)], writes=[B(stat)])
                SS.op("scalar", lambda e: e.activation(
                    out=scr[:], in_=xb[:], func=AF.Square),
                    reads=[B(xb)], writes=[B(scr)])
                SS.op("vector", lambda e: e.reduce_sum(
                    out=stat[:, 1:2], in_=scr[:], axis=AXX),
                    reads=[B(scr)], writes=[B(stat)])
                SS.op("vector", lambda e: e.tensor_scalar(
                    out=stat[:, 2:3], in0=stat[:, 0:1], scalar1=1.0 / D,
                    scalar2=None, op0=ALU.mult),
                    reads=[B(stat)], writes=[B(stat)])
                SS.op("vector", lambda e: e.tensor_tensor(
                    out=stat[:, 3:4], in0=stat[:, 2:3], in1=stat[:, 2:3],
                    op=ALU.mult), reads=[B(stat)], writes=[B(stat)])
                SS.op("vector", lambda e: e.tensor_scalar(
                    out=stat[:, 4:5], in0=stat[:, 1:2], scalar1=1.0 / D,
                    scalar2=stat[:, 3:4], op0=ALU.mult, op1=ALU.subtract),
                    reads=[B(stat)], writes=[B(stat)])
                SS.op("scalar", lambda e: e.activation(
                    out=stat[:, 5:6], in_=stat[:, 4:5], func=AF.Ln,
                    bias=constc[:, 0:1]),
                    reads=[B(stat), B(constc)], writes=[B(stat)])
                SS.op("scalar", lambda e: e.activation(
                    out=stat[:, 6:7], in_=stat[:, 5:6], func=AF.Exp,
                    scale=constc[:, 1:2]),
                    reads=[B(stat), B(constc)], writes=[B(stat)])
                SS.op("vector", lambda e: e.tensor_scalar(
                    out=xout[:], in0=xb[:], scalar1=stat[:, 2:3],
                    scalar2=stat[:, 6:7], op0=ALU.subtract, op1=ALU.mult),
                    reads=[B(xb), B(stat)], writes=[B(xout)])

            for g in range(0 if skipa else NGl):
                T0 = g * GT
                for i in range(GT // 128):
                    ti = g * (GT // 128) + i
                    xb = xbuf[ti % 2]
                    emit_ln(S, dma, xb, x_t[ti], lnstat, scr_big, xnb, None)
                    for j in range(8):
                        S.op("tensor", lambda e, j=j: e.transpose(
                            out=PBb[7][:, j * 128:(j + 1) * 128],
                            in_=xnb[:, j * 128:(j + 1) * 128],
                            identity=ident_b[:]),
                            reads=[B(xnb), B(ident_b)], writes=[B(PB[7])])
                    S.op("vector", lambda e, i=i: e.tensor_copy(
                        out=xnTg[:].rearrange("p (b t) -> p b t", b=8)
                        [:, :, i * 128:(i + 1) * 128],
                        in_=PBb[7][:].rearrange("p (b t) -> p b t", b=8)),
                        reads=[B(PB[7])], writes=[B(xnTg)])

                for cb in range(4):
                    for pj, wsb in enumerate((wq_s, wk_s, wv_s)):
                        pp = PB[pj]
                        for j in range(8):
                            S.op("tensor", lambda e, j=j, cb=cb, wsb=wsb,
                                 pp=pp: e.matmul(
                                     out=pp[:, 0:GT],
                                     lhsT=wsb[:, j * CH + cb * 128:
                                              j * CH + (cb + 1) * 128],
                                     rhs=xnTg[:, j * GT:(j + 1) * GT],
                                     start=(j == 0), stop=(j == 7)),
                                 reads=[B(wsb), B(xnTg)], writes=[B(pp)])
                    for pj in range(3):
                        pp = PB[pj]
                        dstl = [qs, ks, vs][pj]
                        cbf = cbuf[pj]
                        c0 = cb * (GT + 4)
                        S.op("vector", lambda e, cbf=cbf, c0=c0: e.tensor_copy(
                            out=cbf[:, c0:c0 + 3],
                            in_=cbf[:, c0 + GT:c0 + GT + 3]),
                            reads=[B(cbf)], writes=[B(cbf)])
                        S.op("scalar", lambda e, cbf=cbf, c0=c0, pp=pp: e.copy(
                            out=cbf[:, c0 + 3:c0 + 3 + GT], in_=pp[:, 0:GT]),
                            reads=[B(pp)], writes=[B(cbf)])
                        wc0 = (pj * 4 + cb) * 4
                        S.op("vector", lambda e, cbf=cbf, c0=c0, wc0=wc0:
                             e.tensor_scalar(
                                 out=scr_big[:, 0:GT], in0=cbf[:, c0:c0 + GT],
                                 scalar1=cvw_s[:, wc0:wc0 + 1], scalar2=None,
                                 op0=ALU.mult),
                             reads=[B(cbf), B(cvw_s)], writes=[B(scr_big)])
                        for tp in range(1, 4):
                            S.op("vector", lambda e, cbf=cbf, c0=c0, wc0=wc0,
                                 tp=tp: e.scalar_tensor_tensor(
                                     out=scr_big[:, 0:GT],
                                     in0=cbf[:, c0 + tp:c0 + tp + GT],
                                     scalar=cvw_s[:, wc0 + tp:wc0 + tp + 1],
                                     in1=scr_big[:, 0:GT], op0=ALU.mult,
                                     op1=ALU.add),
                                 reads=[B(cbf), B(cvw_s), B(scr_big)],
                                 writes=[B(scr_big)])
                        bc = pj * 4 + cb
                        S.op("scalar", lambda e, bc=bc, dst=dstl[cb]:
                             e.activation(
                                 out=dst[:], in_=scr_big[:, 0:GT],
                                 func=AF.Silu, bias=cvb_s[:, bc:bc + 1]),
                             reads=[B(scr_big), B(cvb_s)], writes=[B(dstl[cb])])

                for j in range(8):
                    S.op("tensor", lambda e, j=j: e.matmul(
                        out=PB[3][0:NH, 0:GT],
                        lhsT=wb_s[:, j * NH:(j + 1) * NH],
                        rhs=xnTg[:, j * GT:(j + 1) * GT],
                        start=(j == 0), stop=(j == 7)),
                        reads=[B(wb_s), B(xnTg)], writes=[B(PB[3])])
                S.op("scalar", lambda e: e.activation(
                    out=rq_s[:], in_=PB[3][0:NH, 0:GT], func=AF.Sigmoid),
                    reads=[B(PB[3])], writes=[B(rq_s)])
                for c in range(NCHUNK):
                    S.op("tensor", lambda e, c=c: e.transpose(
                        out=PBb[5][0:C, c * NH:(c + 1) * NH],
                        in_=rq_s[0:NH, c * 128:(c + 1) * 128],
                        identity=ident_b[0:NH, 0:NH]),
                        reads=[B(rq_s), B(ident_b)], writes=[B(PB[5])])
                S.op("scalar", lambda e: e.copy(
                    out=bcols[:], in_=PBb[5][:, 0:NCHUNK * NH]),
                    reads=[B(PB[5])], writes=[B(bcols)])

                for src, rmap in ((qs, rq_s), (ks, rk_s)):
                    for cb in range(4):
                        S.op("scalar", lambda e, src=src, cb=cb: e.activation(
                            out=sqscr[:, 0:GT], in_=src[cb][:], func=AF.Square),
                            reads=[B(src[cb])], writes=[B(sqscr)])
                        S.op("tensor", lambda e, cb=cb: e.matmul(
                            out=PB[3][0:NH, 0:GT],
                            lhsT=ind8T_s[:, cb * NH:(cb + 1) * NH],
                            rhs=sqscr[:, 0:GT], start=(cb == 0), stop=(cb == 3)),
                            reads=[B(ind8T_s), B(sqscr)], writes=[B(PB[3])])
                    S.op("scalar", lambda e: e.activation(
                        out=spscr[0][0:NH, 0:GT], in_=PB[3][0:NH, 0:GT],
                        func=AF.Ln), reads=[B(PB[3])], writes=[B(spscr[0])])
                    S.op("scalar", lambda e: e.activation(
                        out=spscr[0][0:NH, 0:GT], in_=spscr[0][0:NH, 0:GT],
                        func=AF.Exp, scale=constc[0:NH, 2:3]),
                        reads=[B(spscr[0]), B(constc)], writes=[B(spscr[0])])
                    S.op("vector", lambda e: e.tensor_scalar(
                        out=spscr[0][0:NH, 0:GT], in0=spscr[0][0:NH, 0:GT],
                        scalar1=1e-6, scalar2=None, op0=ALU.add),
                        reads=[B(spscr[0])], writes=[B(spscr[0])])
                    S.op("vector", lambda e: e.reciprocal(
                        out=spscr[1][0:NH, 0:GT], in_=spscr[0][0:NH, 0:GT]),
                        reads=[B(spscr[0])], writes=[B(spscr[1])])
                    S.op("vector", lambda e, rmap=rmap: e.tensor_copy(
                        out=rmap[:], in_=spscr[1][0:NH, 0:GT]),
                        reads=[B(spscr[1])], writes=[B(rmap)])
                for cb in range(4):
                    S.op("tensor", lambda e, cb=cb: e.matmul(
                        out=PB[4][:, 0:GT],
                        lhsT=ind8_s[:, cb * 128:(cb + 1) * 128], rhs=rk_s[:],
                        start=True, stop=True),
                        reads=[B(ind8_s), B(rk_s)], writes=[B(PB[4])])
                    S.op("vector", lambda e, cb=cb: e.tensor_tensor(
                        out=kn[cb][:], in0=ks[cb][:], in1=PB[4][:, 0:GT],
                        op=ALU.mult),
                        reads=[B(ks[cb]), B(PB[4])], writes=[B(kn[cb])])
                    S.op("tensor", lambda e, cb=cb: e.matmul(
                        out=PB[4][:, 0:GT],
                        lhsT=ind8_s[:, cb * 128:(cb + 1) * 128], rhs=rq_s[:],
                        start=True, stop=True),
                        reads=[B(ind8_s), B(rq_s)], writes=[B(PB[4])])
                    S.op("vector", lambda e, cb=cb: e.tensor_tensor(
                        out=qn[cb][:], in0=qs[cb][:], in1=PB[4][:, 0:GT],
                        op=ALU.mult),
                        reads=[B(qs[cb]), B(PB[4])], writes=[B(qn[cb])])

                for w1sb, w2sb, is_gate in ((wf1_s, wf2_s, False),
                                            (wg1_s, wg2_s, True)):
                    for j in range(8):
                        S.op("tensor", lambda e, j=j, w1sb=w1sb: e.matmul(
                            out=PB[4][0:64, 0:GT],
                            lhsT=w1sb[:, j * 64:(j + 1) * 64],
                            rhs=xnTg[:, j * GT:(j + 1) * GT],
                            start=(j == 0), stop=(j == 7)),
                            reads=[B(w1sb), B(xnTg)], writes=[B(PB[4])])
                    S.op("scalar", lambda e: e.copy(
                        out=g1s[:], in_=PB[4][0:64, 0:GT]),
                        reads=[B(PB[4])], writes=[B(g1s)])
                    for cb in range(4):
                        S.op("tensor", lambda e, cb=cb, w2sb=w2sb: e.matmul(
                            out=PB[5][:, 0:GT],
                            lhsT=w2sb[:, cb * 128:(cb + 1) * 128], rhs=g1s[:],
                            start=True, stop=True),
                            reads=[B(w2sb), B(g1s)], writes=[B(PB[5])])
                        if is_gate:
                            S.op("scalar", lambda e, cb=cb: e.activation(
                                out=sgate[cb][:], in_=PB[5][:, 0:GT],
                                func=AF.Sigmoid, bias=bg2_s[:, cb:cb + 1]),
                                reads=[B(PB[5]), B(bg2_s)],
                                writes=[B(sgate[cb])])
                        else:
                            S.op("scalar", lambda e, cb=cb: e.activation(
                                out=spscr[0][:], in_=PB[5][:, 0:GT],
                                func=AF.Abs, bias=dtb_s[:, cb:cb + 1]),
                                reads=[B(PB[5]), B(dtb_s)],
                                writes=[B(spscr[0])])
                            S.op("scalar", lambda e: e.activation(
                                out=spscr[0][:], in_=spscr[0][:], func=AF.Exp,
                                scale=-1.0),
                                reads=[B(spscr[0])], writes=[B(spscr[0])])
                            S.op("scalar", lambda e: e.activation(
                                out=spscr[0][:], in_=spscr[0][:], func=AF.Ln,
                                bias=1.0),
                                reads=[B(spscr[0])], writes=[B(spscr[0])])
                            S.op("scalar", lambda e, cb=cb: e.activation(
                                out=spscr[1][:], in_=PB[5][:, 0:GT],
                                func=AF.Relu, bias=dtb_s[:, cb:cb + 1]),
                                reads=[B(PB[5]), B(dtb_s)],
                                writes=[B(spscr[1])])
                            S.op("vector", lambda e: e.tensor_tensor(
                                out=spscr[0][:], in0=spscr[0][:],
                                in1=spscr[1][:], op=ALU.add),
                                reads=[B(spscr[0]), B(spscr[1])],
                                writes=[B(spscr[0])])
                            S.op("vector", lambda e, cb=cb: e.tensor_scalar(
                                out=gplus[cb][:, 0:GT], in0=spscr[0][:],
                                scalar1=acol_s[:, cb:cb + 1], scalar2=None,
                                op0=ALU.mult),
                                reads=[B(spscr[0]), B(acol_s)],
                                writes=[B(gplus[cb])])

                for cb in range(4):
                    for c in range(NCHUNK):
                        S.op("vector", lambda e, cb=cb, c=c:
                             e.tensor_tensor_scan(
                                 out=negb[cb][:, c * PAD:c * PAD + 128],
                                 data0=gplus[cb][:, c * 128:(c + 1) * 128],
                                 data1=zeros128[:], initial=0.0, op0=ALU.add,
                                 op1=ALU.add),
                             reads=[B(gplus[cb]), B(zeros128)],
                             writes=[B(negb[cb])])
                    S.op("scalar", lambda e, cb=cb: e.activation(
                        out=lamf[cb][:].rearrange("p (c s) -> p c s", c=NCHUNK),
                        in_=nbv[cb][:, :, 0:128], func=AF.Exp, scale=-1.0),
                        reads=[B(negb[cb])], writes=[B(lamf[cb])])
                    S.op("vector", lambda e, cb=cb: e.tensor_tensor(
                        out=kp[cb][:], in0=kn[cb][:], in1=lamf[cb][:],
                        op=ALU.mult),
                        reads=[B(kn[cb]), B(lamf[cb])], writes=[B(kp[cb])])
                    S.op("vector", lambda e, cb=cb: e.scalar_tensor_tensor(
                        out=qp[cb][:], in0=qn[cb][:], scalar=0.125,
                        in1=lamf[cb][:], op0=ALU.mult, op1=ALU.mult),
                        reads=[B(qn[cb]), B(lamf[cb])], writes=[B(qp[cb])])

                    def facpair(h_, cb=cb):
                        na = 128 // h_
                        S.op("vector", lambda e, cb=cb, h_=h_, na=na:
                             e.tensor_tensor(
                                 out=facscr[:].rearrange(
                                     "p (c a b) -> p c a b", c=NCHUNK, b=h_),
                                 in0=nbv[cb][:, :, 0:128:h_].unsqueeze(3)
                                 .broadcast_to((128, NCHUNK, na, h_)),
                                 in1=nbv[cb][:, :, 0:128].rearrange(
                                     "p c (a b) -> p c a b", b=h_),
                                 op=ALU.subtract),
                             reads=[B(negb[cb])], writes=[B(facscr)])
                        S.op("scalar", lambda e: e.activation(
                            out=rowF[:], in_=facscr[:], func=AF.Exp),
                            reads=[B(facscr)], writes=[B(rowF)])
                        S.op("vector", lambda e, cb=cb, h_=h_, na=na:
                             e.tensor_tensor(
                                 out=facscr[:].rearrange(
                                     "p (c a b) -> p c a b", c=NCHUNK, b=h_),
                                 in0=nbv[cb][:, :, 0:128].rearrange(
                                     "p c (a b) -> p c a b", b=h_),
                                 in1=nbv[cb][:, :, h_:129:h_].unsqueeze(3)
                                 .broadcast_to((128, NCHUNK, na, h_)),
                                 op=ALU.subtract),
                             reads=[B(negb[cb])], writes=[B(facscr)])
                        S.op("scalar", lambda e: e.activation(
                            out=colF[:], in_=facscr[:], func=AF.Exp),
                            reads=[B(facscr)], writes=[B(colF)])

                    def rowmuls(rn_, ra_, cb=cb):
                        S.op("vector", lambda e, cb=cb, rn_=rn_:
                             e.tensor_tensor(out=rn_[:], in0=kn[cb][:],
                                             in1=rowF[:], op=ALU.mult),
                             reads=[B(kn[cb]), B(rowF)], writes=[B(rn_)])
                        S.op("vector", lambda e, cb=cb, ra_=ra_:
                             e.scalar_tensor_tensor(
                                 out=ra_[:], in0=qn[cb][:], scalar=0.125,
                                 in1=rowF[:], op0=ALU.mult, op1=ALU.mult),
                             reads=[B(qn[cb]), B(rowF)], writes=[B(ra_)])

                    def colmul(ck_, cb=cb):
                        S.op("vector", lambda e, cb=cb, ck_=ck_:
                             e.tensor_tensor(out=ck_[:], in0=kn[cb][:],
                                             in1=colF[:], op=ALU.mult),
                             reads=[B(kn[cb]), B(colF)], writes=[B(ck_)])

                    for li, hl in enumerate((16, 32, 64)):
                        facpair(hl)
                        rowmuls(rnR[li][cb], raR[li][cb])
                        colmul(ckR[li][cb])
                    for li, hl in enumerate((2, 4, 8)):
                        facpair(hl)
                        rowmuls(rnL[li][cb], raL[li][cb])
                        colmul(ckL[li][cb])
                    S.op("scalar", lambda e, cb=cb: e.activation(
                        out=e2[:, 0:GT], in_=gplus[cb][:, 1:GT + 1],
                        func=AF.Exp, scale=-1.0),
                        reads=[B(gplus[cb])], writes=[B(e2)])
                    S.op("vector", lambda e, cb=cb: e.tensor_tensor(
                        out=ck2[cb][:], in0=kn[cb][:], in1=e2[:, 0:GT],
                        op=ALU.mult),
                        reads=[B(kn[cb]), B(e2)], writes=[B(ck2[cb])])
                    for c in range(NCHUNK):
                        S.op("vector", lambda e, cb=cb, c=c: e.tensor_scalar(
                            out=negcol[:, cb:cb + 1],
                            in0=negb[cb][:, c * PAD + 127:c * PAD + 128],
                            scalar1=-1.0, scalar2=None, op0=ALU.mult),
                            reads=[B(negb[cb])], writes=[B(negcol)])
                        S.op("scalar", lambda e, cb=cb, c=c: e.activation(
                            out=facscr[:, c * 128:(c + 1) * 128],
                            in_=negb[cb][:, c * PAD:c * PAD + 128],
                            func=AF.Exp, bias=negcol[:, cb:cb + 1]),
                            reads=[B(negb[cb]), B(negcol)], writes=[B(facscr)])
                    S.op("vector", lambda e, cb=cb: e.tensor_tensor(
                        out=kd[cb][:], in0=kn[cb][:], in1=facscr[:],
                        op=ALU.mult),
                        reads=[B(kn[cb]), B(facscr)], writes=[B(kd[cb])])

                for c in range(NCHUNK):
                    for cb in range(4):
                        S.op("vector", lambda e, cb=cb, c=c: e.tensor_tensor(
                            out=qkscr[:], in0=qn[cb][:, c * 128:(c + 1) * 128],
                            in1=kn[cb][:, c * 128:(c + 1) * 128], op=ALU.mult),
                            reads=[B(qn[cb]), B(kn[cb])], writes=[B(qkscr)])
                        S.op("tensor", lambda e, cb=cb, c=c: e.matmul(
                            out=PB[5][:, 480 + c * NH:480 + (c + 1) * NH],
                            lhsT=qkscr[:],
                            rhs=ind8T_s[:, cb * NH:(cb + 1) * NH],
                            start=(cb == 0 and c == 0),
                            stop=(cb == 3 and c == NCHUNK - 1)),
                            reads=[B(qkscr), B(ind8T_s)], writes=[B(PB[5])])
                S.op("scalar", lambda e: e.mul(
                    out=aqd_s[:], in_=PB[5][:, 480:480 + NCHUNK * NH],
                    mul=0.125),
                    reads=[B(PB[5])], writes=[B(aqd_s)])

                for c in range(NCHUNK):
                    tc = slice(c * 128, (c + 1) * 128)
                    for cb in range(4):
                        S.op("scalar", lambda e, cb=cb, c=c: e.activation(
                            out=lamC[:, cb:cb + 1],
                            in_=negb[cb][:, c * PAD + 127:c * PAD + 128],
                            func=AF.Exp, scale=-1.0),
                            reads=[B(negb[cb])], writes=[B(lamC)])
                    S.op("vector", lambda e, c=c: e.tensor_copy(
                        out=bmap[:].rearrange("p (h v) -> p h v", h=NH),
                        in_=bcols[:, c * NH:(c + 1) * NH].unsqueeze(2)
                        .broadcast_to((128, NH, HK))),
                        reads=[B(bcols)], writes=[B(bmap)])

                    RECTS = ((0, [(0, 16), (32, 48), (64, 80), (96, 112)]),
                             (1, [(0, 32), (64, 96)]),
                             (2, [(0, 64)]))
                    for mat in range(2):
                        rR_ = rnR if mat == 0 else raR
                        rL_ = rnL if mat == 0 else raL
                        dst = AT0 if mat == 0 else AQT
                        for grp in range(2):
                            S.op("vector", lambda e: e.memset(PB[0][:], 0.0),
                                 writes=[B(PB[0])])
                            for hh in range(4):
                                h = grp * 4 + hh
                                cb, par = h // 2, h % 2
                                kb = par * 64
                                for li, rects in RECTS:
                                    hw_ = 16 << li
                                    for s0, t0 in rects:
                                        S.op("tensor", lambda e, hh=hh, li=li,
                                             cb=cb, kb=kb, c=c, rR_=rR_,
                                             s0=s0, t0=t0, hw_=hw_: e.matmul(
                                                 out=PB[0][s0:s0 + hw_,
                                                           hh * 128 + t0:
                                                           hh * 128 + t0 + hw_],
                                                 lhsT=ckR[li][cb][kb:kb + 64,
                                                                  c * 128 + s0:
                                                                  c * 128 + s0 + hw_],
                                                 rhs=rR_[li][cb][kb:kb + 64,
                                                                 c * 128 + t0:
                                                                 c * 128 + t0 + hw_],
                                                 start=True, stop=True,
                                                 tile_position=(kb, s0 - s0 % 32)),
                                             reads=[B(ckR[li][cb]), B(rR_[li][cb])],
                                             writes=[B(PB[0])])
                                br = kn if mat == 0 else qn
                                S.op("tensor", lambda e, hh=hh, cb=cb, kb=kb,
                                     c=c, br=br: e.matmul(
                                         out=PB[1][:, hh * 128:(hh + 1) * 128],
                                         lhsT=ck2[cb][kb:kb + 64,
                                                      c * 128:(c + 1) * 128],
                                         rhs=br[cb][kb:kb + 64,
                                                    c * 128:(c + 1) * 128],
                                         start=True, stop=True,
                                         tile_position=(kb, 0)),
                                     reads=[B(ck2[cb]), B(br[cb])],
                                     writes=[B(PB[1])])
                                for li in range(3):
                                    pbk = PB[2 + li]
                                    S.op("tensor", lambda e, hh=hh, cb=cb,
                                         kb=kb, c=c, li=li, pbk=pbk, rL_=rL_:
                                         e.matmul(
                                             out=pbk[:, hh * 128:(hh + 1) * 128],
                                             lhsT=ckL[li][cb][kb:kb + 64,
                                                             c * 128:
                                                             (c + 1) * 128],
                                             rhs=rL_[li][cb][kb:kb + 64,
                                                             c * 128:
                                                             (c + 1) * 128],
                                             start=True, stop=True,
                                             tile_position=(kb, 0)),
                                         reads=[B(ckL[li][cb]),
                                                B(rL_[li][cb])],
                                         writes=[B(pbk)])
                            S.op("scalar", lambda e: e.copy(
                                out=farsb[:], in_=PB[0][:]),
                                reads=[B(PB[0])], writes=[B(farsb)])
                            mview = lambda li: masks_s[:, li * 512:
                                                       (li + 1) * 512]
                            S.op("vector", lambda e, mview=mview: e.select(
                                out=selscr[:], mask=mview(3),
                                on_true=PB[4][:], on_false=farsb[:]),
                                reads=[B(PB[4]), B(farsb), B(masks_s)],
                                writes=[B(selscr)])
                            S.op("vector", lambda e, mview=mview: e.select(
                                out=selscr[:], mask=mview(2),
                                on_true=PB[3][:], on_false=selscr[:]),
                                reads=[B(PB[3]), B(selscr), B(masks_s)],
                                writes=[B(selscr)])
                            S.op("vector", lambda e, mview=mview: e.select(
                                out=selscr[:], mask=mview(1),
                                on_true=PB[2][:], on_false=selscr[:]),
                                reads=[B(PB[2]), B(selscr), B(masks_s)],
                                writes=[B(selscr)])
                            S.op("vector", lambda e, dst=dst, grp=grp,
                                 mview=mview: e.select(
                                     out=dst[:, grp * 512:(grp + 1) * 512],
                                     mask=mview(0), on_true=PB[1][:],
                                     on_false=selscr[:]),
                                 reads=[B(PB[1]), B(selscr), B(masks_s)],
                                 writes=[B(dst)])

                    for h in range(NH):
                        S.op("tensor", lambda e, h=h: e.transpose(
                            out=PBb[6][:, (h % 4) * 128:(h % 4 + 1) * 128],
                            in_=AT0[:, h * 128:(h + 1) * 128],
                            identity=ident_b[:]),
                            reads=[B(AT0), B(ident_b)], writes=[B(PB[6])])
                        if h == 3 or h == 7:
                            off = (h // 4) * 512
                            S.op("vector", lambda e, off=off: e.tensor_copy(
                                out=A0[:, off:off + 512],
                                in_=PBb[6][:, 0:512]),
                                reads=[B(PB[6])], writes=[B(A0)])

                    prevA, prevAT = A0, AT0
                    for k_ in range(1, 7):
                        curA = Afrm[k_ % 2]
                        curAT = ATk[k_ - 1]
                        for h in range(NH):
                            S.op("vector", lambda e, h=h, prevA=prevA, c=c:
                                 e.tensor_scalar(
                                     out=Bfrm[:, h * 128:(h + 1) * 128],
                                     in0=prevA[:, h * 128:(h + 1) * 128],
                                     scalar1=bcols[:, c * NH + h:
                                                   c * NH + h + 1],
                                     scalar2=None, op0=ALU.mult),
                                 reads=[B(prevA), B(bcols)], writes=[B(Bfrm)])
                        for h in range(NH):
                            pa = PB[0] if h < 4 else PB[1]
                            pt_ = PB[2] if h < 4 else PB[3]
                            hh = h % 4
                            S.op("tensor", lambda e, h=h, hh=hh, pa=pa,
                                 prevAT=prevAT: e.matmul(
                                     out=pa[:, hh * 128:(hh + 1) * 128],
                                     lhsT=prevAT[:, h * 128:(h + 1) * 128],
                                     rhs=Bfrm[:, h * 128:(h + 1) * 128],
                                     start=True, stop=True),
                                 reads=[B(prevAT), B(Bfrm)], writes=[B(pa)])
                            S.op("tensor", lambda e, h=h, hh=hh, pt_=pt_,
                                 prevAT=prevAT: e.matmul(
                                     out=pt_[:, hh * 128:(hh + 1) * 128],
                                     lhsT=Bfrm[:, h * 128:(h + 1) * 128],
                                     rhs=prevAT[:, h * 128:(h + 1) * 128],
                                     start=True, stop=True),
                                 reads=[B(prevAT), B(Bfrm)], writes=[B(pt_)])
                        for pi, dst_, off in ((0, curA, 0), (1, curA, 512),
                                              (2, curAT, 0), (3, curAT, 512)):
                            S.op("vector", lambda e, pi=pi, dst_=dst_, off=off:
                                 e.tensor_copy(
                                     out=dst_[:, off:off + 512],
                                     in_=PB[pi][:]),
                                 reads=[B(PB[pi])], writes=[B(dst_)])
                        prevA, prevAT = curA, curAT

                    for cb in range(4):
                        S.op("tensor", lambda e, cb=cb, c=c: e.matmul(
                            out=PB[5][:, cb * 128:(cb + 1) * 128],
                            lhsT=kp[cb][:, c * 128:(c + 1) * 128],
                            rhs=S0b[cb][:], start=(cb == 0), stop=(cb == 3)),
                            reads=[B(kp[cb]), B(S0b[cb])], writes=[B(PB[5])])
                        S.op("tensor", lambda e, cb=cb, c=c: e.transpose(
                            out=PBb[6][:, cb * 128:(cb + 1) * 128],
                            in_=vs[cb][:, c * 128:(c + 1) * 128],
                            identity=ident_b[:]),
                            reads=[B(vs[cb]), B(ident_b)], writes=[B(PB[6])])
                    S.op("vector", lambda e: e.tensor_copy(
                        out=vT_s[:], in_=PBb[6][:, 0:512]),
                        reads=[B(PB[6])], writes=[B(vT_s)])
                    S.op("vector", lambda e: e.tensor_tensor(
                        out=Xscr[:], in0=vT_s[:], in1=PB[5][:, 0:512],
                        op=ALU.subtract),
                        reads=[B(vT_s), B(PB[5])], writes=[B(Xscr)])
                    S.op("vector", lambda e: e.tensor_tensor(
                        out=Rb[:], in0=Xscr[:], in1=bmap[:], op=ALU.mult),
                        reads=[B(Xscr), B(bmap)], writes=[B(Rb)])

                    Xcur = Rb
                    seq = [ATk[5], ATk[4], ATk[3], ATk[2], ATk[1], ATk[0], AT0]
                    for ridx, frm in enumerate(seq):
                        minus = ridx == len(seq) - 1
                        Xnext = Ub if minus else Xb[ridx % 2]
                        for h in range(NH):
                            S.op("tensor", lambda e, h=h, frm=frm, Xcur=Xcur:
                                 e.matmul(
                                     out=PB[4][:, h * HK:(h + 1) * HK],
                                     lhsT=frm[:, h * 128:(h + 1) * 128],
                                     rhs=Xcur[:, h * HK:(h + 1) * HK],
                                     start=(h == 0), stop=(h == 7)),
                                 reads=[B(frm), B(Xcur)], writes=[B(PB[4])])
                        S.op("vector", lambda e: e.tensor_tensor(
                            out=Xscr[:], in0=PB[4][:, 0:512], in1=bmap[:],
                            op=ALU.mult),
                            reads=[B(PB[4]), B(bmap)], writes=[B(Xscr)])
                        if minus:
                            S.op("vector", lambda e, Xcur=Xcur, Xnext=Xnext:
                                 e.tensor_tensor(
                                     out=Xnext[:], in0=Xcur[:], in1=Xscr[:],
                                     op=ALU.subtract),
                                 reads=[B(Xcur), B(Xscr)], writes=[B(Xnext)])
                        else:
                            S.op("vector", lambda e, Xcur=Xcur, Xnext=Xnext:
                                 e.tensor_tensor(
                                     out=Xnext[:], in0=Xscr[:], in1=Xcur[:],
                                     op=ALU.add),
                                 reads=[B(Xscr), B(Xcur)], writes=[B(Xnext)])
                        Xcur = Xnext

                    for cb in range(4):
                        S.op("tensor", lambda e, cb=cb, c=c: e.matmul(
                            out=PB[5][:, cb * 128:(cb + 1) * 128],
                            lhsT=qp[cb][:, c * 128:(c + 1) * 128],
                            rhs=S0b[cb][:], start=(cb == 0), stop=False),
                            reads=[B(qp[cb]), B(S0b[cb])], writes=[B(PB[5])])
                    for h in range(NH):
                        S.op("tensor", lambda e, h=h: e.matmul(
                            out=PB[5][:, h * HK:(h + 1) * HK],
                            lhsT=AQT[:, h * 128:(h + 1) * 128],
                            rhs=Ub[:, h * HK:(h + 1) * HK],
                            start=False, stop=(h == 7)),
                            reads=[B(AQT), B(Ub)], writes=[B(PB[5])])
                    S.op("scalar", lambda e: e.activation(
                        out=sqscr[:, 0:512], in_=PB[5][:, 0:512],
                        func=AF.Square),
                        reads=[B(PB[5])], writes=[B(sqscr)])
                    S.op("vector", lambda e: e.tensor_reduce(
                        out=rstd_s[:].unsqueeze(2),
                        in_=sqscr[:, 0:512].rearrange("p (h v) -> p h v",
                                                      h=NH),
                        axis=AXX, op=ALU.add),
                        reads=[B(sqscr)], writes=[B(rstd_s)])
                    S.op("scalar", lambda e: e.activation(
                        out=rstd_s[:], in_=rstd_s[:], func=AF.Ln,
                        scale=constc[:, 3:4], bias=constc[:, 0:1]),
                        reads=[B(rstd_s), B(constc)], writes=[B(rstd_s)])
                    S.op("scalar", lambda e: e.activation(
                        out=rstd_s[:], in_=rstd_s[:], func=AF.Exp,
                        scale=constc[:, 1:2]),
                        reads=[B(rstd_s), B(constc)], writes=[B(rstd_s)])
                    for h in range(NH):
                        S.op("vector", lambda e, h=h, c=c:
                             e.scalar_tensor_tensor(
                                 out=omid[:, h * HK:(h + 1) * HK],
                                 in0=Ub[:, h * HK:(h + 1) * HK],
                                 scalar=aqd_s[:, c * NH + h:c * NH + h + 1],
                                 in1=PB[5][:, h * HK:(h + 1) * HK],
                                 op0=ALU.mult, op1=ALU.add),
                             reads=[B(Ub), B(aqd_s), B(PB[5])],
                             writes=[B(omid)])
                        S.op("vector", lambda e, h=h: e.tensor_scalar(
                            out=omid[:, h * HK:(h + 1) * HK],
                            in0=omid[:, h * HK:(h + 1) * HK],
                            scalar1=rstd_s[:, h:h + 1], scalar2=None,
                            op0=ALU.mult),
                            reads=[B(omid), B(rstd_s)], writes=[B(omid)])
                    for cb in range(4):
                        S.op("tensor", lambda e, cb=cb: e.transpose(
                            out=PBb[6][:, cb * 128:(cb + 1) * 128],
                            in_=omid[:, cb * 128:(cb + 1) * 128],
                            identity=ident_b[:]),
                            reads=[B(omid), B(ident_b)], writes=[B(PB[6])])
                        S.op("vector", lambda e, cb=cb, c=c: e.tensor_tensor(
                            out=ofinT[:].rearrange("p (b t) -> p b t", b=4)
                            [:, cb, c * 128:(c + 1) * 128],
                            in0=PBb[6][:, cb * 128:(cb + 1) * 128],
                            in1=sgate[cb][:, c * 128:(c + 1) * 128],
                            op=ALU.mult),
                            reads=[B(PB[6]), B(sgate[cb])], writes=[B(ofinT)])

                    for cb in range(4):
                        S.op("tensor", lambda e, cb=cb, c=c: e.transpose(
                            out=PBb[6][:, cb * 128:(cb + 1) * 128],
                            in_=kd[cb][:, c * 128:(c + 1) * 128],
                            identity=ident_b[:]),
                            reads=[B(kd[cb]), B(ident_b)], writes=[B(PB[6])])
                    S.op("vector", lambda e: e.tensor_copy(
                        out=kdT_s[:], in_=PBb[6][:, 0:512]),
                        reads=[B(PB[6])], writes=[B(kdT_s)])
                    for h in range(NH):
                        cb, par = h // 2, h % 2
                        S.op("tensor", lambda e, h=h, cb=cb, par=par: e.matmul(
                            out=PB[7][par * 64:par * 64 + 64,
                                      cb * 128 + par * 64:
                                      cb * 128 + par * 64 + 64],
                            lhsT=kdT_s[:, cb * 128 + par * 64:
                                       cb * 128 + par * 64 + 64],
                            rhs=Ub[:, h * HK:(h + 1) * HK],
                            start=True, stop=True,
                            tile_position=(0, par * 64)),
                            reads=[B(kdT_s), B(Ub)], writes=[B(PB[7])])
                    for cb in range(4):
                        for par in range(2):
                            o_ = par * 64
                            S.op("vector", lambda e, cb=cb, o_=o_:
                                 e.scalar_tensor_tensor(
                                     out=S0[cb][o_:o_ + 64, o_:o_ + 64],
                                     in0=S0[cb][o_:o_ + 64, o_:o_ + 64],
                                     scalar=lamC[o_:o_ + 64, cb:cb + 1],
                                     in1=PB[7][o_:o_ + 64,
                                               cb * 128 + o_:
                                               cb * 128 + o_ + 64],
                                     op0=ALU.mult, op1=ALU.add),
                                 reads=[B(lamC), B(PB[7]), B(S0[cb])],
                                 writes=[B(S0[cb])])
                    for cb in range(4):
                        S.op("vector", lambda e, cb=cb: e.tensor_copy(
                            out=S0b[cb][:], in_=S0[cb][:]),
                            reads=[B(S0[cb])], writes=[B(S0b[cb])])

                for cb in range(4):
                    dma("sync",
                        cc_in.rearrange("(b p) t -> b p t", p=128)
                        [cb, :, T0:T0 + GT],
                        ofinT[:].rearrange("p (b t) -> p b t", b=4)[:, cb, :],
                        reads=[B(ofinT)], writes=[ccin_res])

            if no_cc:
                dma("gpsimd", cc_out[0:CH, :], cc_in, reads=[ccin_res],
                    writes=[])
            else:
                S.op("gpsimd", lambda e: e.collective_compute(
                    "AllGather", ALU.bypass, ins=[cc_in_f.opt()],
                    outs=[cc_out_f.opt()],
                    replica_groups=rgroups),
                    reads=[ccin_res], writes=[], kind="cc")

            with nc.Block("phaseA") as blockA:
                S.emit(blockA, sems)
            carry = S.final
            for sl in range(Sched.NPOOL):
                carry[f"d{sl}"] = S.dmagen[sl]

        for r_ in R.values():
            r_.w = None
            r_.rs = []
        ccin_res.w = None
        ccin_res.rs = []

        S2 = Sched(carry=carry, dmagen=S.dmagen)
        dma2 = mkdma(S2)

        bctx = contextlib.ExitStack()
        sbb = lambda n, s_, d_: sb(n, s_, d_, bctx)
        with bctx:
            wgT_s = sbb("wgT_s", (128, 8 * FFH), BF16)
            wuT_s = sbb("wuT_s", (128, 8 * FFH), BF16)
            wdT_s = sbb("wdT_s", (128, 24 * D), BF16)
            wo_s = sbb("wo_s", (128, 8 * D), BF16)
            onw_s = sbb("onw_s", (128, 1), F32)
            istage = [sbb(f"istage{i}", (128, FFH // 2), I32) for i in range(2)]
            fstage = sbb("fstage", (128, D), F32)
            xb2 = [sbb(f"xb2_{i}", (128, D), F32) for i in range(2)]
            ln2 = sbb("ln2", (128, 16), F32)
            scr2 = sbb("scr2", (128, D), F32)
            xnb2 = sbb("xnb2", (128, D), BF16)
            xnT2 = sbb("xnT2", (128, 8 * SUB), BF16)
            ofin2 = sbb("ofin2", (128, 8 * SUB), BF16)
            hbuf = sbb("hbuf", (128, 2 * SUB), BF16)
            sil2 = sbb("sil2", (128, SUB), F32)
            outsb = sbb("outsb", (128, D), F32)

            dma2("sync", onw_s[:], onw_in, writes=[B(onw_s)])
            for dblk in range(8):
                dma2("sync", fstage[:], wo_in[dblk * 128:(dblk + 1) * 128, :],
                     reads=[B(wo_s)], writes=[B(fstage)])
                S2.op("vector", lambda e, dblk=dblk: e.tensor_scalar(
                    out=wo_s[:, dblk * D:(dblk + 1) * D], in0=fstage[:],
                    scalar1=onw_s[:], scalar2=None, op0=ALU.mult),
                    reads=[B(fstage), B(onw_s)], writes=[B(wo_s)])
            HF = FFH // 2
            for win, wsb2 in ((wgT_in, wgT_s), (wuT_in, wuT_s)):
                for dblk in range(8):
                    for hf in range(2):
                        st_ = istage[(dblk * 2 + hf) % 2]
                        dma2("sync", st_[:],
                             win[dblk * 128:(dblk + 1) * 128,
                                 hf * HF:(hf + 1) * HF],
                             reads=[B(wsb2)], writes=[B(st_)])
                        S2.op("vector", lambda e, dblk=dblk, hf=hf, wsb2=wsb2,
                              st_=st_: e.tensor_copy(
                                  out=wsb2[:, dblk * FFH + hf * HF:
                                           dblk * FFH + (hf + 1) * HF],
                                  in_=st_[:]),
                              reads=[B(st_)], writes=[B(wsb2)])
            for hb in range(24):
                st_ = istage[hb % 2]
                dma2("sync", st_[:, 0:D], wdT_in[hb * 128:(hb + 1) * 128, :],
                     reads=[B(wdT_s)], writes=[B(st_)])
                S2.op("vector", lambda e, hb=hb, st_=st_: e.tensor_copy(
                    out=wdT_s[:, hb * D:(hb + 1) * D], in_=st_[:, 0:D]),
                    reads=[B(st_)], writes=[B(wdT_s)])

            regres = Res()

            def loadreg(e):
                reg = e.alloc_register("hgreg")
                loadreg.reg = reg
                return e.reg_load(reg, soff[0:1, 0:1])
            beng = "scalar" if simdma else "gpsimd"
            S2.op(beng, loadreg, reads=[B(soff)], writes=[regres])

            xh_t = xh_in.rearrange("(n p) d -> n p d", p=128)
            out_tt = out_t.rearrange("(n p) d -> n p d", p=128)
            ccov = cc_out.rearrange("(b p) t -> b p t", p=128)

            for s_ in range(0 if skipb else NSUBl):
                s0 = s_ * SUB
                for i in range(2):
                    xb = xb2[i]
                    emit_ln(S2, dma2, xb, xh_t[s_ * 2 + i], ln2, scr2, xnb2,
                            None)
                    for j in range(8):
                        S2.op("tensor", lambda e, j=j: e.transpose(
                            out=PBb[7][:, j * 128:(j + 1) * 128],
                            in_=xnb2[:, j * 128:(j + 1) * 128],
                            identity=ident_b[:]),
                            reads=[B(xnb2), B(ident_b)], writes=[B(PB[7])])
                    S2.op("vector", lambda e, i=i: e.tensor_copy(
                        out=xnT2[:].rearrange("p (b t) -> p b t", b=8)
                        [:, :, i * 128:(i + 1) * 128],
                        in_=PBb[7][:].rearrange("p (b t) -> p b t", b=8)),
                        reads=[B(PB[7])], writes=[B(xnT2)])

                def ccread(e, dsem, s0=s0):
                    if no_branch:
                        for bb_ in range(8):
                            e.dma_start(
                                out=ofin2[:].rearrange("p (b t) -> p b t",
                                                       b=8)[:, bb_, :],
                                in_=ccov[bb_, :, s0:s0 + SUB]).then_inc(
                                    dsem, 16)
                        return e.nop()
                    reg = loadreg.reg
                    with e.If_eq(reg, 0):
                        for bb_ in range(8):
                            e.dma_start(
                                out=ofin2[:].rearrange("p (b t) -> p b t",
                                                       b=8)[:, bb_, :],
                                in_=ccov[bb_, :, s0:s0 + SUB]).then_inc(
                                    dsem, 16)
                    with e.Else():
                        for bb_ in range(8):
                            e.dma_start(
                                out=ofin2[:].rearrange("p (b t) -> p b t",
                                                       b=8)[:, bb_, :],
                                in_=ccov[bb_, :, HALF_ + s0:HALF_ + s0 + SUB]
                            ).then_inc(dsem, 16)
                    return e.nop()
                S2.op(beng, ccread, reads=[regres], writes=[B(ofin2)],
                      kind="dma", n_dma=8)

                for hb in range(24):
                    for wsb2, pb_ in ((wgT_s, PB[4]), (wuT_s, PB[5])):
                        for j in range(8):
                            S2.op("tensor", lambda e, j=j, hb=hb, wsb2=wsb2,
                                  pb_=pb_: e.matmul(
                                      out=pb_[:, 0:SUB],
                                      lhsT=wsb2[:, j * FFH + hb * 128:
                                                j * FFH + (hb + 1) * 128],
                                      rhs=xnT2[:, j * SUB:(j + 1) * SUB],
                                      start=(j == 0), stop=(j == 7)),
                                  reads=[B(wsb2), B(xnT2)], writes=[B(pb_)])
                    S2.op("scalar", lambda e: e.activation(
                        out=sil2[:, 0:SUB], in_=PB[4][:, 0:SUB], func=AF.Silu,
                        scale=constc[:, 4:5]),
                        reads=[B(PB[4]), B(constc)], writes=[B(sil2)])
                    S2.op("vector", lambda e, hb=hb: e.scalar_tensor_tensor(
                        out=hbuf[:, (hb % 2) * SUB:(hb % 2 + 1) * SUB],
                        in0=sil2[:, 0:SUB], scalar=1.0 / 16129.0,
                        in1=PB[5][:, 0:SUB], op0=ALU.mult, op1=ALU.mult),
                        reads=[B(sil2), B(PB[5])], writes=[B(hbuf)])
                    for mb in range(2):
                        for nh_ in range(2):
                            S2.op("tensor", lambda e, hb=hb, mb=mb, nh_=nh_:
                                  e.matmul(
                                      out=PB[mb * 2 + nh_][:],
                                      lhsT=hbuf[:, (hb % 2) * SUB + mb * 128:
                                                (hb % 2) * SUB +
                                                (mb + 1) * 128],
                                      rhs=wdT_s[:, hb * D + nh_ * 512:
                                                hb * D + (nh_ + 1) * 512],
                                      start=(hb == 0), stop=False),
                                  reads=[B(hbuf), B(wdT_s)],
                                  writes=[B(PB[mb * 2 + nh_])])
                for bb_ in range(8):
                    for mb in range(2):
                        for nh_ in range(2):
                            S2.op("tensor", lambda e, bb_=bb_, mb=mb,
                                  nh_=nh_: e.matmul(
                                      out=PB[mb * 2 + nh_][:],
                                      lhsT=ofin2[:, bb_ * SUB + mb * 128:
                                                 bb_ * SUB + (mb + 1) * 128],
                                      rhs=wo_s[:, bb_ * D + nh_ * 512:
                                               bb_ * D + (nh_ + 1) * 512],
                                      start=False, stop=(bb_ == 7)),
                                  reads=[B(ofin2), B(wo_s)],
                                  writes=[B(PB[mb * 2 + nh_])])
                for mb in range(2):
                    for nh_ in range(2):
                        S2.op("vector", lambda e, mb=mb, nh_=nh_:
                              e.tensor_tensor(
                                  out=outsb[:, nh_ * 512:(nh_ + 1) * 512],
                                  in0=PB[mb * 2 + nh_][:],
                                  in1=xb2[mb][:, nh_ * 512:(nh_ + 1) * 512],
                                  op=ALU.add),
                              reads=[B(PB[mb * 2 + nh_]), B(xb2[mb])],
                              writes=[B(outsb)])
                    dma2(beng, out_tt[s_ * 2 + mb], outsb[:],
                         reads=[B(outsb)], writes=[outres])

            if skipb:
                S2.op("vector", lambda e: e.memset(outsb[:], 0.0),
                      writes=[B(outsb)])
                for ti_ in range(HALF_ // 128):
                    dma2(beng, out_tt[ti_], outsb[:], reads=[B(outsb)],
                         writes=[outres])
            S2.op(beng, lambda e: e.nop() if simdma else e.engine_nop(),
                  reads=[outres], writes=[])

            with nc.Block("phaseB") as blockB:
                S2.emit(blockB, sems, barrier_in=True)

    return nc


def kernel(**inputs):
    x = np.ascontiguousarray(inputs["L0_attn_hidden_states"])
    Wq = inputs["L0_attn_Wq"]
    Wk = inputs["L0_attn_Wk"]
    Wv = inputs["L0_attn_Wv"]
    qcw = inputs["L0_attn_qcw"]
    qcb = inputs["L0_attn_qcb"]
    kcw = inputs["L0_attn_kcw"]
    kcb = inputs["L0_attn_kcb"]
    vcw = inputs["L0_attn_vcw"]
    vcb = inputs["L0_attn_vcb"]
    Wf1 = inputs["L0_attn_Wf1"]
    Wf2 = inputs["L0_attn_Wf2"]
    Wb = inputs["L0_attn_Wb"]
    A_log = np.asarray(inputs["L0_attn_A_log"])
    dt_bias = np.asarray(inputs["L0_attn_dt_bias"])
    Wg1 = inputs["L0_attn_Wg1"]
    Wg2 = inputs["L0_attn_Wg2"]
    bg2 = np.asarray(inputs["L0_attn_bg2"])
    onw = np.asarray(inputs["L0_attn_o_norm_w"])
    Wo = inputs["L0_attn_Wo"]
    wg = inputs["L0_ffn_w_gate"]
    wu = inputs["L0_ffn_w_up"]
    wd = inputs["L0_ffn_w_down"]

    ident = np.eye(128, dtype=np.float32)
    s = np.arange(C)[:, None]
    t = np.arange(C)[None, :]
    masks = []
    for lv in (2, 4, 8, 16):
        h = lv // 2
        m = ((s // lv) == (t // lv)) & ((s % lv) < h) & ((t % lv) >= h)
        masks.append(np.tile(m.astype(np.uint8), (1, 4)))
    masks = np.ascontiguousarray(np.stack(masks))
    ind8 = np.zeros((4, NH, 128), np.float32)
    ind8T = np.zeros((4, 128, NH), np.float32)
    for cb in range(4):
        for par in range(2):
            h = cb * 2 + par
            ind8[cb, h, par * 64:(par + 1) * 64] = 1.0
            ind8T[cb, par * 64:(par + 1) * 64, h] = 1.0

    wgT = np.ascontiguousarray(wg.T)
    wuT = np.ascontiguousarray(wu.T)
    wdT = np.ascontiguousarray(wd.T)
    onw_t = np.ascontiguousarray(np.tile(onw.reshape(-1, 1), (2, 1)))

    in_maps = []
    for core in range(8):
        b, hg = core // 2, core % 2
        sl = slice(hg * CH, (hg + 1) * CH)
        hsl = slice(hg * NH, (hg + 1) * NH)
        cvw = np.ascontiguousarray(np.stack([qcw[sl], kcw[sl], vcw[sl]]))
        cvb = np.ascontiguousarray(np.stack(
            [qcb[sl].reshape(-1, 1), kcb[sl].reshape(-1, 1),
             vcb[sl].reshape(-1, 1)]))
        m = dict(
            x=np.ascontiguousarray(x[b]),
            xh=np.ascontiguousarray(x[b, hg * HALF:(hg + 1) * HALF]),
            wq=np.ascontiguousarray(Wq[:, sl]),
            wk=np.ascontiguousarray(Wk[:, sl]),
            wv=np.ascontiguousarray(Wv[:, sl]),
            cvw=cvw, cvb=cvb,
            wf1=np.ascontiguousarray(Wf1),
            wf2=np.ascontiguousarray(Wf2[:, sl]),
            wb=np.ascontiguousarray(Wb[:, hsl]),
            alog=np.ascontiguousarray(A_log[hsl].reshape(-1, 1)),
            dtb=np.ascontiguousarray(dt_bias[sl].reshape(-1, 1)),
            wg1=np.ascontiguousarray(Wg1),
            wg2=np.ascontiguousarray(Wg2[:, sl]),
            bg2=np.ascontiguousarray(bg2[sl].reshape(-1, 1)),
            onw=onw_t,
            wo=np.ascontiguousarray(Wo),
            wgT=wgT, wuT=wuT, wdT=wdT,
            ident=ident, masks=masks, ind8=ind8, ind8T=ind8T,
            coreoff=np.array([[hg]], dtype=np.int32),
        )
        in_maps.append(m)

    nc = build_nc()
    res = run_bass_kernel_spmd(nc, in_maps, core_ids=list(range(8)))
    out = np.empty((4, T, D), np.float32)
    for core in range(8):
        b, hg = core // 2, core % 2
        out[b, hg * HALF:(hg + 1) * HALF] = res.results[core]["out"]
    return out


# version 06 (final)
# speedup vs baseline: 1.0026x; 1.0018x over previous
import contextlib
import numpy as np
import concourse.bass as bass
import concourse.mybir as mybir
from concourse.bass_utils import run_bass_kernel_spmd

F32 = mybir.dt.float32
BF16 = mybir.dt.bfloat16
I32 = mybir.dt.int32
U8 = mybir.dt.uint8
AF = mybir.ActivationFunctionType
ALU = mybir.AluOpType
AXX = mybir.AxisListType.X

T = 8192
D = 1024
CH = 512
NH = 8
HK = 64
C = 128
GT = 256
NCHUNK = 2
NG = T // GT
HALF = 4096
SUB = 256
NSUB = HALF // SUB
FFH = 3072
PAD = 136


class Res:
    __slots__ = ("w", "rs", "wkind")

    def __init__(self):
        self.w = None
        self.rs = []
        self.wkind = "compute"


class Sched:
    ENGS = ("sync", "tensor", "vector", "scalar", "gpsimd")
    NPOOL = 6

    def __init__(self, carry=None, dmagen=None):
        self.ops = []
        self.carry = carry or {e: 0 for e in self.ENGS}
        self.carry.setdefault("cc", 0)
        self.dmagen = list(dmagen) if dmagen else [0] * self.NPOOL
        self.dmaslot_next = 0

    def op(self, engine, fn, reads=(), writes=(), kind="compute",
           extra_deps=(), n_dma=1):
        deps = set()
        for r in reads:
            if r.w is not None:
                deps.add(r.w)
        for w in writes:
            if w.w is not None:
                deps.add(w.w)
            deps.update(w.rs)
        for t_ in extra_deps:
            if t_ is not None:
                deps.add(t_)
        tok = len(self.ops)
        o = dict(engine=engine, fn=fn, deps=deps, kind=kind, inc=False,
                 cnt=None, slot=None, base=None, n_dma=n_dma)
        if kind == "dma":
            sl = self.dmaslot_next
            self.dmaslot_next = (sl + 1) % self.NPOOL
            o["slot"] = sl
            o["base"] = self.dmagen[sl]
            self.dmagen[sl] += 16 * n_dma
            o["cnt"] = self.dmagen[sl]
        self.ops.append(o)
        for r in reads:
            if tok not in r.rs:
                r.rs.append(tok)
        for w in writes:
            w.w = tok
            w.rs = []
        return tok

    def emit(self, block, sems, barrier_in=False):
        GAP = 6
        pos = {}
        pc = {e: 0 for e in self.ENGS}
        for i, o in enumerate(self.ops):
            pos[i] = pc[o["engine"]]
            pc[o["engine"]] += 1
        for i, o in enumerate(self.ops):
            if o["kind"] == "cc":
                o["inc"] = True
            o["syncdeps"] = []
            for d in o["deps"]:
                dop = self.ops[d]
                if dop["kind"] in ("dma", "cc"):
                    dop["inc"] = True
                    o["syncdeps"].append(d)
                elif dop["engine"] != o["engine"]:
                    dop["inc"] = True
                    o["syncdeps"].append(d)
                elif (dop["engine"] != "tensor"
                      and pos[i] - pos[d] < GAP):
                    dop["inc"] = True
                    o["syncdeps"].append(d)
        last = {}
        for i, o in enumerate(self.ops):
            if o["kind"] == "compute":
                last[o["engine"]] = i
        for e_, i in last.items():
            self.ops[i]["inc"] = True
        cnt = dict(self.carry)
        for o in self.ops:
            if o["kind"] == "dma":
                continue
            if o["inc"]:
                key = "cc" if o["kind"] == "cc" else o["engine"]
                cnt[key] += 1
                o["cnt"] = cnt[key]
        self.final = cnt

        active = {o["engine"] for o in self.ops}
        for ename in self.ENGS:
            if ename not in active:
                continue

            def body(e, ename=ename):
                hw = {k: self.carry.get(k, 0) for k in
                      list(self.ENGS) + ["cc"]}
                dhw = [0] * self.NPOOL
                if barrier_in:
                    for o_e in list(self.ENGS) + ["cc"]:
                        if o_e != ename and self.carry.get(o_e, 0) > 0:
                            e.wait_ge(sems[o_e], self.carry[o_e])
                    for sl in range(self.NPOOL):
                        g0 = self.carry.get(f"d{sl}", 0)
                        if g0 > 0:
                            e.wait_ge(sems[f"d{sl}"], g0)
                            dhw[sl] = g0
                for o in self.ops:
                    if o["engine"] != ename:
                        continue
                    for d in sorted(o["syncdeps"]):
                        dop = self.ops[d]
                        if dop["kind"] == "dma":
                            sl, c_ = dop["slot"], dop["cnt"]
                            if c_ > dhw[sl]:
                                e.wait_ge(sems[f"d{sl}"], c_)
                                dhw[sl] = c_
                            continue

                        key = "cc" if dop["kind"] == "cc" else dop["engine"]
                        c_ = dop["cnt"]
                        if c_ is not None and c_ > hw[key]:
                            e.wait_ge(sems[key], c_)
                            hw[key] = c_
                    if o["kind"] == "dma":
                        sl = o["slot"]
                        if o["base"] > dhw[sl]:
                            e.wait_ge(sems[f"d{sl}"], o["base"])
                            dhw[sl] = o["base"]
                        inst = o["fn"](e, sems[f"d{sl}"])
                    else:
                        inst = o["fn"](e)
                        if o["inc"]:
                            if o["kind"] == "cc":
                                inst.then_inc(sems["cc"])
                            else:
                                inst.then_inc(sems[ename], 1)
            getattr(block, ename)(body)


def build_nc(T_=T, HALF_=HALF, ncores=8, no_branch=False, no_cc=False, simdma=False, skipb=False, skipa=False):
    NGl = T_ // GT
    NSUBl = HALF_ // SUB
    rgroups = [[2 * i, 2 * i + 1] for i in range(ncores // 2)]
    nc = bass.Bass("TRN2", target_bir_lowering=False)

    def din(name, shape, dt):
        return nc.dram_tensor(name, list(shape), dt, kind="ExternalInput").ap()

    x_in = din("x", (T_, D), F32)
    xh_in = din("xh", (HALF_, D), F32)
    wq_in = din("wq", (D, CH), F32)
    wk_in = din("wk", (D, CH), F32)
    wv_in = din("wv", (D, CH), F32)
    cvw_in = din("cvw", (3, CH, 4), F32)
    cvb_in = din("cvb", (3, CH, 1), F32)
    wf1_in = din("wf1", (D, 64), F32)
    wf2_in = din("wf2", (64, CH), F32)
    wb_in = din("wb", (D, NH), F32)
    alog_in = din("alog", (NH, 1), F32)
    dtb_in = din("dtb", (CH, 1), F32)
    wg1_in = din("wg1", (D, 64), F32)
    wg2_in = din("wg2", (64, CH), F32)
    bg2_in = din("bg2", (CH, 1), F32)
    onw_in = din("onw", (128, 1), F32)
    wo_in = din("wo", (D, D), F32)
    wgT_in = din("wgT", (D, FFH), I32)
    wuT_in = din("wuT", (D, FFH), I32)
    wdT_in = din("wdT", (FFH, D), I32)
    ident_in = din("ident", (128, 128), F32)
    masks_in = din("masks", (4, C, 4 * C), U8)
    ind8_in = din("ind8", (4, NH, 128), F32)
    ind8T_in = din("ind8T", (4, 128, NH), F32)
    coreoff_in = din("coreoff", (1, 1), I32)

    out_t = nc.dram_tensor("out", [HALF_, D], F32, kind="ExternalOutput").ap()
    cc_in_f = nc.dram_tensor("cc_in", [CH, T_ // 2], F32, kind="Internal").ap()
    cc_out_f = nc.dram_tensor("cc_out", [D, T_ // 2], F32, kind="Internal").ap()
    cc_in = cc_in_f.bitcast(BF16)
    cc_out = cc_out_f.bitcast(BF16)

    ctx = contextlib.ExitStack()
    with ctx:
        def sb(name, shape, dt, c=ctx):
            return c.enter_context(nc.sbuf_tensor(name, list(shape), dt))

        sems = {e: ctx.enter_context(nc.semaphore("sem_" + e))
                for e in Sched.ENGS}
        sems["cc"] = ctx.enter_context(nc.semaphore("sem_cc"))
        for sl in range(Sched.NPOOL):
            sems[f"d{sl}"] = ctx.enter_context(nc.semaphore(f"sem_d{sl}"))
        PB = [ctx.enter_context(nc.psum_tensor(f"PB{i}", [128, 512], F32))
              for i in range(8)]
        PBb = [p[:].bitcast(BF16) for p in PB]

        R = {}

        def B(buf):
            key = id(buf)
            if key not in R:
                R[key] = Res()
            return R[key]

        ccin_res = Res()
        outres = Res()

        S = Sched()

        def mkdma(SS):
            def dma(engine, dst, src, reads=(), writes=(), deps=()):
                return SS.op(engine,
                             lambda e, dsem, dst=dst, src=src: e.dma_start(
                                 out=dst, in_=src).then_inc(dsem, 16),
                             reads=reads, writes=writes, kind="dma",
                             extra_deps=deps)
            return dma
        dma = mkdma(S)

        ident_f = sb("ident_f", (128, 128), F32)
        ident_b = sb("ident_b", (128, 128), BF16)
        soff = sb("soff", (1, 1), I32)
        constc = sb("constc", (128, 8), F32)
        dma("sync", ident_f[:], ident_in, writes=[B(ident_f)])
        dma("sync", soff[:], coreoff_in, writes=[B(soff)])
        S.op("vector", lambda e: e.tensor_copy(out=ident_b[:], in_=ident_f[:]),
             reads=[B(ident_f)], writes=[B(ident_b)])
        for ci, cv in enumerate((1e-5, -0.5, 0.5, 1.0 / HK, 1.0 / 127.0)):
            S.op("vector", lambda e, ci=ci, cv=cv: e.memset(
                constc[:, ci:ci + 1], cv), writes=[B(constc)])

        actx = contextlib.ExitStack()
        sba = lambda n, s_, d_: sb(n, s_, d_, actx)
        with actx:
            masks_s = sba("masks_s", (128, 4 * 4 * C), U8)
            ind8_s = sba("ind8_s", (NH, 4 * 128), BF16)
            ind8T_s = sba("ind8T_s", (128, 4 * NH), BF16)
            stage_f = sba("stage_f", (128, 8 * CH), F32)

            wq_s = sba("wq_s", (128, 8 * CH), BF16)
            wk_s = sba("wk_s", (128, 8 * CH), BF16)
            wv_s = sba("wv_s", (128, 8 * CH), BF16)
            wf1_s = sba("wf1_s", (128, 8 * 64), BF16)
            wg1_s = sba("wg1_s", (128, 8 * 64), BF16)
            wb_s = sba("wb_s", (128, 8 * NH), BF16)
            wf2_s = sba("wf2_s", (64, CH), BF16)
            wg2_s = sba("wg2_s", (64, CH), BF16)

            cvw_s = sba("cvw_s", (128, 3 * 4 * 4), F32)
            cvb_s = sba("cvb_s", (128, 3 * 4), F32)
            dtb_s = sba("dtb_s", (128, 4), F32)
            bg2_s = sba("bg2_s", (128, 4), F32)
            acol_s = sba("acol_s", (128, 4), F32)
            alog_s = sba("alog_s", (NH, 1), F32)
            aexp_b = sba("aexp_b", (NH, 1), BF16)

            xbuf = [sba(f"xbuf{i}", (128, D), F32) for i in range(2)]
            lnstat = sba("lnstat", (128, 16), F32)
            scr_big = sba("scr_big", (128, D), F32)
            xnb = sba("xnb", (128, D), BF16)
            xnTg = sba("xnTg", (128, 8 * GT), BF16)

            qs = [sba(f"qs{cb}", (128, GT), BF16) for cb in range(4)]
            convacc = [sba(f"convacc{p}", (128, GT), F32) for p in range(3)]
            ks = [sba(f"ks{cb}", (128, GT), BF16) for cb in range(4)]
            vs = [sba(f"vs{cb}", (128, GT), BF16) for cb in range(4)]
            cbuf = [sba(f"cbuf{p}", (128, 4 * (GT + 4)), BF16)
                    for p in range(3)]
            sqscr = sba("sqscr", (128, 512), BF16)
            rq_s = sba("rq_s", (NH, GT), BF16)
            rk_s = sba("rk_s", (NH, GT), BF16)
            kn = [sba(f"kn{cb}", (128, GT), BF16) for cb in range(4)]
            qn = [sba(f"qn{cb}", (128, GT), BF16) for cb in range(4)]
            bcols = sba("bcols", (128, NCHUNK * NH), F32)
            g1s = sba("g1s", (64, GT), BF16)
            spscr = [sba(f"spscr{i}", (128, GT), F32) for i in range(2)]
            gplus = [sba(f"gplus{cb}", (128, GT + 8), F32) for cb in range(4)]
            negb = [sba(f"negb{cb}", (128, NCHUNK * PAD), F32)
                    for cb in range(4)]
            zeros128 = sba("zeros128", (128, 128), F32)
            sgate = [sba(f"sgate{cb}", (128, GT), BF16) for cb in range(4)]

            lamf = [sba(f"lamf{cb}", (128, GT), BF16) for cb in range(4)]
            kp = [sba(f"kp{cb}", (128, GT), BF16) for cb in range(4)]
            qp = [sba(f"qp{cb}", (128, GT), BF16) for cb in range(4)]
            kd = [sba(f"kd{cb}", (128, GT), BF16) for cb in range(4)]
            facscrs = [sba(f"facscr{i}", (128, GT), F32) for i in range(2)]
            rowFs = [sba(f"rowF{i}", (128, GT), BF16) for i in range(2)]
            colFs = [sba(f"colF{i}", (128, GT), BF16) for i in range(2)]
            rnR = [[sba(f"rnR{lv}_{cb}", (128, GT), BF16) for cb in range(4)]
                   for lv in range(3)]
            raR = [[sba(f"raR{lv}_{cb}", (128, GT), BF16) for cb in range(4)]
                   for lv in range(3)]
            ckR = [[sba(f"ckR{lv}_{cb}", (128, GT), BF16) for cb in range(4)]
                   for lv in range(3)]
            rnL = [[sba(f"rnL{lv}_{cb}", (128, GT), BF16) for cb in range(4)]
                   for lv in range(3)]
            raL = [[sba(f"raL{lv}_{cb}", (128, GT), BF16) for cb in range(4)]
                   for lv in range(3)]
            ckL = [[sba(f"ckL{lv}_{cb}", (128, GT), BF16) for cb in range(4)]
                   for lv in range(3)]
            ck2 = [sba(f"ck2_{cb}", (128, GT), BF16) for cb in range(4)]
            e2 = sba("e2", (128, GT + 8), BF16)
            aqd_s = sba("aqd_s", (128, NCHUNK * NH), F32)
            rstd_s = sba("rstd_s", (128, NH), F32)
            lamC = sba("lamC", (128, 4), F32)
            negcol = sba("negcol", (128, 4), F32)
            bmap = sba("bmap", (128, NH * HK), F32)

            AT0 = sba("AT0", (128, NH * C), BF16)
            AQT = sba("AQT", (128, NH * C), BF16)
            A0 = sba("A0", (128, NH * C), BF16)
            Afrm = [sba(f"Afrm{k}", (128, NH * C), BF16) for k in range(2)]
            ATk = [sba(f"ATk{k}", (128, NH * C), BF16) for k in range(6)]
            Bfrm = sba("Bfrm", (128, NH * C), BF16)
            selscrs = [sba(f"selscr{i}", (128, 512), BF16) for i in range(2)]
            farsbs = [sba(f"farsb{i}", (128, 512), BF16) for i in range(2)]
            Xb = [sba(f"Xb{k}", (128, NH * HK), BF16) for k in range(2)]
            Rb = sba("Rb", (128, NH * HK), BF16)
            Xscr = sba("Xscr", (128, NH * HK), F32)
            vT_s = sba("vT_s", (128, NH * HK), BF16)
            Ub = sba("Ub", (128, NH * HK), BF16)
            S0 = [sba(f"S0_{cb}", (128, 128), F32) for cb in range(4)]
            S0b = [sba(f"S0b_{cb}", (128, 128), BF16) for cb in range(4)]
            kdT_s = sba("kdT_s", (128, NH * HK), BF16)
            omid = sba("omid", (128, NH * HK), BF16)
            ofinT = sba("ofinT", (128, 4 * GT), BF16)
            qkscr = sba("qkscr", (128, C), BF16)

            import os
            KLVL = int(os.environ.get("KLVL", "9"))
            for li in range(4 if KLVL >= 1 else 0):
                dma("sync", masks_s[:, li * 4 * C:(li + 1) * 4 * C],
                    masks_in[li], writes=[B(masks_s)])
            for cb in range(4 if KLVL >= 1 else 0):
                dma("sync", stage_f[0:NH, cb * 128:(cb + 1) * 128],
                    ind8_in[cb], writes=[B(stage_f)])
            S.op("vector", lambda e: e.tensor_copy(
                out=ind8_s[:], in_=stage_f[0:NH, 0:4 * 128]),
                reads=[B(stage_f)], writes=[B(ind8_s)])
            for cb in range(4):
                dma("sync", stage_f[:, cb * NH:(cb + 1) * NH], ind8T_in[cb],
                    reads=[B(ind8_s)], writes=[B(stage_f)])
            S.op("vector", lambda e: e.tensor_copy(
                out=ind8T_s[:], in_=stage_f[:, 0:4 * NH]),
                reads=[B(stage_f)], writes=[B(ind8T_s)])
            for win, wsb in (((wq_in, wq_s), (wk_in, wk_s), (wv_in, wv_s))
                             if KLVL >= 2 else ()):
                for j in range(8):
                    dma("sync", stage_f[:, j * CH:(j + 1) * CH],
                        win[j * 128:(j + 1) * 128, :],
                        reads=[B(wsb)], writes=[B(stage_f)])
                S.op("vector", lambda e, wsb=wsb: e.tensor_copy(
                    out=wsb[:], in_=stage_f[:]),
                    reads=[B(stage_f)], writes=[B(wsb)])
            for win, wsb, w_ in (((wf1_in, wf1_s, 64), (wg1_in, wg1_s, 64),
                                 (wb_in, wb_s, NH)) if KLVL >= 2 else ()):
                for j in range(8):
                    dma("sync", stage_f[:, j * w_:(j + 1) * w_],
                        win[j * 128:(j + 1) * 128, :],
                        reads=[B(wsb)], writes=[B(stage_f)])
                S.op("vector", lambda e, wsb=wsb, w_=w_: e.tensor_copy(
                    out=wsb[:], in_=stage_f[:, 0:8 * w_]),
                    reads=[B(stage_f)], writes=[B(wsb)])
            for win, wsb in (((wf2_in, wf2_s), (wg2_in, wg2_s))
                             if KLVL >= 2 else ()):
                dma("sync", stage_f[0:64, 0:CH], win,
                    reads=[B(wsb)], writes=[B(stage_f)])
                S.op("vector", lambda e, wsb=wsb: e.tensor_copy(
                    out=wsb[:], in_=stage_f[0:64, 0:CH]),
                    reads=[B(stage_f)], writes=[B(wsb)])
            for pj in range(3 if KLVL >= 3 else 0):
                for cb in range(4):
                    dma("sync", cvw_s[:, (pj * 4 + cb) * 4:(pj * 4 + cb) * 4 + 4],
                        cvw_in[pj, cb * 128:(cb + 1) * 128, :],
                        writes=[B(cvw_s)])
                    dma("sync", cvb_s[:, pj * 4 + cb:pj * 4 + cb + 1],
                        cvb_in[pj, cb * 128:(cb + 1) * 128, :],
                        writes=[B(cvb_s)])
            for cb in range(4 if KLVL >= 3 else 0):
                dma("sync", dtb_s[:, cb:cb + 1],
                    dtb_in[cb * 128:(cb + 1) * 128, :], writes=[B(dtb_s)])
                dma("sync", bg2_s[:, cb:cb + 1],
                    bg2_in[cb * 128:(cb + 1) * 128, :], writes=[B(bg2_s)])
            if KLVL >= 4:
                dma("sync", alog_s[:], alog_in, writes=[B(alog_s)])
                S.op("scalar", lambda e: e.activation(
                    out=aexp_b[:], in_=alog_s[:], func=AF.Exp),
                    reads=[B(alog_s)], writes=[B(aexp_b)])
                for cb in range(4):
                    S.op("tensor", lambda e, cb=cb: e.matmul(
                        out=PB[5][:, cb:cb + 1],
                        lhsT=ind8_s[:, cb * 128:(cb + 1) * 128], rhs=aexp_b[:],
                        start=(cb == 0), stop=(cb == 3)),
                        reads=[B(ind8_s), B(aexp_b)], writes=[B(PB[5])])
                S.op("scalar", lambda e: e.copy(
                    out=acol_s[:], in_=PB[5][:, 0:4]),
                    reads=[B(PB[5])], writes=[B(acol_s)])
            S.op("vector", lambda e: e.memset(zeros128[:], 0.0),
                 writes=[B(zeros128)])
            for cb in range(4):
                S.op("vector", lambda e, cb=cb: e.memset(S0[cb][:], 0.0),
                     writes=[B(S0[cb])])
                S.op("vector", lambda e, cb=cb: e.memset(S0b[cb][:], 0.0),
                     writes=[B(S0b[cb])])
                S.op("vector", lambda e, cb=cb: e.memset(negb[cb][:], 1e9),
                     writes=[B(negb[cb])])
                S.op("vector", lambda e, cb=cb: e.memset(gplus[cb][:], 1e9),
                     writes=[B(gplus[cb])])
            for p in range(3):
                S.op("vector", lambda e, p=p: e.memset(cbuf[p][:], 0.0),
                     writes=[B(cbuf[p])])
            S.op("vector", lambda e: e.memset(e2[:], 0.0), writes=[B(e2)])

            x_t = x_in.rearrange("(n p) d -> n p d", p=128)
            nbv = [negb[cb][:].rearrange("p (c s) -> p c s", c=NCHUNK)
                   for cb in range(4)]

            def emit_ln(SS, dmaf, xb, src_ap, stat, scr, xout, sem_unused):
                dmaf("sync", xb[:], src_ap, writes=[B(xb)])
                SS.op("vector", lambda e: e.reduce_sum(
                    out=stat[:, 0:1], in_=xb[:], axis=AXX),
                    reads=[B(xb)], writes=[B(stat)])
                SS.op("scalar", lambda e: e.activation(
                    out=scr[:], in_=xb[:], func=AF.Square),
                    reads=[B(xb)], writes=[B(scr)])
                SS.op("vector", lambda e: e.reduce_sum(
                    out=stat[:, 1:2], in_=scr[:], axis=AXX),
                    reads=[B(scr)], writes=[B(stat)])
                SS.op("vector", lambda e: e.tensor_scalar(
                    out=stat[:, 2:3], in0=stat[:, 0:1], scalar1=1.0 / D,
                    scalar2=None, op0=ALU.mult),
                    reads=[B(stat)], writes=[B(stat)])
                SS.op("vector", lambda e: e.tensor_tensor(
                    out=stat[:, 3:4], in0=stat[:, 2:3], in1=stat[:, 2:3],
                    op=ALU.mult), reads=[B(stat)], writes=[B(stat)])
                SS.op("vector", lambda e: e.tensor_scalar(
                    out=stat[:, 4:5], in0=stat[:, 1:2], scalar1=1.0 / D,
                    scalar2=stat[:, 3:4], op0=ALU.mult, op1=ALU.subtract),
                    reads=[B(stat)], writes=[B(stat)])
                SS.op("scalar", lambda e: e.activation(
                    out=stat[:, 5:6], in_=stat[:, 4:5], func=AF.Ln,
                    bias=constc[:, 0:1]),
                    reads=[B(stat), B(constc)], writes=[B(stat)])
                SS.op("scalar", lambda e: e.activation(
                    out=stat[:, 6:7], in_=stat[:, 5:6], func=AF.Exp,
                    scale=constc[:, 1:2]),
                    reads=[B(stat), B(constc)], writes=[B(stat)])
                SS.op("vector", lambda e: e.tensor_scalar(
                    out=xout[:], in0=xb[:], scalar1=stat[:, 2:3],
                    scalar2=stat[:, 6:7], op0=ALU.subtract, op1=ALU.mult),
                    reads=[B(xb), B(stat)], writes=[B(xout)])

            for g in range(0 if skipa else NGl):
                T0 = g * GT
                for i in range(GT // 128):
                    ti = g * (GT // 128) + i
                    xb = xbuf[ti % 2]
                    emit_ln(S, dma, xb, x_t[ti], lnstat, scr_big, xnb, None)
                    for j in range(8):
                        S.op("tensor", lambda e, j=j: e.transpose(
                            out=PBb[7][:, j * 128:(j + 1) * 128],
                            in_=xnb[:, j * 128:(j + 1) * 128],
                            identity=ident_b[:]),
                            reads=[B(xnb), B(ident_b)], writes=[B(PB[7])])
                    S.op("vector", lambda e, i=i: e.tensor_copy(
                        out=xnTg[:].rearrange("p (b t) -> p b t", b=8)
                        [:, :, i * 128:(i + 1) * 128],
                        in_=PBb[7][:].rearrange("p (b t) -> p b t", b=8)),
                        reads=[B(PB[7])], writes=[B(xnTg)])

                for cb in range(4):
                    for pj, wsb in enumerate((wq_s, wk_s, wv_s)):
                        pp = PB[pj]
                        for j in range(8):
                            S.op("tensor", lambda e, j=j, cb=cb, wsb=wsb,
                                 pp=pp: e.matmul(
                                     out=pp[:, 0:GT],
                                     lhsT=wsb[:, j * CH + cb * 128:
                                              j * CH + (cb + 1) * 128],
                                     rhs=xnTg[:, j * GT:(j + 1) * GT],
                                     start=(j == 0), stop=(j == 7)),
                                 reads=[B(wsb), B(xnTg)], writes=[B(pp)])
                    for pj in range(3):
                        pp = PB[pj]
                        dstl = [qs, ks, vs][pj]
                        cbf = cbuf[pj]
                        c0 = cb * (GT + 4)
                        S.op("vector", lambda e, cbf=cbf, c0=c0: e.tensor_copy(
                            out=cbf[:, c0:c0 + 3],
                            in_=cbf[:, c0 + GT:c0 + GT + 3]),
                            reads=[B(cbf)], writes=[B(cbf)])
                        S.op("scalar", lambda e, cbf=cbf, c0=c0, pp=pp: e.copy(
                            out=cbf[:, c0 + 3:c0 + 3 + GT], in_=pp[:, 0:GT]),
                            reads=[B(pp)], writes=[B(cbf)])
                        wc0 = (pj * 4 + cb) * 4
                        cacc = convacc[pj]
                        S.op("vector", lambda e, cbf=cbf, c0=c0, wc0=wc0,
                             cacc=cacc: e.tensor_scalar(
                                 out=cacc[:], in0=cbf[:, c0:c0 + GT],
                                 scalar1=cvw_s[:, wc0:wc0 + 1], scalar2=None,
                                 op0=ALU.mult),
                             reads=[B(cbf), B(cvw_s)], writes=[B(cacc)])
                        for tp in range(1, 4):
                            S.op("vector", lambda e, cbf=cbf, c0=c0, wc0=wc0,
                                 tp=tp, cacc=cacc: e.scalar_tensor_tensor(
                                     out=cacc[:],
                                     in0=cbf[:, c0 + tp:c0 + tp + GT],
                                     scalar=cvw_s[:, wc0 + tp:wc0 + tp + 1],
                                     in1=cacc[:], op0=ALU.mult,
                                     op1=ALU.add),
                                 reads=[B(cbf), B(cvw_s), B(cacc)],
                                 writes=[B(cacc)])
                        bc = pj * 4 + cb
                        S.op("scalar", lambda e, bc=bc, dst=dstl[cb],
                             cacc=cacc: e.activation(
                                 out=dst[:], in_=cacc[:],
                                 func=AF.Silu, bias=cvb_s[:, bc:bc + 1]),
                             reads=[B(cacc), B(cvb_s)], writes=[B(dstl[cb])])

                for j in range(8):
                    S.op("tensor", lambda e, j=j: e.matmul(
                        out=PB[3][0:NH, 0:GT],
                        lhsT=wb_s[:, j * NH:(j + 1) * NH],
                        rhs=xnTg[:, j * GT:(j + 1) * GT],
                        start=(j == 0), stop=(j == 7)),
                        reads=[B(wb_s), B(xnTg)], writes=[B(PB[3])])
                S.op("scalar", lambda e: e.activation(
                    out=rq_s[:], in_=PB[3][0:NH, 0:GT], func=AF.Sigmoid),
                    reads=[B(PB[3])], writes=[B(rq_s)])
                for c in range(NCHUNK):
                    S.op("tensor", lambda e, c=c: e.transpose(
                        out=PBb[5][0:C, c * NH:(c + 1) * NH],
                        in_=rq_s[0:NH, c * 128:(c + 1) * 128],
                        identity=ident_b[0:NH, 0:NH]),
                        reads=[B(rq_s), B(ident_b)], writes=[B(PB[5])])
                S.op("scalar", lambda e: e.copy(
                    out=bcols[:], in_=PBb[5][:, 0:NCHUNK * NH]),
                    reads=[B(PB[5])], writes=[B(bcols)])

                for qki, (src, rmap) in enumerate(((qs, rq_s),
                                                       (ks, rk_s))):
                    sp_ = spscr[qki]
                    for cb in range(4):
                        S.op("scalar", lambda e, src=src, cb=cb: e.activation(
                            out=sqscr[:, 0:GT], in_=src[cb][:], func=AF.Square),
                            reads=[B(src[cb])], writes=[B(sqscr)])
                        S.op("tensor", lambda e, cb=cb: e.matmul(
                            out=PB[3][0:NH, 0:GT],
                            lhsT=ind8T_s[:, cb * NH:(cb + 1) * NH],
                            rhs=sqscr[:, 0:GT], start=(cb == 0), stop=(cb == 3)),
                            reads=[B(ind8T_s), B(sqscr)], writes=[B(PB[3])])
                    S.op("scalar", lambda e, sp_=sp_: e.activation(
                        out=sp_[0:NH, 0:GT], in_=PB[3][0:NH, 0:GT],
                        func=AF.Ln), reads=[B(PB[3])], writes=[B(sp_)])
                    S.op("scalar", lambda e, sp_=sp_: e.activation(
                        out=sp_[0:NH, 0:GT], in_=sp_[0:NH, 0:GT],
                        func=AF.Exp, scale=constc[0:NH, 2:3]),
                        reads=[B(sp_), B(constc)], writes=[B(sp_)])
                    S.op("vector", lambda e, sp_=sp_: e.tensor_scalar(
                        out=sp_[0:NH, 0:GT], in0=sp_[0:NH, 0:GT],
                        scalar1=1e-6, scalar2=None, op0=ALU.add),
                        reads=[B(sp_)], writes=[B(sp_)])
                    S.op("vector", lambda e, sp_=sp_: e.reciprocal(
                        out=sp_[64:64 + NH, 0:GT], in_=sp_[0:NH, 0:GT]),
                        reads=[B(sp_)], writes=[B(sp_)])
                    S.op("vector", lambda e, rmap=rmap, sp_=sp_: e.tensor_copy(
                        out=rmap[:], in_=sp_[64:64 + NH, 0:GT]),
                        reads=[B(sp_)], writes=[B(rmap)])
                for cb in range(4):
                    S.op("tensor", lambda e, cb=cb: e.matmul(
                        out=PB[4][:, 0:GT],
                        lhsT=ind8_s[:, cb * 128:(cb + 1) * 128], rhs=rk_s[:],
                        start=True, stop=True),
                        reads=[B(ind8_s), B(rk_s)], writes=[B(PB[4])])
                    S.op("vector", lambda e, cb=cb: e.tensor_tensor(
                        out=kn[cb][:], in0=ks[cb][:], in1=PB[4][:, 0:GT],
                        op=ALU.mult),
                        reads=[B(ks[cb]), B(PB[4])], writes=[B(kn[cb])])
                    S.op("tensor", lambda e, cb=cb: e.matmul(
                        out=PB[4][:, 0:GT],
                        lhsT=ind8_s[:, cb * 128:(cb + 1) * 128], rhs=rq_s[:],
                        start=True, stop=True),
                        reads=[B(ind8_s), B(rq_s)], writes=[B(PB[4])])
                    S.op("vector", lambda e, cb=cb: e.tensor_tensor(
                        out=qn[cb][:], in0=qs[cb][:], in1=PB[4][:, 0:GT],
                        op=ALU.mult),
                        reads=[B(qs[cb]), B(PB[4])], writes=[B(qn[cb])])

                for w1sb, w2sb, is_gate in ((wf1_s, wf2_s, False),
                                            (wg1_s, wg2_s, True)):
                    for j in range(8):
                        S.op("tensor", lambda e, j=j, w1sb=w1sb: e.matmul(
                            out=PB[4][0:64, 0:GT],
                            lhsT=w1sb[:, j * 64:(j + 1) * 64],
                            rhs=xnTg[:, j * GT:(j + 1) * GT],
                            start=(j == 0), stop=(j == 7)),
                            reads=[B(w1sb), B(xnTg)], writes=[B(PB[4])])
                    S.op("scalar", lambda e: e.copy(
                        out=g1s[:], in_=PB[4][0:64, 0:GT]),
                        reads=[B(PB[4])], writes=[B(g1s)])
                    for cb in range(4):
                        S.op("tensor", lambda e, cb=cb, w2sb=w2sb: e.matmul(
                            out=PB[5][:, 0:GT],
                            lhsT=w2sb[:, cb * 128:(cb + 1) * 128], rhs=g1s[:],
                            start=True, stop=True),
                            reads=[B(w2sb), B(g1s)], writes=[B(PB[5])])
                        if is_gate:
                            S.op("scalar", lambda e, cb=cb: e.activation(
                                out=sgate[cb][:], in_=PB[5][:, 0:GT],
                                func=AF.Sigmoid, bias=bg2_s[:, cb:cb + 1]),
                                reads=[B(PB[5]), B(bg2_s)],
                                writes=[B(sgate[cb])])
                        else:
                            S.op("scalar", lambda e, cb=cb: e.activation(
                                out=spscr[0][:], in_=PB[5][:, 0:GT],
                                func=AF.Abs, bias=dtb_s[:, cb:cb + 1]),
                                reads=[B(PB[5]), B(dtb_s)],
                                writes=[B(spscr[0])])
                            S.op("scalar", lambda e: e.activation(
                                out=spscr[0][:], in_=spscr[0][:], func=AF.Exp,
                                scale=-1.0),
                                reads=[B(spscr[0])], writes=[B(spscr[0])])
                            S.op("scalar", lambda e: e.activation(
                                out=spscr[0][:], in_=spscr[0][:], func=AF.Ln,
                                bias=1.0),
                                reads=[B(spscr[0])], writes=[B(spscr[0])])
                            S.op("scalar", lambda e, cb=cb: e.activation(
                                out=spscr[1][:], in_=PB[5][:, 0:GT],
                                func=AF.Relu, bias=dtb_s[:, cb:cb + 1]),
                                reads=[B(PB[5]), B(dtb_s)],
                                writes=[B(spscr[1])])
                            S.op("vector", lambda e: e.tensor_tensor(
                                out=spscr[0][:], in0=spscr[0][:],
                                in1=spscr[1][:], op=ALU.add),
                                reads=[B(spscr[0]), B(spscr[1])],
                                writes=[B(spscr[0])])
                            S.op("vector", lambda e, cb=cb: e.tensor_scalar(
                                out=gplus[cb][:, 0:GT], in0=spscr[0][:],
                                scalar1=acol_s[:, cb:cb + 1], scalar2=None,
                                op0=ALU.mult),
                                reads=[B(spscr[0]), B(acol_s)],
                                writes=[B(gplus[cb])])

                for cb in range(4):
                    for c in range(NCHUNK):
                        S.op("vector", lambda e, cb=cb, c=c:
                             e.tensor_tensor_scan(
                                 out=negb[cb][:, c * PAD:c * PAD + 128],
                                 data0=gplus[cb][:, c * 128:(c + 1) * 128],
                                 data1=zeros128[:], initial=0.0, op0=ALU.add,
                                 op1=ALU.add),
                             reads=[B(gplus[cb]), B(zeros128)],
                             writes=[B(negb[cb])])
                    S.op("scalar", lambda e, cb=cb: e.activation(
                        out=lamf[cb][:].rearrange("p (c s) -> p c s", c=NCHUNK),
                        in_=nbv[cb][:, :, 0:128], func=AF.Exp, scale=-1.0),
                        reads=[B(negb[cb])], writes=[B(lamf[cb])])
                    S.op("vector", lambda e, cb=cb: e.tensor_tensor(
                        out=kp[cb][:], in0=kn[cb][:], in1=lamf[cb][:],
                        op=ALU.mult),
                        reads=[B(kn[cb]), B(lamf[cb])], writes=[B(kp[cb])])
                    S.op("vector", lambda e, cb=cb: e.scalar_tensor_tensor(
                        out=qp[cb][:], in0=qn[cb][:], scalar=0.125,
                        in1=lamf[cb][:], op0=ALU.mult, op1=ALU.mult),
                        reads=[B(qn[cb]), B(lamf[cb])], writes=[B(qp[cb])])

                    facscr = facscrs[cb % 2]
                    rowF = rowFs[cb % 2]
                    colF = colFs[cb % 2]

                    def facpair(h_, cb=cb, facscr=facscr, rowF=rowF,
                                colF=colF):
                        na = 128 // h_
                        S.op("vector", lambda e, cb=cb, h_=h_, na=na:
                             e.tensor_tensor(
                                 out=facscr[:].rearrange(
                                     "p (c a b) -> p c a b", c=NCHUNK, b=h_),
                                 in0=nbv[cb][:, :, 0:128:h_].unsqueeze(3)
                                 .broadcast_to((128, NCHUNK, na, h_)),
                                 in1=nbv[cb][:, :, 0:128].rearrange(
                                     "p c (a b) -> p c a b", b=h_),
                                 op=ALU.subtract),
                             reads=[B(negb[cb])], writes=[B(facscr)])
                        S.op("scalar", lambda e: e.activation(
                            out=rowF[:], in_=facscr[:], func=AF.Exp),
                            reads=[B(facscr)], writes=[B(rowF)])
                        S.op("vector", lambda e, cb=cb, h_=h_, na=na:
                             e.tensor_tensor(
                                 out=facscr[:].rearrange(
                                     "p (c a b) -> p c a b", c=NCHUNK, b=h_),
                                 in0=nbv[cb][:, :, 0:128].rearrange(
                                     "p c (a b) -> p c a b", b=h_),
                                 in1=nbv[cb][:, :, h_:129:h_].unsqueeze(3)
                                 .broadcast_to((128, NCHUNK, na, h_)),
                                 op=ALU.subtract),
                             reads=[B(negb[cb])], writes=[B(facscr)])
                        S.op("scalar", lambda e: e.activation(
                            out=colF[:], in_=facscr[:], func=AF.Exp),
                            reads=[B(facscr)], writes=[B(colF)])

                    def rowmuls(rn_, ra_, cb=cb, rowF=rowF):
                        S.op("vector", lambda e, cb=cb, rn_=rn_:
                             e.tensor_tensor(out=rn_[:], in0=kn[cb][:],
                                             in1=rowF[:], op=ALU.mult),
                             reads=[B(kn[cb]), B(rowF)], writes=[B(rn_)])
                        S.op("vector", lambda e, cb=cb, ra_=ra_:
                             e.scalar_tensor_tensor(
                                 out=ra_[:], in0=qn[cb][:], scalar=0.125,
                                 in1=rowF[:], op0=ALU.mult, op1=ALU.mult),
                             reads=[B(qn[cb]), B(rowF)], writes=[B(ra_)])

                    def colmul(ck_, cb=cb, colF=colF):
                        S.op("vector", lambda e, cb=cb, ck_=ck_:
                             e.tensor_tensor(out=ck_[:], in0=kn[cb][:],
                                             in1=colF[:], op=ALU.mult),
                             reads=[B(kn[cb]), B(colF)], writes=[B(ck_)])

                    for li, hl in enumerate((16, 32, 64)):
                        facpair(hl)
                        rowmuls(rnR[li][cb], raR[li][cb])
                        colmul(ckR[li][cb])
                    for li, hl in enumerate((2, 4, 8)):
                        facpair(hl)
                        rowmuls(rnL[li][cb], raL[li][cb])
                        colmul(ckL[li][cb])
                    S.op("scalar", lambda e, cb=cb: e.activation(
                        out=e2[:, 0:GT], in_=gplus[cb][:, 1:GT + 1],
                        func=AF.Exp, scale=-1.0),
                        reads=[B(gplus[cb])], writes=[B(e2)])
                    S.op("vector", lambda e, cb=cb: e.tensor_tensor(
                        out=ck2[cb][:], in0=kn[cb][:], in1=e2[:, 0:GT],
                        op=ALU.mult),
                        reads=[B(kn[cb]), B(e2)], writes=[B(ck2[cb])])
                    for c in range(NCHUNK):
                        S.op("vector", lambda e, cb=cb, c=c: e.tensor_scalar(
                            out=negcol[:, cb:cb + 1],
                            in0=negb[cb][:, c * PAD + 127:c * PAD + 128],
                            scalar1=-1.0, scalar2=None, op0=ALU.mult),
                            reads=[B(negb[cb])], writes=[B(negcol)])
                        S.op("scalar", lambda e, cb=cb, c=c,
                             facscr=facscr: e.activation(
                                 out=facscr[:, c * 128:(c + 1) * 128],
                                 in_=negb[cb][:, c * PAD:c * PAD + 128],
                                 func=AF.Exp, bias=negcol[:, cb:cb + 1]),
                             reads=[B(negb[cb]), B(negcol)],
                             writes=[B(facscr)])
                    S.op("vector", lambda e, cb=cb, facscr=facscr:
                         e.tensor_tensor(
                             out=kd[cb][:], in0=kn[cb][:], in1=facscr[:],
                             op=ALU.mult),
                         reads=[B(kn[cb]), B(facscr)], writes=[B(kd[cb])])

                for c in range(NCHUNK):
                    for cb in range(4):
                        S.op("vector", lambda e, cb=cb, c=c: e.tensor_tensor(
                            out=qkscr[:], in0=qn[cb][:, c * 128:(c + 1) * 128],
                            in1=kn[cb][:, c * 128:(c + 1) * 128], op=ALU.mult),
                            reads=[B(qn[cb]), B(kn[cb])], writes=[B(qkscr)])
                        S.op("tensor", lambda e, cb=cb, c=c: e.matmul(
                            out=PB[5][:, 480 + c * NH:480 + (c + 1) * NH],
                            lhsT=qkscr[:],
                            rhs=ind8T_s[:, cb * NH:(cb + 1) * NH],
                            start=(cb == 0 and c == 0),
                            stop=(cb == 3 and c == NCHUNK - 1)),
                            reads=[B(qkscr), B(ind8T_s)], writes=[B(PB[5])])
                S.op("scalar", lambda e: e.mul(
                    out=aqd_s[:], in_=PB[5][:, 480:480 + NCHUNK * NH],
                    mul=0.125),
                    reads=[B(PB[5])], writes=[B(aqd_s)])

                for c in range(NCHUNK):
                    tc = slice(c * 128, (c + 1) * 128)
                    for cb in range(4):
                        S.op("scalar", lambda e, cb=cb, c=c: e.activation(
                            out=lamC[:, cb:cb + 1],
                            in_=negb[cb][:, c * PAD + 127:c * PAD + 128],
                            func=AF.Exp, scale=-1.0),
                            reads=[B(negb[cb])], writes=[B(lamC)])
                    S.op("vector", lambda e, c=c: e.tensor_copy(
                        out=bmap[:].rearrange("p (h v) -> p h v", h=NH),
                        in_=bcols[:, c * NH:(c + 1) * NH].unsqueeze(2)
                        .broadcast_to((128, NH, HK))),
                        reads=[B(bcols)], writes=[B(bmap)])

                    RECTS = ((0, [(0, 16), (32, 48), (64, 80), (96, 112)]),
                             (1, [(0, 32), (64, 96)]),
                             (2, [(0, 64)]))
                    for mat in range(2):
                        rR_ = rnR if mat == 0 else raR
                        rL_ = rnL if mat == 0 else raL
                        dst = AT0 if mat == 0 else AQT
                        selscr = selscrs[mat]
                        farsb = farsbs[mat]
                        for grp in range(2):
                            S.op("vector", lambda e: e.memset(PB[0][:], 0.0),
                                 writes=[B(PB[0])])
                            for hh in range(4):
                                h = grp * 4 + hh
                                cb, par = h // 2, h % 2
                                kb = par * 64
                                for li, rects in RECTS:
                                    hw_ = 16 << li
                                    for s0, t0 in rects:
                                        S.op("tensor", lambda e, hh=hh, li=li,
                                             cb=cb, kb=kb, c=c, rR_=rR_,
                                             s0=s0, t0=t0, hw_=hw_: e.matmul(
                                                 out=PB[0][s0:s0 + hw_,
                                                           hh * 128 + t0:
                                                           hh * 128 + t0 + hw_],
                                                 lhsT=ckR[li][cb][kb:kb + 64,
                                                                  c * 128 + s0:
                                                                  c * 128 + s0 + hw_],
                                                 rhs=rR_[li][cb][kb:kb + 64,
                                                                 c * 128 + t0:
                                                                 c * 128 + t0 + hw_],
                                                 start=True, stop=True,
                                                 tile_position=(kb, s0 - s0 % 32)),
                                             reads=[B(ckR[li][cb]), B(rR_[li][cb])],
                                             writes=[B(PB[0])])
                                br = kn if mat == 0 else qn
                                S.op("tensor", lambda e, hh=hh, cb=cb, kb=kb,
                                     c=c, br=br: e.matmul(
                                         out=PB[1][:, hh * 128:(hh + 1) * 128],
                                         lhsT=ck2[cb][kb:kb + 64,
                                                      c * 128:(c + 1) * 128],
                                         rhs=br[cb][kb:kb + 64,
                                                    c * 128:(c + 1) * 128],
                                         start=True, stop=True,
                                         tile_position=(kb, 0)),
                                     reads=[B(ck2[cb]), B(br[cb])],
                                     writes=[B(PB[1])])
                                for li in range(3):
                                    pbk = PB[2 + li]
                                    S.op("tensor", lambda e, hh=hh, cb=cb,
                                         kb=kb, c=c, li=li, pbk=pbk, rL_=rL_:
                                         e.matmul(
                                             out=pbk[:, hh * 128:(hh + 1) * 128],
                                             lhsT=ckL[li][cb][kb:kb + 64,
                                                             c * 128:
                                                             (c + 1) * 128],
                                             rhs=rL_[li][cb][kb:kb + 64,
                                                             c * 128:
                                                             (c + 1) * 128],
                                             start=True, stop=True,
                                             tile_position=(kb, 0)),
                                         reads=[B(ckL[li][cb]),
                                                B(rL_[li][cb])],
                                         writes=[B(pbk)])
                            S.op("scalar", lambda e: e.copy(
                                out=farsb[:], in_=PB[0][:]),
                                reads=[B(PB[0])], writes=[B(farsb)])
                            mview = lambda li: masks_s[:, li * 512:
                                                       (li + 1) * 512]
                            S.op("vector", lambda e, mview=mview: e.select(
                                out=selscr[:], mask=mview(3),
                                on_true=PB[4][:], on_false=farsb[:]),
                                reads=[B(PB[4]), B(farsb), B(masks_s)],
                                writes=[B(selscr)])
                            S.op("vector", lambda e, mview=mview: e.select(
                                out=selscr[:], mask=mview(2),
                                on_true=PB[3][:], on_false=selscr[:]),
                                reads=[B(PB[3]), B(selscr), B(masks_s)],
                                writes=[B(selscr)])
                            S.op("vector", lambda e, mview=mview: e.select(
                                out=selscr[:], mask=mview(1),
                                on_true=PB[2][:], on_false=selscr[:]),
                                reads=[B(PB[2]), B(selscr), B(masks_s)],
                                writes=[B(selscr)])
                            S.op("vector", lambda e, dst=dst, grp=grp,
                                 mview=mview: e.select(
                                     out=dst[:, grp * 512:(grp + 1) * 512],
                                     mask=mview(0), on_true=PB[1][:],
                                     on_false=selscr[:]),
                                 reads=[B(PB[1]), B(selscr), B(masks_s)],
                                 writes=[B(dst)])

                    for h in range(NH):
                        S.op("tensor", lambda e, h=h: e.transpose(
                            out=PBb[6][:, (h % 4) * 128:(h % 4 + 1) * 128],
                            in_=AT0[:, h * 128:(h + 1) * 128],
                            identity=ident_b[:]),
                            reads=[B(AT0), B(ident_b)], writes=[B(PB[6])])
                        if h == 3 or h == 7:
                            off = (h // 4) * 512
                            S.op("vector", lambda e, off=off: e.tensor_copy(
                                out=A0[:, off:off + 512],
                                in_=PBb[6][:, 0:512]),
                                reads=[B(PB[6])], writes=[B(A0)])

                    prevA, prevAT = A0, AT0
                    for k_ in range(1, 7):
                        curA = Afrm[k_ % 2]
                        curAT = ATk[k_ - 1]
                        for h in range(NH):
                            S.op("vector", lambda e, h=h, prevA=prevA, c=c:
                                 e.tensor_scalar(
                                     out=Bfrm[:, h * 128:(h + 1) * 128],
                                     in0=prevA[:, h * 128:(h + 1) * 128],
                                     scalar1=bcols[:, c * NH + h:
                                                   c * NH + h + 1],
                                     scalar2=None, op0=ALU.mult),
                                 reads=[B(prevA), B(bcols)], writes=[B(Bfrm)])
                        for h in range(NH):
                            pa = PB[0] if h < 4 else PB[1]
                            pt_ = PB[2] if h < 4 else PB[3]
                            hh = h % 4
                            S.op("tensor", lambda e, h=h, hh=hh, pa=pa,
                                 prevAT=prevAT: e.matmul(
                                     out=pa[:, hh * 128:(hh + 1) * 128],
                                     lhsT=prevAT[:, h * 128:(h + 1) * 128],
                                     rhs=Bfrm[:, h * 128:(h + 1) * 128],
                                     start=True, stop=True),
                                 reads=[B(prevAT), B(Bfrm)], writes=[B(pa)])
                            S.op("tensor", lambda e, h=h, hh=hh, pt_=pt_,
                                 prevAT=prevAT: e.matmul(
                                     out=pt_[:, hh * 128:(hh + 1) * 128],
                                     lhsT=Bfrm[:, h * 128:(h + 1) * 128],
                                     rhs=prevAT[:, h * 128:(h + 1) * 128],
                                     start=True, stop=True),
                                 reads=[B(prevAT), B(Bfrm)], writes=[B(pt_)])
                        for pi, dst_, off in ((0, curA, 0), (1, curA, 512),
                                              (2, curAT, 0), (3, curAT, 512)):
                            S.op("vector", lambda e, pi=pi, dst_=dst_, off=off:
                                 e.tensor_copy(
                                     out=dst_[:, off:off + 512],
                                     in_=PB[pi][:]),
                                 reads=[B(PB[pi])], writes=[B(dst_)])
                        prevA, prevAT = curA, curAT

                    for cb in range(4):
                        S.op("tensor", lambda e, cb=cb, c=c: e.matmul(
                            out=PB[5][:, cb * 128:(cb + 1) * 128],
                            lhsT=kp[cb][:, c * 128:(c + 1) * 128],
                            rhs=S0b[cb][:], start=(cb == 0), stop=(cb == 3)),
                            reads=[B(kp[cb]), B(S0b[cb])], writes=[B(PB[5])])
                        S.op("tensor", lambda e, cb=cb, c=c: e.transpose(
                            out=PBb[6][:, cb * 128:(cb + 1) * 128],
                            in_=vs[cb][:, c * 128:(c + 1) * 128],
                            identity=ident_b[:]),
                            reads=[B(vs[cb]), B(ident_b)], writes=[B(PB[6])])
                    S.op("vector", lambda e: e.tensor_copy(
                        out=vT_s[:], in_=PBb[6][:, 0:512]),
                        reads=[B(PB[6])], writes=[B(vT_s)])
                    S.op("vector", lambda e: e.tensor_tensor(
                        out=Xscr[:], in0=vT_s[:], in1=PB[5][:, 0:512],
                        op=ALU.subtract),
                        reads=[B(vT_s), B(PB[5])], writes=[B(Xscr)])
                    S.op("vector", lambda e: e.tensor_tensor(
                        out=Rb[:], in0=Xscr[:], in1=bmap[:], op=ALU.mult),
                        reads=[B(Xscr), B(bmap)], writes=[B(Rb)])

                    Xcur = Rb
                    seq = [ATk[5], ATk[4], ATk[3], ATk[2], ATk[1], ATk[0], AT0]
                    for ridx, frm in enumerate(seq):
                        minus = ridx == len(seq) - 1
                        Xnext = Ub if minus else Xb[ridx % 2]
                        for h in range(NH):
                            S.op("tensor", lambda e, h=h, frm=frm, Xcur=Xcur:
                                 e.matmul(
                                     out=PB[4][:, h * HK:(h + 1) * HK],
                                     lhsT=frm[:, h * 128:(h + 1) * 128],
                                     rhs=Xcur[:, h * HK:(h + 1) * HK],
                                     start=(h == 0), stop=(h == 7)),
                                 reads=[B(frm), B(Xcur)], writes=[B(PB[4])])
                        S.op("vector", lambda e: e.tensor_tensor(
                            out=Xscr[:], in0=PB[4][:, 0:512], in1=bmap[:],
                            op=ALU.mult),
                            reads=[B(PB[4]), B(bmap)], writes=[B(Xscr)])
                        if minus:
                            S.op("vector", lambda e, Xcur=Xcur, Xnext=Xnext:
                                 e.tensor_tensor(
                                     out=Xnext[:], in0=Xcur[:], in1=Xscr[:],
                                     op=ALU.subtract),
                                 reads=[B(Xcur), B(Xscr)], writes=[B(Xnext)])
                        else:
                            S.op("vector", lambda e, Xcur=Xcur, Xnext=Xnext:
                                 e.tensor_tensor(
                                     out=Xnext[:], in0=Xscr[:], in1=Xcur[:],
                                     op=ALU.add),
                                 reads=[B(Xscr), B(Xcur)], writes=[B(Xnext)])
                        Xcur = Xnext

                    for cb in range(4):
                        S.op("tensor", lambda e, cb=cb, c=c: e.matmul(
                            out=PB[5][:, cb * 128:(cb + 1) * 128],
                            lhsT=qp[cb][:, c * 128:(c + 1) * 128],
                            rhs=S0b[cb][:], start=(cb == 0), stop=False),
                            reads=[B(qp[cb]), B(S0b[cb])], writes=[B(PB[5])])
                    for h in range(NH):
                        S.op("tensor", lambda e, h=h: e.matmul(
                            out=PB[5][:, h * HK:(h + 1) * HK],
                            lhsT=AQT[:, h * 128:(h + 1) * 128],
                            rhs=Ub[:, h * HK:(h + 1) * HK],
                            start=False, stop=(h == 7)),
                            reads=[B(AQT), B(Ub)], writes=[B(PB[5])])
                    S.op("scalar", lambda e: e.activation(
                        out=sqscr[:, 0:512], in_=PB[5][:, 0:512],
                        func=AF.Square),
                        reads=[B(PB[5])], writes=[B(sqscr)])
                    S.op("vector", lambda e: e.tensor_reduce(
                        out=rstd_s[:].unsqueeze(2),
                        in_=sqscr[:, 0:512].rearrange("p (h v) -> p h v",
                                                      h=NH),
                        axis=AXX, op=ALU.add),
                        reads=[B(sqscr)], writes=[B(rstd_s)])
                    S.op("scalar", lambda e: e.activation(
                        out=rstd_s[:], in_=rstd_s[:], func=AF.Ln,
                        scale=constc[:, 3:4], bias=constc[:, 0:1]),
                        reads=[B(rstd_s), B(constc)], writes=[B(rstd_s)])
                    S.op("scalar", lambda e: e.activation(
                        out=rstd_s[:], in_=rstd_s[:], func=AF.Exp,
                        scale=constc[:, 1:2]),
                        reads=[B(rstd_s), B(constc)], writes=[B(rstd_s)])
                    for h in range(NH):
                        S.op("vector", lambda e, h=h, c=c:
                             e.scalar_tensor_tensor(
                                 out=omid[:, h * HK:(h + 1) * HK],
                                 in0=Ub[:, h * HK:(h + 1) * HK],
                                 scalar=aqd_s[:, c * NH + h:c * NH + h + 1],
                                 in1=PB[5][:, h * HK:(h + 1) * HK],
                                 op0=ALU.mult, op1=ALU.add),
                             reads=[B(Ub), B(aqd_s), B(PB[5])],
                             writes=[B(omid)])
                        S.op("vector", lambda e, h=h: e.tensor_scalar(
                            out=omid[:, h * HK:(h + 1) * HK],
                            in0=omid[:, h * HK:(h + 1) * HK],
                            scalar1=rstd_s[:, h:h + 1], scalar2=None,
                            op0=ALU.mult),
                            reads=[B(omid), B(rstd_s)], writes=[B(omid)])
                    for cb in range(4):
                        S.op("tensor", lambda e, cb=cb: e.transpose(
                            out=PBb[6][:, cb * 128:(cb + 1) * 128],
                            in_=omid[:, cb * 128:(cb + 1) * 128],
                            identity=ident_b[:]),
                            reads=[B(omid), B(ident_b)], writes=[B(PB[6])])
                        S.op("vector", lambda e, cb=cb, c=c: e.tensor_tensor(
                            out=ofinT[:].rearrange("p (b t) -> p b t", b=4)
                            [:, cb, c * 128:(c + 1) * 128],
                            in0=PBb[6][:, cb * 128:(cb + 1) * 128],
                            in1=sgate[cb][:, c * 128:(c + 1) * 128],
                            op=ALU.mult),
                            reads=[B(PB[6]), B(sgate[cb])], writes=[B(ofinT)])

                    for cb in range(4):
                        S.op("tensor", lambda e, cb=cb, c=c: e.transpose(
                            out=PBb[6][:, cb * 128:(cb + 1) * 128],
                            in_=kd[cb][:, c * 128:(c + 1) * 128],
                            identity=ident_b[:]),
                            reads=[B(kd[cb]), B(ident_b)], writes=[B(PB[6])])
                    S.op("vector", lambda e: e.tensor_copy(
                        out=kdT_s[:], in_=PBb[6][:, 0:512]),
                        reads=[B(PB[6])], writes=[B(kdT_s)])
                    for h in range(NH):
                        cb, par = h // 2, h % 2
                        S.op("tensor", lambda e, h=h, cb=cb, par=par: e.matmul(
                            out=PB[7][par * 64:par * 64 + 64,
                                      cb * 128 + par * 64:
                                      cb * 128 + par * 64 + 64],
                            lhsT=kdT_s[:, cb * 128 + par * 64:
                                       cb * 128 + par * 64 + 64],
                            rhs=Ub[:, h * HK:(h + 1) * HK],
                            start=True, stop=True,
                            tile_position=(0, par * 64)),
                            reads=[B(kdT_s), B(Ub)], writes=[B(PB[7])])
                    for cb in range(4):
                        for par in range(2):
                            o_ = par * 64
                            S.op("vector", lambda e, cb=cb, o_=o_:
                                 e.scalar_tensor_tensor(
                                     out=S0[cb][o_:o_ + 64, o_:o_ + 64],
                                     in0=S0[cb][o_:o_ + 64, o_:o_ + 64],
                                     scalar=lamC[o_:o_ + 64, cb:cb + 1],
                                     in1=PB[7][o_:o_ + 64,
                                               cb * 128 + o_:
                                               cb * 128 + o_ + 64],
                                     op0=ALU.mult, op1=ALU.add),
                                 reads=[B(lamC), B(PB[7]), B(S0[cb])],
                                 writes=[B(S0[cb])])
                    for cb in range(4):
                        S.op("vector", lambda e, cb=cb: e.tensor_copy(
                            out=S0b[cb][:], in_=S0[cb][:]),
                            reads=[B(S0[cb])], writes=[B(S0b[cb])])

                for cb in range(4):
                    dma("sync",
                        cc_in.rearrange("(b p) t -> b p t", p=128)
                        [cb, :, T0:T0 + GT],
                        ofinT[:].rearrange("p (b t) -> p b t", b=4)[:, cb, :],
                        reads=[B(ofinT)], writes=[ccin_res])

            if no_cc:
                dma("gpsimd", cc_out[0:CH, :], cc_in, reads=[ccin_res],
                    writes=[])
            else:
                S.op("gpsimd", lambda e: e.collective_compute(
                    "AllGather", ALU.bypass, ins=[cc_in_f.opt()],
                    outs=[cc_out_f.opt()],
                    replica_groups=rgroups),
                    reads=[ccin_res], writes=[], kind="cc")

            with nc.Block("phaseA") as blockA:
                S.emit(blockA, sems)
            carry = S.final
            for sl in range(Sched.NPOOL):
                carry[f"d{sl}"] = S.dmagen[sl]

        for r_ in R.values():
            r_.w = None
            r_.rs = []
        ccin_res.w = None
        ccin_res.rs = []

        S2 = Sched(carry=carry, dmagen=S.dmagen)
        dma2 = mkdma(S2)

        bctx = contextlib.ExitStack()
        sbb = lambda n, s_, d_: sb(n, s_, d_, bctx)
        with bctx:
            wgT_s = sbb("wgT_s", (128, 8 * FFH), BF16)
            wuT_s = sbb("wuT_s", (128, 8 * FFH), BF16)
            wdT_s = sbb("wdT_s", (128, 24 * D), BF16)
            wo_s = sbb("wo_s", (128, 8 * D), BF16)
            onw_s = sbb("onw_s", (128, 1), F32)
            istage = [sbb(f"istage{i}", (128, FFH // 2), I32) for i in range(2)]
            fstage = sbb("fstage", (128, D), F32)
            xb2 = [sbb(f"xb2_{i}", (128, D), F32) for i in range(2)]
            ln2 = sbb("ln2", (128, 16), F32)
            scr2 = sbb("scr2", (128, D), F32)
            xnb2 = sbb("xnb2", (128, D), BF16)
            xnT2 = sbb("xnT2", (128, 8 * SUB), BF16)
            ofin2 = sbb("ofin2", (128, 8 * SUB), BF16)
            hbuf = sbb("hbuf", (128, 2 * SUB), BF16)
            sil2 = sbb("sil2", (128, SUB), F32)
            outsb = sbb("outsb", (128, D), F32)

            dma2("sync", onw_s[:], onw_in, writes=[B(onw_s)])
            for dblk in range(8):
                dma2("sync", fstage[:], wo_in[dblk * 128:(dblk + 1) * 128, :],
                     reads=[B(wo_s)], writes=[B(fstage)])
                S2.op("vector", lambda e, dblk=dblk: e.tensor_scalar(
                    out=wo_s[:, dblk * D:(dblk + 1) * D], in0=fstage[:],
                    scalar1=onw_s[:], scalar2=None, op0=ALU.mult),
                    reads=[B(fstage), B(onw_s)], writes=[B(wo_s)])
            HF = FFH // 2
            for win, wsb2 in ((wgT_in, wgT_s), (wuT_in, wuT_s)):
                for dblk in range(8):
                    for hf in range(2):
                        st_ = istage[(dblk * 2 + hf) % 2]
                        dma2("sync", st_[:],
                             win[dblk * 128:(dblk + 1) * 128,
                                 hf * HF:(hf + 1) * HF],
                             reads=[B(wsb2)], writes=[B(st_)])
                        S2.op("vector", lambda e, dblk=dblk, hf=hf, wsb2=wsb2,
                              st_=st_: e.tensor_copy(
                                  out=wsb2[:, dblk * FFH + hf * HF:
                                           dblk * FFH + (hf + 1) * HF],
                                  in_=st_[:]),
                              reads=[B(st_)], writes=[B(wsb2)])
            for hb in range(24):
                st_ = istage[hb % 2]
                dma2("sync", st_[:, 0:D], wdT_in[hb * 128:(hb + 1) * 128, :],
                     reads=[B(wdT_s)], writes=[B(st_)])
                S2.op("vector", lambda e, hb=hb, st_=st_: e.tensor_copy(
                    out=wdT_s[:, hb * D:(hb + 1) * D], in_=st_[:, 0:D]),
                    reads=[B(st_)], writes=[B(wdT_s)])

            regres = Res()

            def loadreg(e):
                reg = e.alloc_register("hgreg")
                loadreg.reg = reg
                return e.reg_load(reg, soff[0:1, 0:1])
            beng = "scalar" if simdma else "gpsimd"
            S2.op(beng, loadreg, reads=[B(soff)], writes=[regres])

            xh_t = xh_in.rearrange("(n p) d -> n p d", p=128)
            out_tt = out_t.rearrange("(n p) d -> n p d", p=128)
            ccov = cc_out.rearrange("(b p) t -> b p t", p=128)

            for s_ in range(0 if skipb else NSUBl):
                s0 = s_ * SUB
                for i in range(2):
                    xb = xb2[i]
                    emit_ln(S2, dma2, xb, xh_t[s_ * 2 + i], ln2, scr2, xnb2,
                            None)
                    for j in range(8):
                        S2.op("tensor", lambda e, j=j: e.transpose(
                            out=PBb[7][:, j * 128:(j + 1) * 128],
                            in_=xnb2[:, j * 128:(j + 1) * 128],
                            identity=ident_b[:]),
                            reads=[B(xnb2), B(ident_b)], writes=[B(PB[7])])
                    S2.op("vector", lambda e, i=i: e.tensor_copy(
                        out=xnT2[:].rearrange("p (b t) -> p b t", b=8)
                        [:, :, i * 128:(i + 1) * 128],
                        in_=PBb[7][:].rearrange("p (b t) -> p b t", b=8)),
                        reads=[B(PB[7])], writes=[B(xnT2)])

                def ccread(e, dsem, s0=s0):
                    if no_branch:
                        for bb_ in range(8):
                            e.dma_start(
                                out=ofin2[:].rearrange("p (b t) -> p b t",
                                                       b=8)[:, bb_, :],
                                in_=ccov[bb_, :, s0:s0 + SUB]).then_inc(
                                    dsem, 16)
                        return e.nop()
                    reg = loadreg.reg
                    with e.If_eq(reg, 0):
                        for bb_ in range(8):
                            e.dma_start(
                                out=ofin2[:].rearrange("p (b t) -> p b t",
                                                       b=8)[:, bb_, :],
                                in_=ccov[bb_, :, s0:s0 + SUB]).then_inc(
                                    dsem, 16)
                    with e.Else():
                        for bb_ in range(8):
                            e.dma_start(
                                out=ofin2[:].rearrange("p (b t) -> p b t",
                                                       b=8)[:, bb_, :],
                                in_=ccov[bb_, :, HALF_ + s0:HALF_ + s0 + SUB]
                            ).then_inc(dsem, 16)
                    return e.nop()
                S2.op(beng, ccread, reads=[regres], writes=[B(ofin2)],
                      kind="dma", n_dma=8)

                for hb in range(24):
                    for wsb2, pb_ in ((wgT_s, PB[4]), (wuT_s, PB[5])):
                        for j in range(8):
                            S2.op("tensor", lambda e, j=j, hb=hb, wsb2=wsb2,
                                  pb_=pb_: e.matmul(
                                      out=pb_[:, 0:SUB],
                                      lhsT=wsb2[:, j * FFH + hb * 128:
                                                j * FFH + (hb + 1) * 128],
                                      rhs=xnT2[:, j * SUB:(j + 1) * SUB],
                                      start=(j == 0), stop=(j == 7)),
                                  reads=[B(wsb2), B(xnT2)], writes=[B(pb_)])
                    S2.op("scalar", lambda e: e.activation(
                        out=sil2[:, 0:SUB], in_=PB[4][:, 0:SUB], func=AF.Silu,
                        scale=constc[:, 4:5]),
                        reads=[B(PB[4]), B(constc)], writes=[B(sil2)])
                    S2.op("vector", lambda e, hb=hb: e.scalar_tensor_tensor(
                        out=hbuf[:, (hb % 2) * SUB:(hb % 2 + 1) * SUB],
                        in0=sil2[:, 0:SUB], scalar=1.0 / 16129.0,
                        in1=PB[5][:, 0:SUB], op0=ALU.mult, op1=ALU.mult),
                        reads=[B(sil2), B(PB[5])], writes=[B(hbuf)])
                    for mb in range(2):
                        for nh_ in range(2):
                            S2.op("tensor", lambda e, hb=hb, mb=mb, nh_=nh_:
                                  e.matmul(
                                      out=PB[mb * 2 + nh_][:],
                                      lhsT=hbuf[:, (hb % 2) * SUB + mb * 128:
                                                (hb % 2) * SUB +
                                                (mb + 1) * 128],
                                      rhs=wdT_s[:, hb * D + nh_ * 512:
                                                hb * D + (nh_ + 1) * 512],
                                      start=(hb == 0), stop=False),
                                  reads=[B(hbuf), B(wdT_s)],
                                  writes=[B(PB[mb * 2 + nh_])])
                for bb_ in range(8):
                    for mb in range(2):
                        for nh_ in range(2):
                            S2.op("tensor", lambda e, bb_=bb_, mb=mb,
                                  nh_=nh_: e.matmul(
                                      out=PB[mb * 2 + nh_][:],
                                      lhsT=ofin2[:, bb_ * SUB + mb * 128:
                                                 bb_ * SUB + (mb + 1) * 128],
                                      rhs=wo_s[:, bb_ * D + nh_ * 512:
                                               bb_ * D + (nh_ + 1) * 512],
                                      start=False, stop=(bb_ == 7)),
                                  reads=[B(ofin2), B(wo_s)],
                                  writes=[B(PB[mb * 2 + nh_])])
                for mb in range(2):
                    for nh_ in range(2):
                        S2.op("vector", lambda e, mb=mb, nh_=nh_:
                              e.tensor_tensor(
                                  out=outsb[:, nh_ * 512:(nh_ + 1) * 512],
                                  in0=PB[mb * 2 + nh_][:],
                                  in1=xb2[mb][:, nh_ * 512:(nh_ + 1) * 512],
                                  op=ALU.add),
                              reads=[B(PB[mb * 2 + nh_]), B(xb2[mb])],
                              writes=[B(outsb)])
                    dma2(beng, out_tt[s_ * 2 + mb], outsb[:],
                         reads=[B(outsb)], writes=[outres])

            if skipb:
                S2.op("vector", lambda e: e.memset(outsb[:], 0.0),
                      writes=[B(outsb)])
                for ti_ in range(HALF_ // 128):
                    dma2(beng, out_tt[ti_], outsb[:], reads=[B(outsb)],
                         writes=[outres])
            S2.op(beng, lambda e: e.nop() if simdma else e.engine_nop(),
                  reads=[outres], writes=[])

            with nc.Block("phaseB") as blockB:
                S2.emit(blockB, sems, barrier_in=True)

    return nc


def kernel(**inputs):
    x = np.ascontiguousarray(inputs["L0_attn_hidden_states"])
    Wq = inputs["L0_attn_Wq"]
    Wk = inputs["L0_attn_Wk"]
    Wv = inputs["L0_attn_Wv"]
    qcw = inputs["L0_attn_qcw"]
    qcb = inputs["L0_attn_qcb"]
    kcw = inputs["L0_attn_kcw"]
    kcb = inputs["L0_attn_kcb"]
    vcw = inputs["L0_attn_vcw"]
    vcb = inputs["L0_attn_vcb"]
    Wf1 = inputs["L0_attn_Wf1"]
    Wf2 = inputs["L0_attn_Wf2"]
    Wb = inputs["L0_attn_Wb"]
    A_log = np.asarray(inputs["L0_attn_A_log"])
    dt_bias = np.asarray(inputs["L0_attn_dt_bias"])
    Wg1 = inputs["L0_attn_Wg1"]
    Wg2 = inputs["L0_attn_Wg2"]
    bg2 = np.asarray(inputs["L0_attn_bg2"])
    onw = np.asarray(inputs["L0_attn_o_norm_w"])
    Wo = inputs["L0_attn_Wo"]
    wg = inputs["L0_ffn_w_gate"]
    wu = inputs["L0_ffn_w_up"]
    wd = inputs["L0_ffn_w_down"]

    ident = np.eye(128, dtype=np.float32)
    s = np.arange(C)[:, None]
    t = np.arange(C)[None, :]
    masks = []
    for lv in (2, 4, 8, 16):
        h = lv // 2
        m = ((s // lv) == (t // lv)) & ((s % lv) < h) & ((t % lv) >= h)
        masks.append(np.tile(m.astype(np.uint8), (1, 4)))
    masks = np.ascontiguousarray(np.stack(masks))
    ind8 = np.zeros((4, NH, 128), np.float32)
    ind8T = np.zeros((4, 128, NH), np.float32)
    for cb in range(4):
        for par in range(2):
            h = cb * 2 + par
            ind8[cb, h, par * 64:(par + 1) * 64] = 1.0
            ind8T[cb, par * 64:(par + 1) * 64, h] = 1.0

    wgT = np.ascontiguousarray(wg.T)
    wuT = np.ascontiguousarray(wu.T)
    wdT = np.ascontiguousarray(wd.T)
    onw_t = np.ascontiguousarray(np.tile(onw.reshape(-1, 1), (2, 1)))

    in_maps = []
    for core in range(8):
        b, hg = core // 2, core % 2
        sl = slice(hg * CH, (hg + 1) * CH)
        hsl = slice(hg * NH, (hg + 1) * NH)
        cvw = np.ascontiguousarray(np.stack([qcw[sl], kcw[sl], vcw[sl]]))
        cvb = np.ascontiguousarray(np.stack(
            [qcb[sl].reshape(-1, 1), kcb[sl].reshape(-1, 1),
             vcb[sl].reshape(-1, 1)]))
        m = dict(
            x=np.ascontiguousarray(x[b]),
            xh=np.ascontiguousarray(x[b, hg * HALF:(hg + 1) * HALF]),
            wq=np.ascontiguousarray(Wq[:, sl]),
            wk=np.ascontiguousarray(Wk[:, sl]),
            wv=np.ascontiguousarray(Wv[:, sl]),
            cvw=cvw, cvb=cvb,
            wf1=np.ascontiguousarray(Wf1),
            wf2=np.ascontiguousarray(Wf2[:, sl]),
            wb=np.ascontiguousarray(Wb[:, hsl]),
            alog=np.ascontiguousarray(A_log[hsl].reshape(-1, 1)),
            dtb=np.ascontiguousarray(dt_bias[sl].reshape(-1, 1)),
            wg1=np.ascontiguousarray(Wg1),
            wg2=np.ascontiguousarray(Wg2[:, sl]),
            bg2=np.ascontiguousarray(bg2[sl].reshape(-1, 1)),
            onw=onw_t,
            wo=np.ascontiguousarray(Wo),
            wgT=wgT, wuT=wuT, wdT=wdT,
            ident=ident, masks=masks, ind8=ind8, ind8T=ind8T,
            coreoff=np.array([[hg]], dtype=np.int32),
        )
        in_maps.append(m)

    nc = build_nc()
    res = run_bass_kernel_spmd(nc, in_maps, core_ids=list(range(8)))
    out = np.empty((4, T, D), np.float32)
    for core in range(8):
        b, hg = core // 2, core % 2
        out[b, hg * HALF:(hg + 1) * HALF] = res.results[core]["out"]
    return out
